# version 35
# baseline (speedup 1.0000x reference)
import numpy as np
import ml_dtypes
import concourse.bass as bass
import concourse.mybir as mybir
from concourse.bass_utils import run_bass_kernel_spmd

F32 = mybir.dt.float32
BF16 = mybir.dt.bfloat16
AF = mybir.ActivationFunctionType
ALU = mybir.AluOpType
AXX = mybir.AxisListType.X

T, D, H, DK = 2048, 1024, 8, 64
FH = H * DK
CH = 128
NCHUNK = T // CH
NBLK = NCHUNK // 4
KCH = D // 128
E, R = 8, 4
NEG = -1.0e30
TRUNC = 99


class Ctr:
    def __init__(self, nc, name):
        self.h = nc.alloc_semaphore(name)
        self.n = 0

    def inc(self, instr, k=1):
        instr.then_inc(self.h, k)
        self.n += k
        return self.n


def build_core():
    nc = bass.Bass("TRN2", target_bir_lowering=False)

    x_d = nc.dram_tensor("x", [T, D], F32, kind="ExternalInput").ap()
    wq_d = nc.dram_tensor("wq", [KCH, 128, FH], F32, kind="ExternalInput").ap()
    wk_d = nc.dram_tensor("wk", [KCH, 128, FH], F32, kind="ExternalInput").ap()
    wv_d = nc.dram_tensor("wv", [KCH, 128, FH], F32, kind="ExternalInput").ap()
    wg_d = nc.dram_tensor("wg", [KCH, 128, FH], F32, kind="ExternalInput").ap()
    wo_d = nc.dram_tensor("wo", [4, 128, D], F32, kind="ExternalInput").ap()
    wlin_d = nc.dram_tensor("wlin", [KCH, 128, D], F32, kind="ExternalInput").ap()
    gatew_d = nc.dram_tensor("gatew", [KCH, 128, E], F32, kind="ExternalInput").ap()
    left_d = nc.dram_tensor("left", [KCH, 128, E * R], F32, kind="ExternalInput").ap()
    right_d = nc.dram_tensor("right", [E * R, D], F32, kind="ExternalInput").ap()
    ident_d = nc.dram_tensor("ident", [128, 128], BF16, kind="ExternalInput").ap()
    cmat_d = nc.dram_tensor("cmat", [128, 448], F32, kind="ExternalInput").ap()
    crow_d = nc.dram_tensor("crow", [1, D + FH + H], F32, kind="ExternalInput").ap()
    out_d = nc.dram_tensor("out", [T, D], F32, kind="ExternalOutput").ap()

    A = nc.alloc_sbuf_tensor
    x_sb = A("x_sb", [128, 10 * D], F32)
    xnfm = A("xnfm", [128, KCH * 1024], BF16)
    qfm = A("qfm", [128, 4 * 1024], BF16)
    kfm = A("kfm", [128, 4 * 1024], BF16)
    vtm = A("vtm", [128, 8 * 520], BF16)
    gsig = A("gsig", [128, 2 * FH], BF16)
    ygfm = A("ygfm", [128, 2 * FH], BF16)
    wq_sb = A("wq_sb", [128, KCH * FH], BF16)
    wk_sb = A("wk_sb", [128, KCH * FH], BF16)
    wv_sb = A("wv_sb", [128, KCH * FH], BF16)
    wg_sb = A("wg_sb", [128, KCH * FH], BF16)
    wo_sb = A("wo_sb", [128, 4 * D], BF16)
    wlin_sb = A("wlin_sb", [128, KCH * D], BF16)
    gatew_sb = A("gatew_sb", [128, KCH * E], BF16)
    left_sb = A("left_sb", [128, KCH * E * R], BF16)
    right_sb = A("right_sb", [E * R, D], BF16)
    crow_s = A("crow_s", [1, D + FH + H], F32)
    blin_hl = A("blin_hl", [2, D], BF16)
    blin_lo_t = A("blin_lo_t", [1, D], BF16)
    wgb_lo_t = A("wgb_lo_t", [1, FH], BF16)
    wgb_hl = A("wgb_hl", [2, FH], BF16)
    nlg_u = A("nlg_u", [1, H], F32)
    nlg_row = A("nlg_row", [1, H], F32)
    nlg_hi = A("nlg_hi", [1, H], BF16)
    nlg_lo = A("nlg_lo", [1, H], BF16)
    ident = A("ident_s", [128, 128], BF16)
    cmat_s = A("cmat_s", [128, 448], F32)
    masks = A("masks", [128, H * 128], BF16)
    dpqf = A("dpqf", [128, 4 * 128], BF16)
    dpkf = A("dpkf", [128, 4 * 128], BF16)
    nlgrep = A("nlgrep", [128, H], F32)
    nlgpair = A("nlgpair", [128, 4], F32)
    gcpair = A("gcpair", [128, 4], F32)
    S_sb = A("S_sb", [128, 4 * 64], F32)
    S_blk = A("S_blk", [128, 4 * 128], BF16)
    z_blk = A("z_blk", [128, 8], BF16)
    z_sb = A("z_sb", [128, 4], F32)
    ones_rowb = A("ones_rowb", [1, 128], BF16)
    ones2 = A("ones2", [2, 128], BF16)
    ones_colb = A("ones_colb", [128, 1], BF16)
    eps_row8 = A("eps_row8", [1, H], BF16)
    eps5_col = A("eps5_col", [128, 1], F32)
    one_col = A("one_col", [128, 1], F32)
    xn_tm = A("xn_tm", [128, 4 * D], BF16)
    phiw = A("phiw", [128, 2 * FH], F32)
    phir = A("phir", [128, 2 * FH], BF16)
    A_sb = A("A_sb", [128, 2 * H * 128], BF16)
    qt_sb = A("qt_sb", [128, 2 * 512], BF16)
    ktm_sb = A("ktm_sb", [128, 2 * 512], BF16)
    ygtm = A("ygtm", [128, 2 * FH], BF16)
    den_r = A("den_r", [128, 2 * H], F32)
    wt_tm = A("wt_tm", [128, 4 * E], BF16)
    wt4 = A("wt4", [128, 4 * 4 * E], BF16)
    wtT = A("wtT", [E * R, T], BF16)
    th_sb = A("th_sb", [E * R, 2 * 512], BF16)
    e8 = A("e8", [128, 4 * E], F32)
    dmpA = A("dmpA", [128, 4 * E], F32)
    dmpB = A("dmpB", [128, 4 * E], F32)
    ind1 = A("ind1", [128, 4 * E], F32)
    i1inv = A("i1inv", [128, 4 * E], F32)
    tmp2 = A("tmp2", [128, 4 * E], F32)
    indf = A("indf", [128, 4 * E], F32)
    wtu = A("wtu", [128, 4 * E], F32)
    tcol = A("tcol", [128, 8], F32)
    varn = A("varn", [128, 8], F32)
    stdc = A("stdc", [128, 8], F32)
    rstdc = A("rstdc", [128, 8], F32)
    nmrc = A("nmrc", [128, 8], F32)
    rs8 = A("rs8", [128, 4], F32)
    cols = A("cols", [128, 48], F32)
    scr1b = A("scr1b", [128, D], BF16)
    out_sb = A("out_sb", [128, 2 * D], F32)

    pT = nc.alloc_psum_tensor("pT", [128, 1024], BF16)
    pQK = [nc.alloc_psum_tensor("pQK0", [128, 512], F32),
           nc.alloc_psum_tensor("pQK1", [128, 512], F32)]
    pV = nc.alloc_psum_tensor("pV", [128, 512], F32)
    pG = nc.alloc_psum_tensor("pG", [128, 512], F32)
    pA0 = nc.alloc_psum_tensor("pA0", [128, 512], F32)
    pT2 = nc.alloc_psum_tensor("pT2", [128, 1024], BF16)
    pO = nc.alloc_psum_tensor("pO", [128, 512], F32)
    pVQ = None

    s_pe = Ctr(nc, "s_pe")
    s_vec = Ctr(nc, "s_vec")
    s_sca = Ctr(nc, "s_sca")
    s_gp = Ctr(nc, "s_gp")
    s_xc = [nc.alloc_semaphore(f"s_x{i}") for i in range(NCHUNK)]
    s_wg = {g: nc.alloc_semaphore(f"s_w_{g}") for g in ["qk", "k2", "v", "g", "lg", "ow"]}
    s_c = nc.alloc_semaphore("s_c")
    s_cr = nc.alloc_semaphore("s_cr")
    s_cm = nc.alloc_semaphore("s_cm")
    s_bl = nc.alloc_semaphore("s_bl")
    s_oc = [nc.alloc_semaphore(f"s_o{i}") for i in range(NCHUNK)]

    TE, VE, SE, SY, GP = nc.tensor, nc.vector, nc.scalar, nc.sync, nc.gpsimd
    pVQ = [pV, pO]

    def dma_xc(c):
        cx = c % 10
        SY.dma_start(x_sb[:, cx * D : cx * D + D],
                     x_d[128 * c : 128 * c + 128, :]).then_inc(s_xc[c], 16)

    SY.dma_start(ident[:, :], ident_d[:, :]).then_inc(s_c, 16)
    SY.dma_start(crow_s[:, :], crow_d[:, :]).then_inc(s_cr, 16)
    for _c in range(4):
        dma_xc(_c)

    for t_, v_ in [(ones_rowb, 1.0), (ones2, 1.0), (ones_colb, 1.0), (eps_row8, 1e-6),
                   (eps5_col, 1e-5), (one_col, 1.0), (S_sb, 0.0), (S_blk, 0.0),
                   (z_sb, 0.0), (z_blk, 0.0), (vtm, 1.0)]:
        ins = VE.memset(t_[:, :], v_)
    v_ms = s_vec.inc(ins)
    VE.wait_ge(s_cr, 16)
    VE.tensor_copy(blin_hl[0:1, :], crow_s[0:1, 0:D])
    v_bh = s_vec.inc(VE.tensor_copy(wgb_hl[0:1, :], crow_s[0:1, D : D + FH]))
    GP.wait_ge(s_vec.h, v_bh)
    GP.tensor_tensor(blin_lo_t[:, :], crow_s[0:1, 0:D], blin_hl[0:1, :], ALU.subtract)
    g_bl = s_gp.inc(GP.tensor_tensor(wgb_lo_t[:, :], crow_s[0:1, D : D + FH], wgb_hl[0:1, :],
                                     ALU.subtract))
    wmap = [(gatew_sb, gatew_d, "lg"), (left_sb, left_d, "lg"),
            (wq_sb, wq_d, "qk"), (wk_sb, wk_d, "k2"), (wv_sb, wv_d, "v"),
            (wg_sb, wg_d, "g"), (wlin_sb, wlin_d, "ow"), (wo_sb, wo_d, "ow")]
    w_tot = {g: 0 for g in s_wg}
    for dst, srcd, grp in wmap:
        kk = srcd.shape[0]
        GP.dma_start(dst[:, :].rearrange("p (k c) -> p k c", k=kk),
                     srcd.rearrange("k p c -> p k c")).then_inc(s_wg[grp], 16)
        w_tot[grp] += 16
    GP.dma_start(right_sb[:, :], right_d[:, :]).then_inc(s_wg["ow"], 16)
    w_tot["ow"] += 16
    SY.wait_ge(s_gp.h, g_bl)
    SY.dma_start(blin_hl[1:2, :], blin_lo_t[:, :]).then_inc(s_bl, 16)
    SY.dma_start(wgb_hl[1:2, :], wgb_lo_t[:, :]).then_inc(s_bl, 16)
    SY.dma_start(cmat_s[:, :], cmat_d[:, :]).then_inc(s_cm, 16)
    for _c in range(4, 10):
        dma_xc(_c)

    ini = {"sca": 0, "vec": 0, "gp": 0}

    def init_tail():
        SE.wait_ge(s_cr, 16)
        s_e1 = s_sca.inc(SE.activation(nlg_u[:, :], crow_s[0:1, D + FH : D + FH + H], AF.Exp, scale=-1.0))
        VE.wait_ge(s_sca.h, s_e1)
        v_mn = s_vec.inc(VE.tensor_scalar(nlg_u[:, :], nlg_u[:, :], 1e15, None, ALU.min))
        SE.wait_ge(s_vec.h, max(v_mn, v_ms))
        s_ln = s_sca.inc(SE.activation(nlg_row[:, :], nlg_u[:, :], AF.Ln,
                                       bias=one_col[0:1, :]))
        VE.wait_ge(s_sca.h, s_ln)
        v_nh = s_vec.inc(VE.tensor_copy(nlg_hi[:, :], nlg_row[:, :]))
        GP.wait_ge(s_vec.h, v_nh)
        g_nl = s_gp.inc(GP.tensor_tensor(nlg_lo[:, :], nlg_row[:, :], nlg_hi[:, :],
                                         ALU.subtract))
        TE.wait_ge(s_vec.h, v_nh)
        TE.wait_ge(s_gp.h, g_nl)
        TE.matmul(pG[:, 0:H], ones_rowb[:, :], nlg_hi[:, :], start=True, stop=False)
        t_lg = s_pe.inc(TE.matmul(pG[:, 0:H], ones_rowb[:, :], nlg_lo[:, :],
                                  start=False, stop=True))
        SE.wait_ge(s_pe.h, t_lg)
        SE.copy(nlgrep[:, :], pG[:, 0:H])
        SE.drain()
        for p in range(4):
            SE.copy(nlgpair[0:64, p : p + 1], nlgrep[0:64, 2 * p : 2 * p + 1])
            ins = SE.copy(nlgpair[64:128, p : p + 1], nlgrep[64:128, 2 * p + 1 : 2 * p + 2])
        SE.drain()
        s_gc = s_sca.inc(SE.activation(gcpair[:, :], nlgpair[:, :], AF.Exp, scale=-128.0))
        VE.wait_ge(s_sca.h, s_gc)
        VE.wait_ge(s_cm, 16)
        expt_c = cmat_s[:, 0:128]
        maskadd_c = cmat_s[:, 128:256]
        eqm_c = cmat_s[:, 256:384]
        e127_c = cmat_s[:, 384:448]
        for h in range(H):
            VE.scalar_tensor_tensor(
                out_sb[:, 128 * h : 128 * h + 128], expt_c, nlgrep[:, h : h + 1],
                maskadd_c, ALU.mult, ALU.add)
        for p in range(4):
            VE.tensor_scalar(out_sb[:, 1024 + 128 * p : 1024 + 128 * p + 128],
                             eqm_c, nlgpair[:, p : p + 1], None, ALU.mult)
        for h in range(H):
            ins = VE.tensor_scalar(out_sb[:, 1536 + 64 * h : 1536 + 64 * h + 64],
                                   e127_c, nlgrep[:, h : h + 1], None, ALU.mult)
        v_ini = s_vec.inc(ins)
        SE.wait_ge(s_vec.h, v_ini)
        SE.activation(masks[:, :], out_sb[:, 0:1024], AF.Exp)
        SE.activation(dpqf[:, :], out_sb[:, 1024:1536], AF.Exp)
        ss = s_sca.inc(SE.activation(dpkf[:, :], out_sb[:, 1536:2048], AF.Exp))
        VE.wait_ge(s_sca.h, ss)
        ini["sca"] = s_sca.n
        ini["vec"] = s_vec.n
        ini["gp"] = s_gp.n
        rec["gfree_s"] = s_sca.n


    rec = {
        "ow": [0] * NCHUNK, "res": [0] * NCHUNK, "supd": [0] * NCHUNK,
        "gat": [0] * NCHUNK, "ydeN": [0] * NCHUNK, "maskdone": [0] * NCHUNK,
        "recip": [0] * NCHUNK, "ytpe": [0] * NCHUNK,
        "tp_free_v": 0, "tp_free_s": 0,
        "qk_free": [[0, "v"], [0, "v"], [0, "v"]],
        "vfree": [0, 0], "gfree_s": s_sca.n, "gfree_v": 0,
        "th": [0, 0], "ge_evac": [0] * NCHUNK, "phi_done_v": 0, "s_w4": 0,
        "kq_ready": [0, 0],
        "stats": [0] * NCHUNK, "nmr": [0] * NBLK, "e8b": 0,
    }
    qk_par = [0]

    def qk_bank():
        i = qk_par[0] % 2
        qk_par[0] += 1
        return i

    rec["t_tp"] = [0] * NCHUNK
    rec["s_xn"] = [0] * NCHUNK
    rec["xn_ev"] = [("v", 0)] * NCHUNK
    rec["tp2_free_v"] = 0
    rec["vh0"] = [0] * NCHUNK
    rec["supd_v"] = [0] * NCHUNK
    rec["tp2_free_s"] = 0
    rec["vktm"] = [0] * NCHUNK
    rec["scanv"] = [0] * NCHUNK
    rec["vv"] = [0] * NCHUNK
    rec["qt"] = [0] * NCHUNK

    def A_stats(c):
        cx, cs = c % 10, c % 8
        xc = x_sb[:, cx * D : cx * D + D]
        SE.wait_ge(s_xc[c], 16)
        if c > 0:
            SE.wait_ge(s_sca.h, rec["stats"][c - 1])
        s_cp = s_sca.inc(SE.activation(scr1b[:, :], xc, AF.Copy,
                                       accum_out=cols[:, cs : cs + 1]))
        SE.wait_ge(s_sca.h, s_cp)
        rec["stats"][c] = s_sca.inc(SE.activation(
            scr1b[:, :], xc, AF.Square, accum_out=cols[:, 8 + cs : 8 + cs + 1]))

    def A_pre2(bb):
        base = (4 * bb) % 8
        GP.wait_ge(s_sca.h, rec["stats"][4 * bb + 3])
        for i in range(4):
            cs = (4 * bb + i) % 8
            ins = GP.tensor_scalar(tcol[:, cs : cs + 1], cols[:, cs : cs + 1],
                                   cols[:, cs : cs + 1], -1.0 / (D * D),
                                   ALU.mult, ALU.mult)
        g_t = s_gp.inc(ins)
        VE.wait_ge(s_gp.h, g_t)
        v_var = s_vec.inc(VE.scalar_tensor_tensor(
            varn[:, base : base + 4], cols[:, 8 + base : 8 + base + 4],
            1.0 / D, tcol[:, base : base + 4], ALU.mult, ALU.add))
        SE.wait_ge(s_vec.h, v_var)
        s_lnv = s_sca.inc(SE.activation(stdc[:, base : base + 4],
                                        varn[:, base : base + 4],
                                        AF.Ln, bias=eps5_col[:, :]))
        SE.wait_ge(s_sca.h, s_lnv)
        s_rs = s_sca.inc(SE.activation(rstdc[:, base : base + 4],
                                       stdc[:, base : base + 4],
                                       AF.Exp, scale=-0.5))
        VE.wait_ge(s_sca.h, s_rs)
        rec["nmr"][bb] = s_vec.inc(VE.scalar_tensor_tensor(
            nmrc[:, base : base + 4], cols[:, base : base + 4], -1.0 / D,
            rstdc[:, base : base + 4], ALU.mult, ALU.mult))

    def A_xn(c):
        b = c // 4
        cx, cs, c4 = c % 10, c % 8, c % 4
        xc = x_sb[:, cx * D : cx * D + D]
        if c >= 4:
            SE.wait_ge(s_pe.h, rec["t_tp"][c - 4])
        SE.wait_ge(s_vec.h, rec["nmr"][b])
        rec["s_xn"][c] = s_sca.inc(SE.activation(xn_tm[:, c4 * D : c4 * D + D], xc,
                                       AF.Identity, scale=rstdc[:, cs : cs + 1],
                                       bias=nmrc[:, cs : cs + 1]))

    def A_ln_tp(c, bank2=False):
        cs, cp, c4 = c % 8, c % 2, c % 4
        pTb = pT2 if bank2 else pT
        TE.wait_ge(s_sca.h, rec["s_xn"][c])
        if bank2:
            TE.wait_ge(s_vec.h, rec["tp2_free_v"])
            TE.wait_ge(s_sca.h, rec["tp2_free_s"])
        else:
            TE.wait_ge(s_vec.h, rec["tp_free_v"])
            TE.wait_ge(s_sca.h, rec["tp_free_s"])
        for j in range(KCH):
            ins = TE.transpose(pTb[:, 128 * j : 128 * j + 128],
                               xn_tm[:, c4 * D + 128 * j : c4 * D + 128 * j + 128],
                               ident[:, :])
        t_tp = s_pe.inc(ins)
        rec["t_tp"][c] = t_tp
        xdst = xnfm[:, :].rearrange("p (k w) -> p k w", k=KCH)[:, :, cs * 128 : cs * 128 + 128]
        xsrc = pTb[:, :].rearrange("p (k w) -> p k w", k=KCH)
        if cp == 0:
            if c >= 8:
                VE.wait_ge(s_pe.h, rec["ow"][c - 8])
            VE.wait_ge(s_pe.h, t_tp)
            ev = s_vec.inc(VE.tensor_copy(xdst, xsrc))
            if bank2:
                rec["tp2_free_v"] = ev
            else:
                rec["tp_free_v"] = ev
            rec["xn_ev"][c] = ("v", ev)
        else:
            if c >= 8:
                SE.wait_ge(s_pe.h, rec["ow"][c - 8])
            SE.wait_ge(s_pe.h, t_tp)
            ev = s_sca.inc(SE.copy(xdst, xsrc))
            if bank2:
                rec["tp2_free_s"] = ev
            else:
                rec["tp_free_s"] = ev
            rec["xn_ev"][c] = ("s", ev)

    def wait_xn(b):
        mv, ms = ini["vec"], ini["sca"]
        for cc in range(4 * b, 4 * b + 4):
            e, n = rec["xn_ev"][cc]
            if e == "v":
                mv = max(mv, n)
            else:
                ms = max(ms, n)
        TE.wait_ge(s_vec.h, mv)
        TE.wait_ge(s_sca.h, ms)

    def B_proj(b):
        toff = 512 * (b % 2)
        wait_xn(b)
        TE.wait_ge(s_wg["qk"], w_tot["qk"])
        pbanks = [pQK[0], pQK[1], pA0]
        if b > 0:
            rec["qk_free"][2] = [rec["gat"][4 * b - 1], "v"]
        outs = [(wsb, dstfm, m) for (wsb, dstfm) in [(wq_sb, qfm), (wk_sb, kfm)]
                for m in range(4)]
        pend = None
        v_min = [0, 0]
        v_stt = [0, 0]
        s_exp = [0, 0]
        for i, (wsb, dstfm, m) in enumerate(outs):
            if i == 4:
                TE.wait_ge(s_wg["k2"], w_tot["k2"])
            bank = [2, 0, 1][i % 3]
            cnt, eng = rec["qk_free"][bank]
            TE.wait_ge(s_vec.h if eng == "v" else s_sca.h, cnt)
            for k in range(KCH):
                ins = TE.matmul(pbanks[bank][:, :],
                                wsb[:, FH * k + 128 * m : FH * k + 128 * m + 128],
                                xnfm[:, 1024 * k + toff : 1024 * k + toff + 512],
                                start=(k == 0), stop=(k == KCH - 1))
            t_g = s_pe.inc(ins)
            tp = i % 2
            VE.wait_ge(s_pe.h, t_g)
            if s_exp[tp]:
                VE.wait_ge(s_sca.h, s_exp[tp])
            v_min[tp] = s_vec.inc(VE.tensor_scalar(
                phiw[:, tp * FH : tp * FH + FH], pbanks[bank][:, :], 0.0, None,
                ALU.min))
            SE.wait_ge(s_vec.h, v_min[tp])
            if v_stt[tp]:
                SE.wait_ge(s_vec.h, v_stt[tp])
            s_exp[tp] = s_sca.inc(SE.activation(phir[:, tp * FH : tp * FH + FH],
                                                phiw[:, tp * FH : tp * FH + FH],
                                                AF.Exp))
            if pend is not None:
                pb, pd, pm, ptp, pse = pend
                VE.wait_ge(s_sca.h, pse)
                v_stt[ptp] = s_vec.inc(VE.scalar_tensor_tensor(
                    pd[:, 1024 * pm + toff : 1024 * pm + toff + 512],
                    pbanks[pb][:, :], 0.0,
                    phir[:, ptp * FH : ptp * FH + FH], ALU.max, ALU.add))
                rec["qk_free"][pb] = [v_stt[ptp], "v"]
            pend = (bank, dstfm, m, tp, s_exp[tp])
        pb, pd, pm, ptp, pse = pend
        VE.wait_ge(s_sca.h, pse)
        v_last = s_vec.inc(VE.scalar_tensor_tensor(
            pd[:, 1024 * pm + toff : 1024 * pm + toff + 512],
            pbanks[pb][:, :], 0.0,
            phir[:, ptp * FH : ptp * FH + FH], ALU.max, ALU.add))
        rec["qk_free"][pb] = [v_last, "v"]
        rec["phi_done_v"] = v_last
        rec["kq_ready"][b % 2] = v_last

    def C_log_mm(b):
        blo = 4 * b
        TE.wait_ge(s_sca.h, max(rec["ge_evac"][4 * b], ini["sca"]))
        wait_xn(b)
        TE.wait_ge(s_wg["lg"], w_tot["lg"])
        for ci in range(4):
            cs = (blo + ci) % 8
            for k in range(KCH):
                ins = TE.matmul(pG[:, 8 * ci : 8 * ci + 8],
                                xnfm[:, 1024 * k + cs * 128 : 1024 * k + cs * 128 + 128],
                                gatew_sb[:, E * k : E * k + E],
                                start=(k == 0), stop=(k == KCH - 1))
        t_l = s_pe.inc(ins)

        scol = lambda c, i: cols[:, 16 + (c % 4) * 4 + i : 16 + (c % 4) * 4 + i + 1]
        VE.wait_ge(s_pe.h, t_l)
        for ci in range(4):
            ins = VE.reduce_max(scol(blo + ci, 0), pG[:, 8 * ci : 8 * ci + 8],
                                AXX, negate=True)
        v_nm = s_vec.inc(ins)
        SE.wait_ge(s_vec.h, v_nm)
        for ci in range(4):
            c4 = (blo + ci) % 4
            ins = SE.activation(e8[:, c4 * E : c4 * E + E],
                                pG[:, 8 * ci : 8 * ci + 8],
                                AF.Exp, bias=scol(blo + ci, 0))
        s_e8 = s_sca.inc(ins)
        rec["e8b"] = s_e8
        rec["gfree_s"] = s_e8

    def C_top2(b):
        blo = 4 * b
        scol = lambda c, i: cols[:, 16 + (c % 4) * 4 + i : 16 + (c % 4) * 4 + i + 1]
        VE.wait_ge(s_sca.h, max(rec["e8b"], rec["s_w4"]))
        esl = lambda c4: e8[:, c4 * E : c4 * E + E]
        for ci in range(4):
            c4 = (blo + ci) % 4
            VE.tensor_scalar(dmpA[:, c4 * E : c4 * E + E], esl(c4), 1.0, NEG,
                             ALU.mult, ALU.max, accum_out=scol(blo + ci, 1))
            ins = VE.tensor_scalar(dmpB[:, c4 * E : c4 * E + E], esl(c4),
                                   1.0, 0.0, ALU.mult, ALU.add,
                                   accum_out=scol(blo + ci, 2))
        v_m1 = s_vec.inc(ins)
        VE.drain()
        for ci in range(4):
            ins = VE.reciprocal(rs8[:, (blo + ci) % 4 : (blo + ci) % 4 + 1],
                                scol(blo + ci, 2))
        v_rs = s_vec.inc(ins)
        GP.wait_ge(s_vec.h, v_m1)
        for ci in range(4):
            c4 = (blo + ci) % 4
            ins = GP.tensor_scalar(ind1[:, c4 * E : c4 * E + E], esl(c4),
                                   scol(blo + ci, 1), None, ALU.is_ge)
        g_i1 = s_gp.inc(ins)
        VE.wait_ge(s_gp.h, g_i1)
        for ci in range(4):
            c4 = (blo + ci) % 4
            ins = VE.scalar_tensor_tensor(tmp2[:, c4 * E : c4 * E + E],
                                          ind1[:, c4 * E : c4 * E + E],
                                          -1e30, esl(c4), ALU.mult, ALU.add)
        v_t2 = s_vec.inc(ins)
        VE.wait_ge(s_vec.h, v_t2)
        for ci in range(4):
            c4 = (blo + ci) % 4
            ins = VE.reduce_max(scol(blo + ci, 3), tmp2[:, c4 * E : c4 * E + E], AXX)
        v_m2 = s_vec.inc(ins)
        GP.wait_ge(s_vec.h, v_m2)
        for ci in range(4):
            c4 = (blo + ci) % 4
            ins = GP.tensor_scalar(indf[:, c4 * E : c4 * E + E], esl(c4),
                                   scol(blo + ci, 3), None, ALU.is_ge)
        g_ind = s_gp.inc(ins)
        VE.wait_ge(s_gp.h, g_ind)
        VE.wait_ge(s_vec.h, v_rs)
        for ci in range(4):
            c4 = (blo + ci) % 4
            ins = VE.scalar_tensor_tensor(wt_tm[:, c4 * E : c4 * E + E],
                                          indf[:, c4 * E : c4 * E + E],
                                          rs8[:, c4 : c4 + 1], esl(c4),
                                          ALU.mult, ALU.mult)
        v_wt = s_vec.inc(ins)
        SE.wait_ge(s_vec.h, v_wt)
        for ci in range(4):
            c4 = (blo + ci) % 4
            for r in range(R):
                ins = SE.copy(wt4[:, c4 * 32 + 8 * r : c4 * 32 + 8 * r + 8],
                              wt_tm[:, c4 * E : c4 * E + E])
        rec["s_w4"] = s_sca.inc(ins)

    def C_rest(b):
        blo = 4 * b
        toff = 512 * (b % 2)
        bank_t = qk_bank()
        cnt, eng = rec["qk_free"][bank_t]
        TE.wait_ge(s_vec.h if eng == "v" else s_sca.h, cnt)
        for k in range(KCH):
            ins = TE.matmul(pQK[bank_t][0:32, :],
                            left_sb[:, 32 * k : 32 * k + 32],
                            xnfm[:, 1024 * k + toff : 1024 * k + toff + 512],
                            start=(k == 0), stop=(k == KCH - 1))
        t_t = s_pe.inc(ins)
        TE.wait_ge(s_sca.h, max(rec["s_w4"], rec["tp_free_s"]))
        TE.wait_ge(s_vec.h, rec["tp_free_v"])
        for ci in range(4):
            c4 = (blo + ci) % 4
            ins = TE.transpose(pT[0:32, 128 * ci : 128 * ci + 128],
                               wt4[:, c4 * 32 : c4 * 32 + 32], ident[:, :])
        t_wt = s_pe.inc(ins)
        SE.wait_ge(s_pe.h, t_wt)
        for ci in range(4):
            ins = SE.copy(wtT[:, 128 * (blo + ci) : 128 * (blo + ci) + 128],
                          pT[0:32, 128 * ci : 128 * ci + 128])
        rec["tp_free_s"] = s_sca.inc(ins)

        bp = b % 2
        VE.wait_ge(s_pe.h, t_t)
        VE.wait_ge(s_sca.h, rec["tp_free_s"])
        if b >= 2:
            VE.wait_ge(s_pe.h, rec["ow"][4 * (b - 2) + 3])
        v_th = s_vec.inc(VE.tensor_tensor(th_sb[:, bp * 512 : bp * 512 + 512],
                                          pQK[bank_t][0:32, :],
                                          wtT[:, 512 * b : 512 * b + 512], ALU.mult))
        rec["th"][bp] = v_th
        rec["qk_free"][bank_t] = [v_th, "v"]

    def E_vg(c):
        cs = c % 8
        pVb = pVQ[c % 2]
        eng, cnt = rec["xn_ev"][c]
        TE.wait_ge(s_vec.h if eng == "v" else s_sca.h, cnt)
        TE.wait_ge(s_wg["v"], w_tot["v"])
        if c >= 2:
            TE.wait_ge(s_vec.h, rec["vfree"][c % 2])
        for k in range(KCH):
            ins = TE.matmul(pVb[:, :],
                            xnfm[:, 1024 * k + cs * 128 : 1024 * k + cs * 128 + 128],
                            wv_sb[:, FH * k : FH * k + FH],
                            start=(k == 0), stop=(k == KCH - 1))
        rec["vv"][c] = s_pe.inc(ins)
        TE.wait_ge(s_wg["g"], w_tot["g"])
        TE.wait_ge(s_bl, 32)
        if c == 0:
            TE.wait_ge(s_gp.h, ini["gp"])
        TE.wait_ge(s_sca.h, rec["gfree_s"])
        TE.wait_ge(s_vec.h, rec["gfree_v"])
        for k in range(KCH):
            TE.matmul(pG[:, :],
                      xnfm[:, 1024 * k + cs * 128 : 1024 * k + cs * 128 + 128],
                      wg_sb[:, FH * k : FH * k + FH],
                      start=(k == 0), stop=False)
        t_gp = s_pe.inc(TE.matmul(pG[:, :], ones2[:, :], wgb_hl[:, :],
                                  start=False, stop=True))
        SE.wait_ge(s_pe.h, t_gp)
        s_gs = s_sca.inc(SE.activation(gsig[:, (c % 2) * FH : (c % 2) * FH + FH],
                                       pG[:, :], AF.Sigmoid))
        rec["ge_evac"][c] = s_gs

    def E_sc(c):
        b = c // 4
        ci = c % 4
        cs, cp = c % 8, c % 2
        toff = 512 * (b % 2)
        pVb = pVQ[c % 2]
        TE.wait_ge(s_vec.h, rec["kq_ready"][b % 2])
        if c > 0:
            TE.wait_ge(s_vec.h, rec["gat"][c - 1])
        for h in [0, 2, 4, 6]:
            ins = TE.matmul(
                pA0[:, 128 * (h // 2) : 128 * (h // 2) + 128],
                kfm[0:64, 1024 * (h // 2) + toff + 128 * ci :
                    1024 * (h // 2) + toff + 128 * ci + 128],
                qfm[0:64, 1024 * (h // 2) + toff + 128 * ci :
                    1024 * (h // 2) + toff + 128 * ci + 128],
                start=True, stop=True)
        t_sc0 = s_pe.inc(ins)
        TE.wait_ge(s_sca.h, max(rec["ge_evac"][c], rec["e8b"]))
        for h in [1, 3, 5, 7]:
            ins = TE.matmul(
                pG[:, 128 * (h // 2) : 128 * (h // 2) + 128],
                kfm[64:128, 1024 * (h // 2) + toff + 128 * ci :
                    1024 * (h // 2) + toff + 128 * ci + 128],
                qfm[64:128, 1024 * (h // 2) + toff + 128 * ci :
                    1024 * (h // 2) + toff + 128 * ci + 128],
                start=True, stop=True)
        t_sc1 = s_pe.inc(ins)
        if c >= 2:
            VE.wait_ge(s_pe.h, rec["ydeN"][c - 2])
        VE.wait_ge(s_pe.h, t_sc1)
        for h in range(H):
            psrc = pA0 if h % 2 == 0 else pG
            ins = VE.tensor_tensor(
                A_sb[:, cp * 1024 + 128 * h : cp * 1024 + 128 * h + 128],
                psrc[:, 128 * (h // 2) : 128 * (h // 2) + 128],
                masks[:, 128 * h : 128 * h + 128], ALU.mult)
        rec["maskdone"][c] = s_vec.inc(ins)
        qsrc = qfm[:, :].rearrange("q (p w) -> q p w", p=4)[
            :, :, toff + 128 * ci : toff + 128 * ci + 128]
        rec["qt"][c] = s_vec.inc(VE.tensor_tensor(
            qt_sb[:, cp * 512 : cp * 512 + 512].rearrange("q (p w) -> q p w", p=4),
            qsrc, dpqf[:, :].rearrange("q (p w) -> q p w", p=4), ALU.mult))

    def E_vev(c):
        cs = c % 8
        pVb = pVQ[c % 2]
        VE.wait_ge(s_pe.h, rec["vv"][c])
        rec["scanv"][c] = s_vec.inc(VE.tensor_copy(
            vtm[:, cs * 520 : cs * 520 + 520].rearrange("p (h w) -> p h w", h=8)[:, :, 0:64],
            pVb[:, :].rearrange("p (h w) -> p h w", h=8)))

    def ktp_stage(c):
        b = c // 4
        ci = c % 4
        cp = c % 2
        toff = 512 * (b % 2)
        TE.wait_ge(s_vec.h, max(rec["tp2_free_v"], rec["kq_ready"][b % 2]))
        TE.wait_ge(s_sca.h, rec["tp2_free_s"])
        if c >= 2:
            TE.wait_ge(s_pe.h, rec["ydeN"][c - 2])
        for p in range(4):
            ins = TE.transpose(
                pT2[:, 128 * p : 128 * p + 128],
                kfm[:, 1024 * p + toff + 128 * ci : 1024 * p + toff + 128 * ci + 128],
                ident[:, :])
        t_ktp = s_pe.inc(ins)
        VE.wait_ge(s_pe.h, t_ktp)
        if c >= 2:
            VE.wait_ge(s_pe.h, rec["ydeN"][c - 2])
        v_ktm = s_vec.inc(VE.tensor_tensor(ktm_sb[:, cp * 512 : cp * 512 + 512],
                                           pT2[:, 0:512], dpkf[:, :], ALU.mult))
        rec["tp2_free_v"] = v_ktm
        rec["vktm"][c] = v_ktm

    def F_scan(c):
        cs, cp = c % 8, c % 2
        vb = cs * 520
        pVb = pVQ[c % 2]
        TE.wait_ge(s_vec.h, max(rec["vktm"][c], rec["scanv"][c]))
        for h in range(H):
            tp_pos = (0, 64 * (h % 2))
            ins = TE.matmul(pVb[64 * (h % 2) : 64 * (h % 2) + 64,
                               65 * (h // 2) : 65 * (h // 2) + 65],
                            ktm_sb[:, cp * 512 + 128 * (h // 2) + 64 * (h % 2) :
                                   cp * 512 + 128 * (h // 2) + 64 * (h % 2) + 64],
                            vtm[:, vb + 65 * h : vb + 65 * h + 65],
                            start=True, stop=True, tile_position=tp_pos)
        t_st = s_pe.inc(ins)
        rec["ydeN"][c] = t_st

        TE.wait_ge(s_vec.h, max(rec["maskdone"][c], rec["qt"][c]))
        if c > 0:
            TE.wait_ge(s_vec.h, rec["supd_v"][c - 1])
        TE.wait_ge(s_sca.h, max(rec["ge_evac"][c], rec["e8b"]))
        TE.matmul(pG[:, 0:H], ones_rowb[:, 0:128], eps_row8[:, :],
                  start=True, stop=False)
        for h in range(H):
            TE.matmul(pG[:, h : h + 1],
                      A_sb[:, cp * 1024 + 128 * h : cp * 1024 + 128 * h + 128],
                      ones_colb[:, :], start=False, stop=False)
        for p in range(4):
            ins = TE.matmul(pG[:, 2 * p : 2 * p + 2],
                            qt_sb[:, cp * 512 + 128 * p : cp * 512 + 128 * p + 128],
                            z_blk[:, 2 * p : 2 * p + 2],
                            start=False, stop=(p == 3))
        t_den = s_pe.inc(ins)
        for p in range(4):
            for e in range(2):
                h = 2 * p + e
                TE.matmul(pA0[:, 64 * h : 64 * h + 64],
                          A_sb[:, cp * 1024 + 128 * h : cp * 1024 + 128 * h + 128],
                          vtm[:, vb + 65 * h : vb + 65 * h + 64],
                          start=(e == 0), stop=False)
            ins = TE.matmul(pA0[:, 128 * p : 128 * p + 128],
                      qt_sb[:, cp * 512 + 128 * p : cp * 512 + 128 * p + 128],
                      S_blk[:, 128 * p : 128 * p + 128],
                      start=False, stop=True)
        t_y = s_pe.inc(ins)

        VE.wait_ge(s_pe.h, t_st)
        for p in range(4):
            VE.scalar_tensor_tensor(S_sb[:, 64 * p : 64 * p + 64],
                                    S_sb[:, 64 * p : 64 * p + 64],
                                    gcpair[:, p : p + 1],
                                    pVb[:, 65 * p : 65 * p + 64],
                                    ALU.mult, ALU.add)
            ins = VE.scalar_tensor_tensor(z_sb[:, p : p + 1], z_sb[:, p : p + 1],
                                          gcpair[:, p : p + 1],
                                          pVb[:, 65 * p + 64 : 65 * p + 65],
                                          ALU.mult, ALU.add)
        v_su = s_vec.inc(ins)
        rec["vfree"][c % 2] = v_su
        VE.wait_ge(s_pe.h, t_den)
        rec["recip"][c] = s_vec.inc(VE.reciprocal(den_r[:, cp * H : cp * H + H],
                                                  pG[:, 0:H]))
        rec["gfree_v"] = rec["recip"][c]
        rec["gfree_s"] = 0
        VE.drain()
        VE.wait_ge(s_pe.h, t_y)
        VE.tensor_copy(
            S_blk[0:64, :].rearrange("p (q w) -> p q w", q=4)[:, :, 0:64],
            S_sb[0:64, :].rearrange("p (q w) -> p q w", q=4))
        VE.tensor_copy(
            S_blk[64:128, :].rearrange("p (q w) -> p q w", q=4)[:, :, 64:128],
            S_sb[64:128, :].rearrange("p (q w) -> p q w", q=4))
        VE.tensor_copy(z_blk[0:64, :].rearrange("p (q w) -> p q w", q=4)[:, :, 0:1],
                       z_sb[0:64, :].rearrange("p (q w) -> p q w", q=4))
        ins = VE.tensor_copy(
            z_blk[64:128, :].rearrange("p (q w) -> p q w", q=4)[:, :, 1:2],
            z_sb[64:128, :].rearrange("p (q w) -> p q w", q=4))
        rec["supd_v"][c] = s_vec.inc(ins)
        VE.wait_ge(s_sca.h, rec["ge_evac"][c])
        VE.wait_ge(s_pe.h, t_y)
        for h in range(H):
            ins = VE.scalar_tensor_tensor(
                ygtm[:, cp * FH + 64 * h : cp * FH + 64 * h + 64],
                gsig[:, cp * FH + 64 * h : cp * FH + 64 * h + 64],
                den_r[:, cp * H + h : cp * H + h + 1],
                pA0[:, 64 * h : 64 * h + 64], ALU.mult, ALU.mult)
        rec["gat"][c] = s_vec.inc(ins)

    def G_ytp(c):
        cp = c % 2
        TE.wait_ge(s_vec.h, max(rec["gat"][c], rec["tp2_free_v"]))
        if c >= 2:
            TE.wait_ge(s_pe.h, rec["ow"][c - 2])
        TE.wait_ge(s_sca.h, rec["tp2_free_s"])
        for j in range(4):
            ins = TE.transpose(pT2[:, 512 + 128 * j : 512 + 128 * j + 128],
                               ygtm[:, cp * FH + 128 * j : cp * FH + 128 * j + 128],
                               ident[:, :])
        t_ytp = s_pe.inc(ins)
        SE.wait_ge(s_pe.h, t_ytp)
        ins = SE.copy(ygfm[:, cp * FH : cp * FH + FH].rearrange("p (j w) -> p j w", j=4),
                      pT2[:, 512:1024].rearrange("p (j w) -> p j w", j=4))
        rec["ytpe"][c] = s_sca.inc(ins)
        rec["tp2_free_s"] = rec["ytpe"][c]

    def G_out(c, n):
        b = c // 4
        ci = c % 4
        cs, cp = c % 8, c % 2
        bank = qk_bank()
        cnt, eng = rec["qk_free"][bank]
        TE.wait_ge(s_vec.h if eng == "v" else s_sca.h, cnt)
        if n == 0:
            TE.wait_ge(s_vec.h, rec["th"][b % 2])
            TE.wait_ge(s_wg["ow"], w_tot["ow"])
            if c == 0:
                TE.wait_ge(s_gp.h, ini["gp"])
        for k in range(KCH):
            TE.matmul(pQK[bank][:, :],
                      xnfm[:, 1024 * k + cs * 128 : 1024 * k + cs * 128 + 128],
                      wlin_sb[:, D * k + 512 * n : D * k + 512 * n + 512],
                      start=(k == 0), stop=False)
        TE.matmul(pQK[bank][:, :],
                  th_sb[:, (b % 2) * 512 + 128 * ci : (b % 2) * 512 + 128 * ci + 128],
                  right_sb[:, 512 * n : 512 * n + 512], start=False, stop=False)
        TE.matmul(pQK[bank][:, :], ones2[:, :],
                  blin_hl[:, 512 * n : 512 * n + 512], start=False, stop=False)
        if n == 0:
            TE.wait_ge(s_sca.h, rec["ytpe"][c])
        for j in range(4):
            ins = TE.matmul(pQK[bank][:, :],
                            ygfm[:, cp * FH + 128 * j : cp * FH + 128 * j + 128],
                            wo_sb[:, D * j + 512 * n : D * j + 512 * n + 512],
                            start=False, stop=(j == 3))
        t_half = s_pe.inc(ins)
        VE.wait_ge(s_pe.h, t_half)
        if n == 0 and c >= 2:
            VE.wait_ge(s_oc[c - 2], 16)
        cx = c % 10
        ins = VE.tensor_tensor(out_sb[:, cp * D + 512 * n : cp * D + 512 * n + 512],
                               pQK[bank][:, :],
                               x_sb[:, cx * D + 512 * n : cx * D + 512 * n + 512],
                               ALU.add)
        r_ev = s_vec.inc(ins)
        rec["qk_free"][bank] = [r_ev, "v"]
        if n == 1:
            rec["res"][c] = r_ev
            rec["ow"][c] = t_half
            SY.wait_ge(s_vec.h, r_ev)
            SY.dma_start(out_d[128 * c : 128 * c + 128, :],
                         out_sb[:, cp * D : cp * D + D]).then_inc(s_oc[c], 16)
            if c + 10 < NCHUNK:
                dma_xc(c + 10)
        else:
            rec["vh0"][c] = r_ev

    TE.wait_ge(s_c, 16)
    for c in range(4):
        A_stats(c)
    A_pre2(0)
    init_tail()
    for c in range(4):
        A_xn(c)
        A_ln_tp(c)
    B_proj(0)
    C_log_mm(0)
    E_vg(0)
    E_vev(0)
    ktp_stage(0)
    E_sc(0)
    C_top2(0)
    C_rest(0)
    for c in range(4, 8):
        A_stats(c)
    A_pre2(1)
    for c in range(NCHUNK):
        ci = c % 4
        F_scan(c)
        if ci != 3:
            E_vg(c + 1)
            E_vev(c + 1)
            G_ytp(c)
            if ci == 0 and c > 0:
                C_rest(c // 4)
            E_sc(c + 1)
            G_out(c, 0)
            ktp_stage(c + 1)
            G_out(c, 1)
            if c + 4 < NCHUNK:
                A_xn(c + 4)
            if c + 8 < NCHUNK:
                A_stats(c + 8)
            if ci in (1, 2) and c + 3 < NCHUNK:
                A_ln_tp(c + 3)
        else:
            if c < NCHUNK - 1:
                E_vg(c + 1)
                E_vev(c + 1)
            G_ytp(c)
            G_out(c, 0)
            G_out(c, 1)
            if c + 4 < NCHUNK:
                A_xn(c + 4)
            if c < NCHUNK - 1:
                A_ln_tp(c + 3)
                A_ln_tp(c + 4, bank2=True)
                B_proj((c + 1) // 4)
                C_log_mm((c + 1) // 4)
                if c + 8 < NCHUNK:
                    A_stats(c + 8)
                if (c + 1) // 4 + 1 < NBLK:
                    A_pre2((c + 1) // 4 + 1)
                ktp_stage(c + 1)
                E_sc(c + 1)
                C_top2((c + 1) // 4)
    for c in range(NCHUNK):
        SY.wait_ge(s_oc[c], 16)

    return nc


_NC_CACHE = []


def _get_nc():
    if not _NC_CACHE:
        _NC_CACHE.append(build_core())
    return _NC_CACHE[0]


def make_inmaps(inputs):
    f32 = np.float32
    asc = np.ascontiguousarray
    x = np.asarray(inputs["L0_attn_x"], f32)

    def wT(w, kk):
        return asc(np.asarray(w, f32).T).reshape(kk, 128, -1)

    wq = wT(inputs["L0_attn_W_Q"], KCH)
    wk = wT(inputs["L0_attn_W_K"], KCH)
    wv = wT(inputs["L0_attn_W_V"], KCH)
    wg = wT(inputs["L0_attn_W_g_w"], KCH)
    wo = wT(inputs["L0_attn_W_O"], 4)
    wlin = wT(inputs["L0_ffn_W_lin"], KCH)
    gatew = wT(inputs["L0_ffn_gate_w"], KCH)
    left = asc(np.transpose(np.asarray(inputs["L0_ffn_left"], f32), (1, 2, 0)
                            ).reshape(D, E * R)).reshape(KCH, 128, E * R)
    right = asc(np.transpose(np.asarray(inputs["L0_ffn_right"], f32), (1, 0, 2)
                             ).reshape(E * R, D))
    blin = np.asarray(inputs["L0_ffn_b_lin"], f32).reshape(1, D)
    wgb = np.asarray(inputs["L0_attn_W_g_b"], f32).reshape(1, FH)
    decay = np.asarray(inputs["L0_attn_decay_log"], f32).reshape(1, H)

    sig = np.arange(128, dtype=f32)[:, None]
    tau = np.arange(128, dtype=f32)[None, :]
    valid = (sig <= tau)
    expt = (-(tau - sig) * valid).astype(f32)
    maskadd = np.where(valid, 0.0, NEG).astype(f32)
    eqm = np.broadcast_to(-(tau + 1.0), (128, 128)).astype(f32)
    e127 = np.broadcast_to(-(127.0 - sig), (128, 64)).astype(f32)
    ident = np.eye(128, dtype=ml_dtypes.bfloat16)
    cmat = asc(np.concatenate([expt, maskadd, eqm, e127], axis=1))
    crow = asc(np.concatenate([blin, wgb, decay], axis=1))

    base = {"wq": wq, "wk": wk, "wv": wv, "wg": wg, "wo": wo, "wlin": wlin,
            "gatew": gatew, "left": left, "right": right,
            "ident": ident, "cmat": cmat, "crow": crow}
    in_maps = []
    for core in range(8):
        m = dict(base)
        m["x"] = asc(x[core])
        in_maps.append(m)
    return in_maps


def kernel(L0_attn_x, L0_attn_W_Q, L0_attn_W_K, L0_attn_W_V, L0_attn_W_g_w,
           L0_attn_W_g_b, L0_attn_W_O, L0_attn_decay_log, L0_ffn_W_lin,
           L0_ffn_b_lin, L0_ffn_gate_w, L0_ffn_left, L0_ffn_right):
    inputs = dict(
        L0_attn_x=L0_attn_x, L0_attn_W_Q=L0_attn_W_Q, L0_attn_W_K=L0_attn_W_K,
        L0_attn_W_V=L0_attn_W_V, L0_attn_W_g_w=L0_attn_W_g_w,
        L0_attn_W_g_b=L0_attn_W_g_b, L0_attn_W_O=L0_attn_W_O,
        L0_attn_decay_log=L0_attn_decay_log, L0_ffn_W_lin=L0_ffn_W_lin,
        L0_ffn_b_lin=L0_ffn_b_lin, L0_ffn_gate_w=L0_ffn_gate_w,
        L0_ffn_left=L0_ffn_left, L0_ffn_right=L0_ffn_right)
    nc = _get_nc()
    in_maps = make_inmaps(inputs)
    res = run_bass_kernel_spmd(nc, in_maps, core_ids=list(range(8)))
    return np.stack([np.asarray(r["out"], np.float32) for r in res.results], axis=0)


# version 36
# speedup vs baseline: 1.0015x; 1.0015x over previous
import numpy as np
import ml_dtypes
import concourse.bass as bass
import concourse.mybir as mybir
from concourse.bass_utils import run_bass_kernel_spmd

F32 = mybir.dt.float32
BF16 = mybir.dt.bfloat16
AF = mybir.ActivationFunctionType
ALU = mybir.AluOpType
AXX = mybir.AxisListType.X

T, D, H, DK = 2048, 1024, 8, 64
FH = H * DK
CH = 128
NCHUNK = T // CH
NBLK = NCHUNK // 4
KCH = D // 128
E, R = 8, 4
NEG = -1.0e30
TRUNC = 99


class Ctr:
    def __init__(self, nc, name):
        self.h = nc.alloc_semaphore(name)
        self.n = 0

    def inc(self, instr, k=1):
        instr.then_inc(self.h, k)
        self.n += k
        return self.n


def build_core():
    nc = bass.Bass("TRN2", target_bir_lowering=False)

    x_d = nc.dram_tensor("x", [T, D], F32, kind="ExternalInput").ap()
    wq_d = nc.dram_tensor("wq", [KCH, 128, FH], F32, kind="ExternalInput").ap()
    wk_d = nc.dram_tensor("wk", [KCH, 128, FH], F32, kind="ExternalInput").ap()
    wv_d = nc.dram_tensor("wv", [KCH, 128, FH], F32, kind="ExternalInput").ap()
    wg_d = nc.dram_tensor("wg", [KCH, 128, FH], F32, kind="ExternalInput").ap()
    wo_d = nc.dram_tensor("wo", [4, 128, D], F32, kind="ExternalInput").ap()
    wlin_d = nc.dram_tensor("wlin", [KCH, 128, D], F32, kind="ExternalInput").ap()
    gatew_d = nc.dram_tensor("gatew", [KCH, 128, E], F32, kind="ExternalInput").ap()
    left_d = nc.dram_tensor("left", [KCH, 128, E * R], F32, kind="ExternalInput").ap()
    right_d = nc.dram_tensor("right", [E * R, D], F32, kind="ExternalInput").ap()
    ident_d = nc.dram_tensor("ident", [128, 128], BF16, kind="ExternalInput").ap()
    cmat_d = nc.dram_tensor("cmat", [128, 448], F32, kind="ExternalInput").ap()
    crow_d = nc.dram_tensor("crow", [1, D + FH + H], F32, kind="ExternalInput").ap()
    out_d = nc.dram_tensor("out", [T, D], F32, kind="ExternalOutput").ap()

    A = nc.alloc_sbuf_tensor
    x_sb = A("x_sb", [128, 10 * D], F32)
    xnfm = A("xnfm", [128, KCH * 1024], BF16)
    qfm = A("qfm", [128, 4 * 1024], BF16)
    kfm = A("kfm", [128, 4 * 1024], BF16)
    vtm = A("vtm", [128, 8 * 520], BF16)
    gsig = A("gsig", [128, 2 * FH], BF16)
    ygfm = A("ygfm", [128, 2 * FH], BF16)
    wq_sb = A("wq_sb", [128, KCH * FH], BF16)
    wk_sb = A("wk_sb", [128, KCH * FH], BF16)
    wv_sb = A("wv_sb", [128, KCH * FH], BF16)
    wg_sb = A("wg_sb", [128, KCH * FH], BF16)
    wo_sb = A("wo_sb", [128, 4 * D], BF16)
    wlin_sb = A("wlin_sb", [128, KCH * D], BF16)
    gatew_sb = A("gatew_sb", [128, KCH * E], BF16)
    left_sb = A("left_sb", [128, KCH * E * R], BF16)
    right_sb = A("right_sb", [E * R, D], BF16)
    crow_s = A("crow_s", [1, D + FH + H], F32)
    blin_hl = A("blin_hl", [2, D], BF16)
    blin_lo_t = A("blin_lo_t", [1, D], BF16)
    wgb_lo_t = A("wgb_lo_t", [1, FH], BF16)
    wgb_hl = A("wgb_hl", [2, FH], BF16)
    nlg_u = A("nlg_u", [1, H], F32)
    nlg_row = A("nlg_row", [1, H], F32)
    nlg_hi = A("nlg_hi", [1, H], BF16)
    nlg_lo = A("nlg_lo", [1, H], BF16)
    ident = A("ident_s", [128, 128], BF16)
    cmat_s = A("cmat_s", [128, 448], F32)
    masks = A("masks", [128, H * 128], BF16)
    dpqf = A("dpqf", [128, 4 * 128], BF16)
    dpkf = A("dpkf", [128, 4 * 128], BF16)
    nlgrep = A("nlgrep", [128, H], F32)
    nlgpair = A("nlgpair", [128, 4], F32)
    gcpair = A("gcpair", [128, 4], F32)
    S_sb = A("S_sb", [128, 4 * 64], F32)
    S_blk = A("S_blk", [128, 4 * 128], BF16)
    z_blk = A("z_blk", [128, 8], BF16)
    z_sb = A("z_sb", [128, 4], F32)
    ones_rowb = A("ones_rowb", [1, 128], BF16)
    ones2 = A("ones2", [2, 128], BF16)
    ones_colb = A("ones_colb", [128, 1], BF16)
    eps_row8 = A("eps_row8", [1, H], BF16)
    eps5_col = A("eps5_col", [128, 1], F32)
    one_col = A("one_col", [128, 1], F32)
    xn_tm = A("xn_tm", [128, 4 * D], BF16)
    phiw = A("phiw", [128, 2 * FH], F32)
    phir = A("phir", [128, 2 * FH], BF16)
    A_sb = A("A_sb", [128, 2 * H * 128], BF16)
    qt_sb = A("qt_sb", [128, 2 * 512], BF16)
    ktm_sb = A("ktm_sb", [128, 2 * 512], BF16)
    ygtm = A("ygtm", [128, 2 * FH], BF16)
    den_r = A("den_r", [128, 2 * H], F32)
    wt_tm = A("wt_tm", [128, 4 * E], BF16)
    wt4 = A("wt4", [128, 4 * 4 * E], BF16)
    wtT = A("wtT", [E * R, T], BF16)
    th_sb = A("th_sb", [E * R, 2 * 512], BF16)
    e8 = A("e8", [128, 4 * E], F32)
    dmpA = A("dmpA", [128, 4 * E], F32)
    dmpB = A("dmpB", [128, 4 * E], F32)
    ind1 = A("ind1", [128, 4 * E], F32)
    i1inv = A("i1inv", [128, 4 * E], F32)
    tmp2 = A("tmp2", [128, 4 * E], F32)
    indf = A("indf", [128, 4 * E], F32)
    wtu = A("wtu", [128, 4 * E], F32)
    tcol = A("tcol", [128, 8], F32)
    varn = A("varn", [128, 8], F32)
    stdc = A("stdc", [128, 8], F32)
    rstdc = A("rstdc", [128, 8], F32)
    nmrc = A("nmrc", [128, 8], F32)
    rs8 = A("rs8", [128, 4], F32)
    cols = A("cols", [128, 48], F32)
    scr1b = A("scr1b", [128, D], BF16)
    out_sb = A("out_sb", [128, 2 * D], F32)

    pT = nc.alloc_psum_tensor("pT", [128, 1024], BF16)
    pQK = [nc.alloc_psum_tensor("pQK0", [128, 512], F32),
           nc.alloc_psum_tensor("pQK1", [128, 512], F32)]
    pV = nc.alloc_psum_tensor("pV", [128, 512], F32)
    pG = nc.alloc_psum_tensor("pG", [128, 512], F32)
    pA0 = nc.alloc_psum_tensor("pA0", [128, 512], F32)
    pT2 = nc.alloc_psum_tensor("pT2", [128, 1024], BF16)
    pO = nc.alloc_psum_tensor("pO", [128, 512], F32)
    pVQ = None

    s_pe = Ctr(nc, "s_pe")
    s_vec = Ctr(nc, "s_vec")
    s_sca = Ctr(nc, "s_sca")
    s_gp = Ctr(nc, "s_gp")
    s_xc = [nc.alloc_semaphore(f"s_x{i}") for i in range(NCHUNK)]
    s_wg = {g: nc.alloc_semaphore(f"s_w_{g}") for g in ["qk", "k2", "v", "g", "lg", "ow", "wo2"]}
    s_c = nc.alloc_semaphore("s_c")
    s_cr = nc.alloc_semaphore("s_cr")
    s_cm = nc.alloc_semaphore("s_cm")
    s_bl = nc.alloc_semaphore("s_bl")
    s_oc = [nc.alloc_semaphore(f"s_o{i}") for i in range(NCHUNK)]

    TE, VE, SE, SY, GP = nc.tensor, nc.vector, nc.scalar, nc.sync, nc.gpsimd
    pVQ = [pV, pO]

    def dma_xc(c):
        cx = c % 10
        SY.dma_start(x_sb[:, cx * D : cx * D + D],
                     x_d[128 * c : 128 * c + 128, :]).then_inc(s_xc[c], 16)

    SY.dma_start(ident[:, :], ident_d[:, :]).then_inc(s_c, 16)
    SY.dma_start(crow_s[:, :], crow_d[:, :]).then_inc(s_cr, 16)
    for _c in range(4):
        dma_xc(_c)

    for t_, v_ in [(ones_rowb, 1.0), (ones2, 1.0), (ones_colb, 1.0), (eps_row8, 1e-6),
                   (eps5_col, 1e-5), (one_col, 1.0), (S_sb, 0.0), (S_blk, 0.0),
                   (z_sb, 0.0), (z_blk, 0.0), (vtm, 1.0)]:
        ins = VE.memset(t_[:, :], v_)
    v_ms = s_vec.inc(ins)
    VE.wait_ge(s_cr, 16)
    VE.tensor_copy(blin_hl[0:1, :], crow_s[0:1, 0:D])
    v_bh = s_vec.inc(VE.tensor_copy(wgb_hl[0:1, :], crow_s[0:1, D : D + FH]))
    GP.wait_ge(s_vec.h, v_bh)
    GP.tensor_tensor(blin_lo_t[:, :], crow_s[0:1, 0:D], blin_hl[0:1, :], ALU.subtract)
    g_bl = s_gp.inc(GP.tensor_tensor(wgb_lo_t[:, :], crow_s[0:1, D : D + FH], wgb_hl[0:1, :],
                                     ALU.subtract))
    wmap = [(gatew_sb, gatew_d, "lg"), (left_sb, left_d, "lg"),
            (wq_sb, wq_d, "qk"), (wk_sb, wk_d, "k2"), (wv_sb, wv_d, "v"),
            (wg_sb, wg_d, "g"), (wlin_sb, wlin_d, "ow"), (wo_sb, wo_d, "wo2")]
    w_tot = {g: 0 for g in s_wg}
    for dst, srcd, grp in wmap:
        kk = srcd.shape[0]
        GP.dma_start(dst[:, :].rearrange("p (k c) -> p k c", k=kk),
                     srcd.rearrange("k p c -> p k c")).then_inc(s_wg[grp], 16)
        w_tot[grp] += 16
    GP.dma_start(right_sb[:, :], right_d[:, :]).then_inc(s_wg["ow"], 16)
    w_tot["ow"] += 16
    SY.wait_ge(s_gp.h, g_bl)
    SY.dma_start(blin_hl[1:2, :], blin_lo_t[:, :]).then_inc(s_bl, 16)
    SY.dma_start(wgb_hl[1:2, :], wgb_lo_t[:, :]).then_inc(s_bl, 16)
    SY.dma_start(cmat_s[:, :], cmat_d[:, :]).then_inc(s_cm, 16)
    for _c in range(4, 10):
        dma_xc(_c)

    ini = {"sca": 0, "vec": 0, "gp": 0}

    def init_tail():
        SE.wait_ge(s_cr, 16)
        s_e1 = s_sca.inc(SE.activation(nlg_u[:, :], crow_s[0:1, D + FH : D + FH + H], AF.Exp, scale=-1.0))
        VE.wait_ge(s_sca.h, s_e1)
        v_mn = s_vec.inc(VE.tensor_scalar(nlg_u[:, :], nlg_u[:, :], 1e15, None, ALU.min))
        SE.wait_ge(s_vec.h, max(v_mn, v_ms))
        s_ln = s_sca.inc(SE.activation(nlg_row[:, :], nlg_u[:, :], AF.Ln,
                                       bias=one_col[0:1, :]))
        VE.wait_ge(s_sca.h, s_ln)
        v_nh = s_vec.inc(VE.tensor_copy(nlg_hi[:, :], nlg_row[:, :]))
        GP.wait_ge(s_vec.h, v_nh)
        g_nl = s_gp.inc(GP.tensor_tensor(nlg_lo[:, :], nlg_row[:, :], nlg_hi[:, :],
                                         ALU.subtract))
        TE.wait_ge(s_vec.h, v_nh)
        TE.wait_ge(s_gp.h, g_nl)
        TE.matmul(pG[:, 0:H], ones_rowb[:, :], nlg_hi[:, :], start=True, stop=False)
        t_lg = s_pe.inc(TE.matmul(pG[:, 0:H], ones_rowb[:, :], nlg_lo[:, :],
                                  start=False, stop=True))
        SE.wait_ge(s_pe.h, t_lg)
        SE.copy(nlgrep[:, :], pG[:, 0:H])
        SE.drain()
        for p in range(4):
            SE.copy(nlgpair[0:64, p : p + 1], nlgrep[0:64, 2 * p : 2 * p + 1])
            ins = SE.copy(nlgpair[64:128, p : p + 1], nlgrep[64:128, 2 * p + 1 : 2 * p + 2])
        SE.drain()
        s_gc = s_sca.inc(SE.activation(gcpair[:, :], nlgpair[:, :], AF.Exp, scale=-128.0))
        VE.wait_ge(s_sca.h, s_gc)
        VE.wait_ge(s_cm, 16)
        expt_c = cmat_s[:, 0:128]
        maskadd_c = cmat_s[:, 128:256]
        eqm_c = cmat_s[:, 256:384]
        e127_c = cmat_s[:, 384:448]
        for h in range(H):
            VE.scalar_tensor_tensor(
                out_sb[:, 128 * h : 128 * h + 128], expt_c, nlgrep[:, h : h + 1],
                maskadd_c, ALU.mult, ALU.add)
        for p in range(4):
            VE.tensor_scalar(out_sb[:, 1024 + 128 * p : 1024 + 128 * p + 128],
                             eqm_c, nlgpair[:, p : p + 1], None, ALU.mult)
        for h in range(H):
            ins = VE.tensor_scalar(out_sb[:, 1536 + 64 * h : 1536 + 64 * h + 64],
                                   e127_c, nlgrep[:, h : h + 1], None, ALU.mult)
        v_ini = s_vec.inc(ins)
        SE.wait_ge(s_vec.h, v_ini)
        SE.activation(masks[:, :], out_sb[:, 0:1024], AF.Exp)
        SE.activation(dpqf[:, :], out_sb[:, 1024:1536], AF.Exp)
        ss = s_sca.inc(SE.activation(dpkf[:, :], out_sb[:, 1536:2048], AF.Exp))
        VE.wait_ge(s_sca.h, ss)
        ini["sca"] = s_sca.n
        ini["vec"] = s_vec.n
        ini["gp"] = s_gp.n
        rec["gfree_s"] = s_sca.n


    rec = {
        "ow": [0] * NCHUNK, "res": [0] * NCHUNK, "supd": [0] * NCHUNK,
        "gat": [0] * NCHUNK, "ydeN": [0] * NCHUNK, "maskdone": [0] * NCHUNK,
        "recip": [0] * NCHUNK, "ytpe": [0] * NCHUNK,
        "tp_free_v": 0, "tp_free_s": 0,
        "qk_free": [[0, "v"], [0, "v"], [0, "v"]],
        "vfree": [0, 0], "gfree_s": s_sca.n, "gfree_v": 0,
        "th": [0, 0], "ge_evac": [0] * NCHUNK, "phi_done_v": 0, "s_w4": 0,
        "kq_ready": [0, 0],
        "stats": [0] * NCHUNK, "nmr": [0] * NBLK, "e8b": 0,
    }
    qk_par = [0]

    def qk_bank():
        i = qk_par[0] % 2
        qk_par[0] += 1
        return i

    rec["t_tp"] = [0] * NCHUNK
    rec["s_xn"] = [0] * NCHUNK
    rec["xn_ev"] = [("v", 0)] * NCHUNK
    rec["tp2_free_v"] = 0
    rec["vh0"] = [0] * NCHUNK
    rec["supd_v"] = [0] * NCHUNK
    rec["tp2_free_s"] = 0
    rec["vktm"] = [0] * NCHUNK
    rec["scanv"] = [0] * NCHUNK
    rec["vv"] = [0] * NCHUNK
    rec["qt"] = [0] * NCHUNK

    def A_stats(c):
        cx, cs = c % 10, c % 8
        xc = x_sb[:, cx * D : cx * D + D]
        SE.wait_ge(s_xc[c], 16)
        if c > 0:
            SE.wait_ge(s_sca.h, rec["stats"][c - 1])
        s_cp = s_sca.inc(SE.activation(scr1b[:, :], xc, AF.Copy,
                                       accum_out=cols[:, cs : cs + 1]))
        SE.wait_ge(s_sca.h, s_cp)
        rec["stats"][c] = s_sca.inc(SE.activation(
            scr1b[:, :], xc, AF.Square, accum_out=cols[:, 8 + cs : 8 + cs + 1]))

    def A_pre2(bb):
        base = (4 * bb) % 8
        GP.wait_ge(s_sca.h, rec["stats"][4 * bb + 3])
        for i in range(4):
            cs = (4 * bb + i) % 8
            ins = GP.tensor_scalar(tcol[:, cs : cs + 1], cols[:, cs : cs + 1],
                                   cols[:, cs : cs + 1], -1.0 / (D * D),
                                   ALU.mult, ALU.mult)
        g_t = s_gp.inc(ins)
        VE.wait_ge(s_gp.h, g_t)
        v_var = s_vec.inc(VE.scalar_tensor_tensor(
            varn[:, base : base + 4], cols[:, 8 + base : 8 + base + 4],
            1.0 / D, tcol[:, base : base + 4], ALU.mult, ALU.add))
        SE.wait_ge(s_vec.h, v_var)
        s_lnv = s_sca.inc(SE.activation(stdc[:, base : base + 4],
                                        varn[:, base : base + 4],
                                        AF.Ln, bias=eps5_col[:, :]))
        SE.wait_ge(s_sca.h, s_lnv)
        s_rs = s_sca.inc(SE.activation(rstdc[:, base : base + 4],
                                       stdc[:, base : base + 4],
                                       AF.Exp, scale=-0.5))
        VE.wait_ge(s_sca.h, s_rs)
        rec["nmr"][bb] = s_vec.inc(VE.scalar_tensor_tensor(
            nmrc[:, base : base + 4], cols[:, base : base + 4], -1.0 / D,
            rstdc[:, base : base + 4], ALU.mult, ALU.mult))

    def A_xn(c):
        b = c // 4
        cx, cs, c4 = c % 10, c % 8, c % 4
        xc = x_sb[:, cx * D : cx * D + D]
        if c >= 4:
            SE.wait_ge(s_pe.h, rec["t_tp"][c - 4])
        SE.wait_ge(s_vec.h, rec["nmr"][b])
        rec["s_xn"][c] = s_sca.inc(SE.activation(xn_tm[:, c4 * D : c4 * D + D], xc,
                                       AF.Identity, scale=rstdc[:, cs : cs + 1],
                                       bias=nmrc[:, cs : cs + 1]))

    def A_ln_tp(c, bank2=False):
        cs, cp, c4 = c % 8, c % 2, c % 4
        pTb = pT2 if bank2 else pT
        TE.wait_ge(s_sca.h, rec["s_xn"][c])
        if bank2:
            TE.wait_ge(s_vec.h, rec["tp2_free_v"])
            TE.wait_ge(s_sca.h, rec["tp2_free_s"])
        else:
            TE.wait_ge(s_vec.h, rec["tp_free_v"])
            TE.wait_ge(s_sca.h, rec["tp_free_s"])
        for j in range(KCH):
            ins = TE.transpose(pTb[:, 128 * j : 128 * j + 128],
                               xn_tm[:, c4 * D + 128 * j : c4 * D + 128 * j + 128],
                               ident[:, :])
        t_tp = s_pe.inc(ins)
        rec["t_tp"][c] = t_tp
        xdst = xnfm[:, :].rearrange("p (k w) -> p k w", k=KCH)[:, :, cs * 128 : cs * 128 + 128]
        xsrc = pTb[:, :].rearrange("p (k w) -> p k w", k=KCH)
        if cp == 0:
            if c >= 8:
                VE.wait_ge(s_pe.h, rec["ow"][c - 8])
            VE.wait_ge(s_pe.h, t_tp)
            ev = s_vec.inc(VE.tensor_copy(xdst, xsrc))
            if bank2:
                rec["tp2_free_v"] = ev
            else:
                rec["tp_free_v"] = ev
            rec["xn_ev"][c] = ("v", ev)
        else:
            if c >= 8:
                SE.wait_ge(s_pe.h, rec["ow"][c - 8])
            SE.wait_ge(s_pe.h, t_tp)
            ev = s_sca.inc(SE.copy(xdst, xsrc))
            if bank2:
                rec["tp2_free_s"] = ev
            else:
                rec["tp_free_s"] = ev
            rec["xn_ev"][c] = ("s", ev)

    def wait_xn(b):
        mv, ms = ini["vec"], ini["sca"]
        for cc in range(4 * b, 4 * b + 4):
            e, n = rec["xn_ev"][cc]
            if e == "v":
                mv = max(mv, n)
            else:
                ms = max(ms, n)
        TE.wait_ge(s_vec.h, mv)
        TE.wait_ge(s_sca.h, ms)

    def B_proj(b):
        toff = 512 * (b % 2)
        wait_xn(b)
        TE.wait_ge(s_wg["qk"], w_tot["qk"])
        pbanks = [pQK[0], pQK[1], pA0]
        if b > 0:
            rec["qk_free"][2] = [rec["gat"][4 * b - 1], "v"]
        outs = [(wsb, dstfm, m) for (wsb, dstfm) in [(wq_sb, qfm), (wk_sb, kfm)]
                for m in range(4)]
        pend = None
        v_min = [0, 0]
        v_stt = [0, 0]
        s_exp = [0, 0]
        for i, (wsb, dstfm, m) in enumerate(outs):
            if i == 4:
                TE.wait_ge(s_wg["k2"], w_tot["k2"])
            bank = [2, 0, 1][i % 3]
            cnt, eng = rec["qk_free"][bank]
            TE.wait_ge(s_vec.h if eng == "v" else s_sca.h, cnt)
            for k in range(KCH):
                ins = TE.matmul(pbanks[bank][:, :],
                                wsb[:, FH * k + 128 * m : FH * k + 128 * m + 128],
                                xnfm[:, 1024 * k + toff : 1024 * k + toff + 512],
                                start=(k == 0), stop=(k == KCH - 1))
            t_g = s_pe.inc(ins)
            tp = i % 2
            VE.wait_ge(s_pe.h, t_g)
            if s_exp[tp]:
                VE.wait_ge(s_sca.h, s_exp[tp])
            v_min[tp] = s_vec.inc(VE.tensor_scalar(
                phiw[:, tp * FH : tp * FH + FH], pbanks[bank][:, :], 0.0, None,
                ALU.min))
            SE.wait_ge(s_vec.h, v_min[tp])
            if v_stt[tp]:
                SE.wait_ge(s_vec.h, v_stt[tp])
            s_exp[tp] = s_sca.inc(SE.activation(phir[:, tp * FH : tp * FH + FH],
                                                phiw[:, tp * FH : tp * FH + FH],
                                                AF.Exp))
            if pend is not None:
                pb, pd, pm, ptp, pse = pend
                VE.wait_ge(s_sca.h, pse)
                v_stt[ptp] = s_vec.inc(VE.scalar_tensor_tensor(
                    pd[:, 1024 * pm + toff : 1024 * pm + toff + 512],
                    pbanks[pb][:, :], 0.0,
                    phir[:, ptp * FH : ptp * FH + FH], ALU.max, ALU.add))
                rec["qk_free"][pb] = [v_stt[ptp], "v"]
            pend = (bank, dstfm, m, tp, s_exp[tp])
        pb, pd, pm, ptp, pse = pend
        VE.wait_ge(s_sca.h, pse)
        v_last = s_vec.inc(VE.scalar_tensor_tensor(
            pd[:, 1024 * pm + toff : 1024 * pm + toff + 512],
            pbanks[pb][:, :], 0.0,
            phir[:, ptp * FH : ptp * FH + FH], ALU.max, ALU.add))
        rec["qk_free"][pb] = [v_last, "v"]
        rec["phi_done_v"] = v_last
        rec["kq_ready"][b % 2] = v_last

    def C_log_mm(b):
        blo = 4 * b
        TE.wait_ge(s_sca.h, max(rec["ge_evac"][4 * b], ini["sca"]))
        wait_xn(b)
        TE.wait_ge(s_wg["lg"], w_tot["lg"])
        for ci in range(4):
            cs = (blo + ci) % 8
            for k in range(KCH):
                ins = TE.matmul(pG[:, 8 * ci : 8 * ci + 8],
                                xnfm[:, 1024 * k + cs * 128 : 1024 * k + cs * 128 + 128],
                                gatew_sb[:, E * k : E * k + E],
                                start=(k == 0), stop=(k == KCH - 1))
        t_l = s_pe.inc(ins)

        scol = lambda c, i: cols[:, 16 + (c % 4) * 4 + i : 16 + (c % 4) * 4 + i + 1]
        VE.wait_ge(s_pe.h, t_l)
        for ci in range(4):
            ins = VE.reduce_max(scol(blo + ci, 0), pG[:, 8 * ci : 8 * ci + 8],
                                AXX, negate=True)
        v_nm = s_vec.inc(ins)
        SE.wait_ge(s_vec.h, v_nm)
        for ci in range(4):
            c4 = (blo + ci) % 4
            ins = SE.activation(e8[:, c4 * E : c4 * E + E],
                                pG[:, 8 * ci : 8 * ci + 8],
                                AF.Exp, bias=scol(blo + ci, 0))
        s_e8 = s_sca.inc(ins)
        rec["e8b"] = s_e8
        rec["gfree_s"] = s_e8

    def C_top2(b):
        blo = 4 * b
        scol = lambda c, i: cols[:, 16 + (c % 4) * 4 + i : 16 + (c % 4) * 4 + i + 1]
        VE.wait_ge(s_sca.h, max(rec["e8b"], rec["s_w4"]))
        esl = lambda c4: e8[:, c4 * E : c4 * E + E]
        for ci in range(4):
            c4 = (blo + ci) % 4
            VE.tensor_scalar(dmpA[:, c4 * E : c4 * E + E], esl(c4), 1.0, NEG,
                             ALU.mult, ALU.max, accum_out=scol(blo + ci, 1))
            ins = VE.tensor_scalar(dmpB[:, c4 * E : c4 * E + E], esl(c4),
                                   1.0, 0.0, ALU.mult, ALU.add,
                                   accum_out=scol(blo + ci, 2))
        v_m1 = s_vec.inc(ins)
        VE.drain()
        for ci in range(4):
            ins = VE.reciprocal(rs8[:, (blo + ci) % 4 : (blo + ci) % 4 + 1],
                                scol(blo + ci, 2))
        v_rs = s_vec.inc(ins)
        GP.wait_ge(s_vec.h, v_m1)
        for ci in range(4):
            c4 = (blo + ci) % 4
            ins = GP.tensor_scalar(ind1[:, c4 * E : c4 * E + E], esl(c4),
                                   scol(blo + ci, 1), None, ALU.is_ge)
        g_i1 = s_gp.inc(ins)
        VE.wait_ge(s_gp.h, g_i1)
        for ci in range(4):
            c4 = (blo + ci) % 4
            ins = VE.scalar_tensor_tensor(tmp2[:, c4 * E : c4 * E + E],
                                          ind1[:, c4 * E : c4 * E + E],
                                          -1e30, esl(c4), ALU.mult, ALU.add)
        v_t2 = s_vec.inc(ins)
        VE.wait_ge(s_vec.h, v_t2)
        for ci in range(4):
            c4 = (blo + ci) % 4
            ins = VE.reduce_max(scol(blo + ci, 3), tmp2[:, c4 * E : c4 * E + E], AXX)
        v_m2 = s_vec.inc(ins)
        GP.wait_ge(s_vec.h, v_m2)
        for ci in range(4):
            c4 = (blo + ci) % 4
            ins = GP.tensor_scalar(indf[:, c4 * E : c4 * E + E], esl(c4),
                                   scol(blo + ci, 3), None, ALU.is_ge)
        g_ind = s_gp.inc(ins)
        VE.wait_ge(s_gp.h, g_ind)
        VE.wait_ge(s_vec.h, v_rs)
        for ci in range(4):
            c4 = (blo + ci) % 4
            ins = VE.scalar_tensor_tensor(wt_tm[:, c4 * E : c4 * E + E],
                                          indf[:, c4 * E : c4 * E + E],
                                          rs8[:, c4 : c4 + 1], esl(c4),
                                          ALU.mult, ALU.mult)
        v_wt = s_vec.inc(ins)
        SE.wait_ge(s_vec.h, v_wt)
        for ci in range(4):
            c4 = (blo + ci) % 4
            for r in range(R):
                ins = SE.copy(wt4[:, c4 * 32 + 8 * r : c4 * 32 + 8 * r + 8],
                              wt_tm[:, c4 * E : c4 * E + E])
        rec["s_w4"] = s_sca.inc(ins)

    def C_rest(b):
        blo = 4 * b
        toff = 512 * (b % 2)
        bank_t = qk_bank()
        cnt, eng = rec["qk_free"][bank_t]
        TE.wait_ge(s_vec.h if eng == "v" else s_sca.h, cnt)
        for k in range(KCH):
            ins = TE.matmul(pQK[bank_t][0:32, :],
                            left_sb[:, 32 * k : 32 * k + 32],
                            xnfm[:, 1024 * k + toff : 1024 * k + toff + 512],
                            start=(k == 0), stop=(k == KCH - 1))
        t_t = s_pe.inc(ins)
        TE.wait_ge(s_sca.h, max(rec["s_w4"], rec["tp_free_s"]))
        TE.wait_ge(s_vec.h, rec["tp_free_v"])
        for ci in range(4):
            c4 = (blo + ci) % 4
            ins = TE.transpose(pT[0:32, 128 * ci : 128 * ci + 128],
                               wt4[:, c4 * 32 : c4 * 32 + 32], ident[:, :])
        t_wt = s_pe.inc(ins)
        SE.wait_ge(s_pe.h, t_wt)
        for ci in range(4):
            ins = SE.copy(wtT[:, 128 * (blo + ci) : 128 * (blo + ci) + 128],
                          pT[0:32, 128 * ci : 128 * ci + 128])
        rec["tp_free_s"] = s_sca.inc(ins)

        bp = b % 2
        VE.wait_ge(s_pe.h, t_t)
        VE.wait_ge(s_sca.h, rec["tp_free_s"])
        if b >= 2:
            VE.wait_ge(s_pe.h, rec["ow"][4 * (b - 2) + 3])
        v_th = s_vec.inc(VE.tensor_tensor(th_sb[:, bp * 512 : bp * 512 + 512],
                                          pQK[bank_t][0:32, :],
                                          wtT[:, 512 * b : 512 * b + 512], ALU.mult))
        rec["th"][bp] = v_th
        rec["qk_free"][bank_t] = [v_th, "v"]

    def E_vg(c):
        cs = c % 8
        pVb = pVQ[c % 2]
        eng, cnt = rec["xn_ev"][c]
        TE.wait_ge(s_vec.h if eng == "v" else s_sca.h, cnt)
        TE.wait_ge(s_wg["v"], w_tot["v"])
        if c >= 2:
            TE.wait_ge(s_vec.h, rec["vfree"][c % 2])
        for k in range(KCH):
            ins = TE.matmul(pVb[:, :],
                            xnfm[:, 1024 * k + cs * 128 : 1024 * k + cs * 128 + 128],
                            wv_sb[:, FH * k : FH * k + FH],
                            start=(k == 0), stop=(k == KCH - 1))
        rec["vv"][c] = s_pe.inc(ins)
        TE.wait_ge(s_wg["g"], w_tot["g"])
        TE.wait_ge(s_bl, 32)
        if c == 0:
            TE.wait_ge(s_gp.h, ini["gp"])
        TE.wait_ge(s_sca.h, rec["gfree_s"])
        TE.wait_ge(s_vec.h, rec["gfree_v"])
        for k in range(KCH):
            TE.matmul(pG[:, :],
                      xnfm[:, 1024 * k + cs * 128 : 1024 * k + cs * 128 + 128],
                      wg_sb[:, FH * k : FH * k + FH],
                      start=(k == 0), stop=False)
        t_gp = s_pe.inc(TE.matmul(pG[:, :], ones2[:, :], wgb_hl[:, :],
                                  start=False, stop=True))
        SE.wait_ge(s_pe.h, t_gp)
        s_gs = s_sca.inc(SE.activation(gsig[:, (c % 2) * FH : (c % 2) * FH + FH],
                                       pG[:, :], AF.Sigmoid))
        rec["ge_evac"][c] = s_gs

    def E_sc(c):
        b = c // 4
        ci = c % 4
        cs, cp = c % 8, c % 2
        toff = 512 * (b % 2)
        pVb = pVQ[c % 2]
        TE.wait_ge(s_vec.h, rec["kq_ready"][b % 2])
        if c > 0:
            TE.wait_ge(s_vec.h, rec["gat"][c - 1])
        for h in [0, 2, 4, 6]:
            ins = TE.matmul(
                pA0[:, 128 * (h // 2) : 128 * (h // 2) + 128],
                kfm[0:64, 1024 * (h // 2) + toff + 128 * ci :
                    1024 * (h // 2) + toff + 128 * ci + 128],
                qfm[0:64, 1024 * (h // 2) + toff + 128 * ci :
                    1024 * (h // 2) + toff + 128 * ci + 128],
                start=True, stop=True)
        t_sc0 = s_pe.inc(ins)
        TE.wait_ge(s_sca.h, max(rec["ge_evac"][c], rec["e8b"]))
        for h in [1, 3, 5, 7]:
            ins = TE.matmul(
                pG[:, 128 * (h // 2) : 128 * (h // 2) + 128],
                kfm[64:128, 1024 * (h // 2) + toff + 128 * ci :
                    1024 * (h // 2) + toff + 128 * ci + 128],
                qfm[64:128, 1024 * (h // 2) + toff + 128 * ci :
                    1024 * (h // 2) + toff + 128 * ci + 128],
                start=True, stop=True)
        t_sc1 = s_pe.inc(ins)
        if c >= 2:
            VE.wait_ge(s_pe.h, rec["ydeN"][c - 2])
        VE.wait_ge(s_pe.h, t_sc1)
        for h in range(H):
            psrc = pA0 if h % 2 == 0 else pG
            ins = VE.tensor_tensor(
                A_sb[:, cp * 1024 + 128 * h : cp * 1024 + 128 * h + 128],
                psrc[:, 128 * (h // 2) : 128 * (h // 2) + 128],
                masks[:, 128 * h : 128 * h + 128], ALU.mult)
        rec["maskdone"][c] = s_vec.inc(ins)
        qsrc = qfm[:, :].rearrange("q (p w) -> q p w", p=4)[
            :, :, toff + 128 * ci : toff + 128 * ci + 128]
        rec["qt"][c] = s_vec.inc(VE.tensor_tensor(
            qt_sb[:, cp * 512 : cp * 512 + 512].rearrange("q (p w) -> q p w", p=4),
            qsrc, dpqf[:, :].rearrange("q (p w) -> q p w", p=4), ALU.mult))

    def E_vev(c):
        cs = c % 8
        pVb = pVQ[c % 2]
        VE.wait_ge(s_pe.h, rec["vv"][c])
        rec["scanv"][c] = s_vec.inc(VE.tensor_copy(
            vtm[:, cs * 520 : cs * 520 + 520].rearrange("p (h w) -> p h w", h=8)[:, :, 0:64],
            pVb[:, :].rearrange("p (h w) -> p h w", h=8)))

    def ktp_stage(c):
        b = c // 4
        ci = c % 4
        cp = c % 2
        toff = 512 * (b % 2)
        TE.wait_ge(s_vec.h, max(rec["tp2_free_v"], rec["kq_ready"][b % 2]))
        TE.wait_ge(s_sca.h, rec["tp2_free_s"])
        if c >= 2:
            TE.wait_ge(s_pe.h, rec["ydeN"][c - 2])
        for p in range(4):
            ins = TE.transpose(
                pT2[:, 128 * p : 128 * p + 128],
                kfm[:, 1024 * p + toff + 128 * ci : 1024 * p + toff + 128 * ci + 128],
                ident[:, :])
        t_ktp = s_pe.inc(ins)
        VE.wait_ge(s_pe.h, t_ktp)
        if c >= 2:
            VE.wait_ge(s_pe.h, rec["ydeN"][c - 2])
        v_ktm = s_vec.inc(VE.tensor_tensor(ktm_sb[:, cp * 512 : cp * 512 + 512],
                                           pT2[:, 0:512], dpkf[:, :], ALU.mult))
        rec["tp2_free_v"] = v_ktm
        rec["vktm"][c] = v_ktm

    def F_scan(c):
        cs, cp = c % 8, c % 2
        vb = cs * 520
        pVb = pVQ[c % 2]
        TE.wait_ge(s_vec.h, max(rec["vktm"][c], rec["scanv"][c]))
        for h in range(H):
            tp_pos = (0, 64 * (h % 2))
            ins = TE.matmul(pVb[64 * (h % 2) : 64 * (h % 2) + 64,
                               65 * (h // 2) : 65 * (h // 2) + 65],
                            ktm_sb[:, cp * 512 + 128 * (h // 2) + 64 * (h % 2) :
                                   cp * 512 + 128 * (h // 2) + 64 * (h % 2) + 64],
                            vtm[:, vb + 65 * h : vb + 65 * h + 65],
                            start=True, stop=True, tile_position=tp_pos)
        t_st = s_pe.inc(ins)
        rec["ydeN"][c] = t_st

        TE.wait_ge(s_vec.h, max(rec["maskdone"][c], rec["qt"][c]))
        if c > 0:
            TE.wait_ge(s_vec.h, rec["supd_v"][c - 1])
        TE.wait_ge(s_sca.h, max(rec["ge_evac"][c], rec["e8b"]))
        TE.matmul(pG[:, 0:H], ones_rowb[:, 0:128], eps_row8[:, :],
                  start=True, stop=False)
        for h in range(H):
            TE.matmul(pG[:, h : h + 1],
                      A_sb[:, cp * 1024 + 128 * h : cp * 1024 + 128 * h + 128],
                      ones_colb[:, :], start=False, stop=False)
        for p in range(4):
            ins = TE.matmul(pG[:, 2 * p : 2 * p + 2],
                            qt_sb[:, cp * 512 + 128 * p : cp * 512 + 128 * p + 128],
                            z_blk[:, 2 * p : 2 * p + 2],
                            start=False, stop=(p == 3))
        t_den = s_pe.inc(ins)
        for p in range(4):
            for e in range(2):
                h = 2 * p + e
                TE.matmul(pA0[:, 64 * h : 64 * h + 64],
                          A_sb[:, cp * 1024 + 128 * h : cp * 1024 + 128 * h + 128],
                          vtm[:, vb + 65 * h : vb + 65 * h + 64],
                          start=(e == 0), stop=False)
            ins = TE.matmul(pA0[:, 128 * p : 128 * p + 128],
                      qt_sb[:, cp * 512 + 128 * p : cp * 512 + 128 * p + 128],
                      S_blk[:, 128 * p : 128 * p + 128],
                      start=False, stop=True)
        t_y = s_pe.inc(ins)

        VE.wait_ge(s_pe.h, t_st)
        for p in range(4):
            VE.scalar_tensor_tensor(S_sb[:, 64 * p : 64 * p + 64],
                                    S_sb[:, 64 * p : 64 * p + 64],
                                    gcpair[:, p : p + 1],
                                    pVb[:, 65 * p : 65 * p + 64],
                                    ALU.mult, ALU.add)
            ins = VE.scalar_tensor_tensor(z_sb[:, p : p + 1], z_sb[:, p : p + 1],
                                          gcpair[:, p : p + 1],
                                          pVb[:, 65 * p + 64 : 65 * p + 65],
                                          ALU.mult, ALU.add)
        v_su = s_vec.inc(ins)
        rec["vfree"][c % 2] = v_su
        VE.wait_ge(s_pe.h, t_den)
        rec["recip"][c] = s_vec.inc(VE.reciprocal(den_r[:, cp * H : cp * H + H],
                                                  pG[:, 0:H]))
        rec["gfree_v"] = rec["recip"][c]
        rec["gfree_s"] = 0
        VE.drain()
        VE.wait_ge(s_pe.h, t_y)
        VE.tensor_copy(
            S_blk[0:64, :].rearrange("p (q w) -> p q w", q=4)[:, :, 0:64],
            S_sb[0:64, :].rearrange("p (q w) -> p q w", q=4))
        VE.tensor_copy(
            S_blk[64:128, :].rearrange("p (q w) -> p q w", q=4)[:, :, 64:128],
            S_sb[64:128, :].rearrange("p (q w) -> p q w", q=4))
        VE.tensor_copy(z_blk[0:64, :].rearrange("p (q w) -> p q w", q=4)[:, :, 0:1],
                       z_sb[0:64, :].rearrange("p (q w) -> p q w", q=4))
        ins = VE.tensor_copy(
            z_blk[64:128, :].rearrange("p (q w) -> p q w", q=4)[:, :, 1:2],
            z_sb[64:128, :].rearrange("p (q w) -> p q w", q=4))
        rec["supd_v"][c] = s_vec.inc(ins)
        VE.wait_ge(s_sca.h, rec["ge_evac"][c])
        VE.wait_ge(s_pe.h, t_y)
        for h in range(H):
            ins = VE.scalar_tensor_tensor(
                ygtm[:, cp * FH + 64 * h : cp * FH + 64 * h + 64],
                gsig[:, cp * FH + 64 * h : cp * FH + 64 * h + 64],
                den_r[:, cp * H + h : cp * H + h + 1],
                pA0[:, 64 * h : 64 * h + 64], ALU.mult, ALU.mult)
        rec["gat"][c] = s_vec.inc(ins)

    def G_ytp(c):
        cp = c % 2
        TE.wait_ge(s_vec.h, max(rec["gat"][c], rec["tp2_free_v"]))
        if c >= 2:
            TE.wait_ge(s_pe.h, rec["ow"][c - 2])
        TE.wait_ge(s_sca.h, rec["tp2_free_s"])
        for j in range(4):
            ins = TE.transpose(pT2[:, 512 + 128 * j : 512 + 128 * j + 128],
                               ygtm[:, cp * FH + 128 * j : cp * FH + 128 * j + 128],
                               ident[:, :])
        t_ytp = s_pe.inc(ins)
        SE.wait_ge(s_pe.h, t_ytp)
        ins = SE.copy(ygfm[:, cp * FH : cp * FH + FH].rearrange("p (j w) -> p j w", j=4),
                      pT2[:, 512:1024].rearrange("p (j w) -> p j w", j=4))
        rec["ytpe"][c] = s_sca.inc(ins)
        rec["tp2_free_s"] = rec["ytpe"][c]

    def G_out(c, n):
        b = c // 4
        ci = c % 4
        cs, cp = c % 8, c % 2
        bank = qk_bank()
        cnt, eng = rec["qk_free"][bank]
        TE.wait_ge(s_vec.h if eng == "v" else s_sca.h, cnt)
        if n == 0:
            TE.wait_ge(s_vec.h, rec["th"][b % 2])
            TE.wait_ge(s_wg["ow"], w_tot["ow"])
            if c == 0:
                TE.wait_ge(s_gp.h, ini["gp"])
        for k in range(KCH):
            TE.matmul(pQK[bank][:, :],
                      xnfm[:, 1024 * k + cs * 128 : 1024 * k + cs * 128 + 128],
                      wlin_sb[:, D * k + 512 * n : D * k + 512 * n + 512],
                      start=(k == 0), stop=False)
        TE.matmul(pQK[bank][:, :],
                  th_sb[:, (b % 2) * 512 + 128 * ci : (b % 2) * 512 + 128 * ci + 128],
                  right_sb[:, 512 * n : 512 * n + 512], start=False, stop=False)
        TE.matmul(pQK[bank][:, :], ones2[:, :],
                  blin_hl[:, 512 * n : 512 * n + 512], start=False, stop=False)
        if n == 0:
            TE.wait_ge(s_sca.h, rec["ytpe"][c])
            TE.wait_ge(s_wg["wo2"], w_tot["wo2"])
        for j in range(4):
            ins = TE.matmul(pQK[bank][:, :],
                            ygfm[:, cp * FH + 128 * j : cp * FH + 128 * j + 128],
                            wo_sb[:, D * j + 512 * n : D * j + 512 * n + 512],
                            start=False, stop=(j == 3))
        t_half = s_pe.inc(ins)
        VE.wait_ge(s_pe.h, t_half)
        if n == 0 and c >= 2:
            VE.wait_ge(s_oc[c - 2], 16)
        cx = c % 10
        ins = VE.tensor_tensor(out_sb[:, cp * D + 512 * n : cp * D + 512 * n + 512],
                               pQK[bank][:, :],
                               x_sb[:, cx * D + 512 * n : cx * D + 512 * n + 512],
                               ALU.add)
        r_ev = s_vec.inc(ins)
        rec["qk_free"][bank] = [r_ev, "v"]
        if n == 1:
            rec["res"][c] = r_ev
            rec["ow"][c] = t_half
            SY.wait_ge(s_vec.h, r_ev)
            SY.dma_start(out_d[128 * c : 128 * c + 128, :],
                         out_sb[:, cp * D : cp * D + D]).then_inc(s_oc[c], 16)
            if c + 10 < NCHUNK:
                dma_xc(c + 10)
        else:
            rec["vh0"][c] = r_ev

    TE.wait_ge(s_c, 16)
    for c in range(4):
        A_stats(c)
    A_pre2(0)
    init_tail()
    for c in range(4):
        A_xn(c)
        A_ln_tp(c)
    B_proj(0)
    C_log_mm(0)
    E_vg(0)
    E_vev(0)
    ktp_stage(0)
    E_sc(0)
    C_top2(0)
    C_rest(0)
    for c in range(4, 8):
        A_stats(c)
    A_pre2(1)
    for c in range(NCHUNK):
        ci = c % 4
        F_scan(c)
        if ci != 3:
            E_vg(c + 1)
            E_vev(c + 1)
            G_ytp(c)
            if ci == 0 and c > 0:
                C_rest(c // 4)
            E_sc(c + 1)
            G_out(c, 0)
            ktp_stage(c + 1)
            G_out(c, 1)
            if c + 4 < NCHUNK:
                A_xn(c + 4)
            if c + 8 < NCHUNK:
                A_stats(c + 8)
            if ci in (1, 2) and c + 3 < NCHUNK:
                A_ln_tp(c + 3)
        else:
            if c < NCHUNK - 1:
                E_vg(c + 1)
                E_vev(c + 1)
            G_ytp(c)
            G_out(c, 0)
            G_out(c, 1)
            if c + 4 < NCHUNK:
                A_xn(c + 4)
            if c < NCHUNK - 1:
                A_ln_tp(c + 3)
                A_ln_tp(c + 4, bank2=True)
                B_proj((c + 1) // 4)
                C_log_mm((c + 1) // 4)
                if c + 8 < NCHUNK:
                    A_stats(c + 8)
                if (c + 1) // 4 + 1 < NBLK:
                    A_pre2((c + 1) // 4 + 1)
                ktp_stage(c + 1)
                E_sc(c + 1)
                C_top2((c + 1) // 4)
    for c in range(NCHUNK):
        SY.wait_ge(s_oc[c], 16)

    return nc


_NC_CACHE = []


def _get_nc():
    if not _NC_CACHE:
        _NC_CACHE.append(build_core())
    return _NC_CACHE[0]


def make_inmaps(inputs):
    f32 = np.float32
    asc = np.ascontiguousarray
    x = np.asarray(inputs["L0_attn_x"], f32)

    def wT(w, kk):
        return asc(np.asarray(w, f32).T).reshape(kk, 128, -1)

    wq = wT(inputs["L0_attn_W_Q"], KCH)
    wk = wT(inputs["L0_attn_W_K"], KCH)
    wv = wT(inputs["L0_attn_W_V"], KCH)
    wg = wT(inputs["L0_attn_W_g_w"], KCH)
    wo = wT(inputs["L0_attn_W_O"], 4)
    wlin = wT(inputs["L0_ffn_W_lin"], KCH)
    gatew = wT(inputs["L0_ffn_gate_w"], KCH)
    left = asc(np.transpose(np.asarray(inputs["L0_ffn_left"], f32), (1, 2, 0)
                            ).reshape(D, E * R)).reshape(KCH, 128, E * R)
    right = asc(np.transpose(np.asarray(inputs["L0_ffn_right"], f32), (1, 0, 2)
                             ).reshape(E * R, D))
    blin = np.asarray(inputs["L0_ffn_b_lin"], f32).reshape(1, D)
    wgb = np.asarray(inputs["L0_attn_W_g_b"], f32).reshape(1, FH)
    decay = np.asarray(inputs["L0_attn_decay_log"], f32).reshape(1, H)

    sig = np.arange(128, dtype=f32)[:, None]
    tau = np.arange(128, dtype=f32)[None, :]
    valid = (sig <= tau)
    expt = (-(tau - sig) * valid).astype(f32)
    maskadd = np.where(valid, 0.0, NEG).astype(f32)
    eqm = np.broadcast_to(-(tau + 1.0), (128, 128)).astype(f32)
    e127 = np.broadcast_to(-(127.0 - sig), (128, 64)).astype(f32)
    ident = np.eye(128, dtype=ml_dtypes.bfloat16)
    cmat = asc(np.concatenate([expt, maskadd, eqm, e127], axis=1))
    crow = asc(np.concatenate([blin, wgb, decay], axis=1))

    base = {"wq": wq, "wk": wk, "wv": wv, "wg": wg, "wo": wo, "wlin": wlin,
            "gatew": gatew, "left": left, "right": right,
            "ident": ident, "cmat": cmat, "crow": crow}
    in_maps = []
    for core in range(8):
        m = dict(base)
        m["x"] = asc(x[core])
        in_maps.append(m)
    return in_maps


def kernel(L0_attn_x, L0_attn_W_Q, L0_attn_W_K, L0_attn_W_V, L0_attn_W_g_w,
           L0_attn_W_g_b, L0_attn_W_O, L0_attn_decay_log, L0_ffn_W_lin,
           L0_ffn_b_lin, L0_ffn_gate_w, L0_ffn_left, L0_ffn_right):
    inputs = dict(
        L0_attn_x=L0_attn_x, L0_attn_W_Q=L0_attn_W_Q, L0_attn_W_K=L0_attn_W_K,
        L0_attn_W_V=L0_attn_W_V, L0_attn_W_g_w=L0_attn_W_g_w,
        L0_attn_W_g_b=L0_attn_W_g_b, L0_attn_W_O=L0_attn_W_O,
        L0_attn_decay_log=L0_attn_decay_log, L0_ffn_W_lin=L0_ffn_W_lin,
        L0_ffn_b_lin=L0_ffn_b_lin, L0_ffn_gate_w=L0_ffn_gate_w,
        L0_ffn_left=L0_ffn_left, L0_ffn_right=L0_ffn_right)
    nc = _get_nc()
    in_maps = make_inmaps(inputs)
    res = run_bass_kernel_spmd(nc, in_maps, core_ids=list(range(8)))
    return np.stack([np.asarray(r["out"], np.float32) for r in res.results], axis=0)


# version 37
# speedup vs baseline: 1.0214x; 1.0198x over previous
import numpy as np
import ml_dtypes
import concourse.bass as bass
import concourse.mybir as mybir
from concourse.bass_utils import run_bass_kernel_spmd

F32 = mybir.dt.float32
BF16 = mybir.dt.bfloat16
AF = mybir.ActivationFunctionType
ALU = mybir.AluOpType
AXX = mybir.AxisListType.X

T, D, H, DK = 2048, 1024, 8, 64
FH = H * DK
CH = 128
NCHUNK = T // CH
NBLK = NCHUNK // 4
KCH = D // 128
E, R = 8, 4
NEG = -1.0e30
TRUNC = 99


class Ctr:
    def __init__(self, nc, name):
        self.h = nc.alloc_semaphore(name)
        self.n = 0

    def inc(self, instr, k=1):
        instr.then_inc(self.h, k)
        self.n += k
        return self.n


def build_core():
    nc = bass.Bass("TRN2", target_bir_lowering=False)

    x_d = nc.dram_tensor("x", [T, D], F32, kind="ExternalInput").ap()
    wq_d = nc.dram_tensor("wq", [KCH, 128, FH], F32, kind="ExternalInput").ap()
    wk_d = nc.dram_tensor("wk", [KCH, 128, FH], F32, kind="ExternalInput").ap()
    wv_d = nc.dram_tensor("wv", [KCH, 128, FH], F32, kind="ExternalInput").ap()
    wg_d = nc.dram_tensor("wg", [KCH, 128, FH], F32, kind="ExternalInput").ap()
    wo_d = nc.dram_tensor("wo", [4, 128, D], F32, kind="ExternalInput").ap()
    wlin_d = nc.dram_tensor("wlin", [KCH, 128, D], F32, kind="ExternalInput").ap()
    gatew_d = nc.dram_tensor("gatew", [KCH, 128, E], F32, kind="ExternalInput").ap()
    left_d = nc.dram_tensor("left", [KCH, 128, E * R], F32, kind="ExternalInput").ap()
    right_d = nc.dram_tensor("right", [E * R, D], F32, kind="ExternalInput").ap()
    ident_d = nc.dram_tensor("ident", [128, 128], BF16, kind="ExternalInput").ap()
    cmat_d = nc.dram_tensor("cmat", [128, 448], F32, kind="ExternalInput").ap()
    crow_d = nc.dram_tensor("crow", [1, D + FH + H], F32, kind="ExternalInput").ap()
    out_d = nc.dram_tensor("out", [T, D], F32, kind="ExternalOutput").ap()

    A = nc.alloc_sbuf_tensor
    x_sb = A("x_sb", [128, 10 * D], F32)
    xnfm = A("xnfm", [128, KCH * 1024], BF16)
    qfm = A("qfm", [128, 4 * 1024], BF16)
    kfm = A("kfm", [128, 4 * 1024], BF16)
    vtm = A("vtm", [128, 8 * 520], BF16)
    gsig = A("gsig", [128, 2 * FH], BF16)
    ygfm = A("ygfm", [128, 2 * FH], BF16)
    wq_sb = A("wq_sb", [128, KCH * FH], BF16)
    wk_sb = A("wk_sb", [128, KCH * FH], BF16)
    wv_sb = A("wv_sb", [128, KCH * FH], BF16)
    wg_sb = A("wg_sb", [128, KCH * FH], BF16)
    wo_sb = A("wo_sb", [128, 4 * D], BF16)
    wlin_sb = A("wlin_sb", [128, KCH * D], BF16)
    gatew_sb = A("gatew_sb", [128, KCH * E], BF16)
    left_sb = A("left_sb", [128, KCH * E * R], BF16)
    right_sb = A("right_sb", [E * R, D], BF16)
    crow_s = A("crow_s", [1, D + FH + H], F32)
    blin_hl = A("blin_hl", [2, D], BF16)
    blin_lo_t = A("blin_lo_t", [1, D], BF16)
    wgb_lo_t = A("wgb_lo_t", [1, FH], BF16)
    wgb_hl = A("wgb_hl", [2, FH], BF16)
    nlg_u = A("nlg_u", [1, H], F32)
    nlg_row = A("nlg_row", [1, H], F32)
    nlg_hi = A("nlg_hi", [1, H], BF16)
    nlg_lo = A("nlg_lo", [1, H], BF16)
    ident = A("ident_s", [128, 128], BF16)
    cmat_s = A("cmat_s", [128, 448], F32)
    masks = A("masks", [128, H * 128], BF16)
    dpqf = A("dpqf", [128, 4 * 128], BF16)
    dpkf = A("dpkf", [128, 4 * 128], BF16)
    nlgrep = A("nlgrep", [128, H], F32)
    nlgpair = A("nlgpair", [128, 4], F32)
    gcpair = A("gcpair", [128, 4], F32)
    S_sb = A("S_sb", [128, 4 * 64], F32)
    S_blk = A("S_blk", [128, 4 * 128], BF16)
    z_blk = A("z_blk", [128, 8], BF16)
    z_sb = A("z_sb", [128, 4], F32)
    ones_rowb = A("ones_rowb", [1, 128], BF16)
    ones2 = A("ones2", [2, 128], BF16)
    ones_colb = A("ones_colb", [128, 1], BF16)
    eps_row8 = A("eps_row8", [1, H], BF16)
    eps5_col = A("eps5_col", [128, 1], F32)
    one_col = A("one_col", [128, 1], F32)
    xn_tm = A("xn_tm", [128, 4 * D], BF16)
    phiw = A("phiw", [128, 2 * FH], F32)
    phir = A("phir", [128, 2 * FH], BF16)
    A_sb = A("A_sb", [128, 2 * H * 128], BF16)
    qt_sb = A("qt_sb", [128, 2 * 512], BF16)
    ktm_sb = A("ktm_sb", [128, 2 * 512], BF16)
    ygtm = A("ygtm", [128, 2 * FH], BF16)
    den_r = A("den_r", [128, 2 * H], F32)
    wt_tm = A("wt_tm", [128, 4 * E], BF16)
    wt4 = A("wt4", [128, 4 * 4 * E], BF16)
    wtT = A("wtT", [E * R, T], BF16)
    th_sb = A("th_sb", [E * R, 2 * 512], BF16)
    e8 = A("e8", [128, 4 * E], F32)
    dmpA = A("dmpA", [128, 4 * E], F32)
    dmpB = A("dmpB", [128, 4 * E], F32)
    ind1 = A("ind1", [128, 4 * E], F32)
    i1inv = A("i1inv", [128, 4 * E], F32)
    tmp2 = A("tmp2", [128, 4 * E], F32)
    indf = A("indf", [128, 4 * E], F32)
    wtu = A("wtu", [128, 4 * E], F32)
    tcol = A("tcol", [128, 8], F32)
    varn = A("varn", [128, 8], F32)
    stdc = A("stdc", [128, 8], F32)
    rstdc = A("rstdc", [128, 8], F32)
    nmrc = A("nmrc", [128, 8], F32)
    rs8 = A("rs8", [128, 4], F32)
    cols = A("cols", [128, 48], F32)
    scr1b = A("scr1b", [128, D], BF16)
    out_sb = A("out_sb", [128, 2 * D], F32)

    pT = nc.alloc_psum_tensor("pT", [128, 1024], BF16)
    pQK = [nc.alloc_psum_tensor("pQK0", [128, 512], F32),
           nc.alloc_psum_tensor("pQK1", [128, 512], F32)]
    pV = nc.alloc_psum_tensor("pV", [128, 512], F32)
    pG = nc.alloc_psum_tensor("pG", [128, 512], F32)
    pA0 = nc.alloc_psum_tensor("pA0", [128, 512], F32)
    pT2 = nc.alloc_psum_tensor("pT2", [128, 1024], BF16)
    pO = nc.alloc_psum_tensor("pO", [128, 512], F32)
    pVQ = None

    s_pe = Ctr(nc, "s_pe")
    s_vec = Ctr(nc, "s_vec")
    s_sca = Ctr(nc, "s_sca")
    s_gp = Ctr(nc, "s_gp")
    s_xc = [nc.alloc_semaphore(f"s_x{i}") for i in range(NCHUNK)]
    s_wg = {g: nc.alloc_semaphore(f"s_w_{g}") for g in ["qk", "k2", "v", "g", "lg", "ow", "wo2"]}
    s_c = nc.alloc_semaphore("s_c")
    s_cr = nc.alloc_semaphore("s_cr")
    s_cm = nc.alloc_semaphore("s_cm")
    s_bl = nc.alloc_semaphore("s_bl")
    s_oc = [nc.alloc_semaphore(f"s_o{i}") for i in range(NCHUNK)]

    TE, VE, SE, SY, GP = nc.tensor, nc.vector, nc.scalar, nc.sync, nc.gpsimd
    pVQ = [pV, pO]

    def dma_xc(c):
        cx = c % 10
        SY.dma_start(x_sb[:, cx * D : cx * D + D],
                     x_d[128 * c : 128 * c + 128, :]).then_inc(s_xc[c], 16)

    SY.dma_start(ident[:, :], ident_d[:, :]).then_inc(s_c, 16)
    SY.dma_start(crow_s[:, :], crow_d[:, :]).then_inc(s_cr, 16)
    for _c in range(4):
        dma_xc(_c)

    for t_, v_ in [(ones_rowb, 1.0), (ones2, 1.0), (ones_colb, 1.0), (eps_row8, 1e-6),
                   (eps5_col, 1e-5), (one_col, 1.0), (S_sb, 0.0), (S_blk, 0.0),
                   (z_sb, 0.0), (z_blk, 0.0), (vtm, 1.0)]:
        ins = VE.memset(t_[:, :], v_)
    v_ms = s_vec.inc(ins)
    VE.wait_ge(s_cr, 16)
    VE.tensor_copy(blin_hl[0:1, :], crow_s[0:1, 0:D])
    v_bh = s_vec.inc(VE.tensor_copy(wgb_hl[0:1, :], crow_s[0:1, D : D + FH]))
    GP.wait_ge(s_vec.h, v_bh)
    GP.tensor_tensor(blin_lo_t[:, :], crow_s[0:1, 0:D], blin_hl[0:1, :], ALU.subtract)
    g_bl = s_gp.inc(GP.tensor_tensor(wgb_lo_t[:, :], crow_s[0:1, D : D + FH], wgb_hl[0:1, :],
                                     ALU.subtract))
    wmap = [(gatew_sb, gatew_d, "lg"), (left_sb, left_d, "lg"),
            (wq_sb, wq_d, "qk"), (wk_sb, wk_d, "k2"), (wv_sb, wv_d, "v"),
            (wg_sb, wg_d, "g"), (wlin_sb, wlin_d, "ow"), (wo_sb, wo_d, "wo2")]
    w_tot = {g: 0 for g in s_wg}
    for dst, srcd, grp in wmap:
        kk = srcd.shape[0]
        GP.dma_start(dst[:, :].rearrange("p (k c) -> p k c", k=kk),
                     srcd.rearrange("k p c -> p k c")).then_inc(s_wg[grp], 16)
        w_tot[grp] += 16
    GP.dma_start(right_sb[:, :], right_d[:, :]).then_inc(s_wg["ow"], 16)
    w_tot["ow"] += 16
    SY.wait_ge(s_gp.h, g_bl)
    SY.dma_start(blin_hl[1:2, :], blin_lo_t[:, :]).then_inc(s_bl, 16)
    SY.dma_start(wgb_hl[1:2, :], wgb_lo_t[:, :]).then_inc(s_bl, 16)
    SY.dma_start(cmat_s[:, :], cmat_d[:, :]).then_inc(s_cm, 16)
    for _c in range(4, 10):
        dma_xc(_c)

    ini = {"sca": 0, "vec": 0, "gp": 0}

    def init_tail():
        SE.wait_ge(s_cr, 16)
        s_e1 = s_sca.inc(SE.activation(nlg_u[:, :], crow_s[0:1, D + FH : D + FH + H], AF.Exp, scale=-1.0))
        VE.wait_ge(s_sca.h, s_e1)
        v_mn = s_vec.inc(VE.tensor_scalar(nlg_u[:, :], nlg_u[:, :], 1e15, None, ALU.min))
        SE.wait_ge(s_vec.h, max(v_mn, v_ms))
        s_ln = s_sca.inc(SE.activation(nlg_row[:, :], nlg_u[:, :], AF.Ln,
                                       bias=one_col[0:1, :]))
        VE.wait_ge(s_sca.h, s_ln)
        v_nh = s_vec.inc(VE.tensor_copy(nlg_hi[:, :], nlg_row[:, :]))
        GP.wait_ge(s_vec.h, v_nh)
        g_nl = s_gp.inc(GP.tensor_tensor(nlg_lo[:, :], nlg_row[:, :], nlg_hi[:, :],
                                         ALU.subtract))
        TE.wait_ge(s_vec.h, v_nh)
        TE.wait_ge(s_gp.h, g_nl)
        TE.matmul(pG[:, 0:H], ones_rowb[:, :], nlg_hi[:, :], start=True, stop=False)
        t_lg = s_pe.inc(TE.matmul(pG[:, 0:H], ones_rowb[:, :], nlg_lo[:, :],
                                  start=False, stop=True))
        SE.wait_ge(s_pe.h, t_lg)
        SE.copy(nlgrep[:, :], pG[:, 0:H])
        SE.drain()
        for p in range(4):
            SE.copy(nlgpair[0:64, p : p + 1], nlgrep[0:64, 2 * p : 2 * p + 1])
            ins = SE.copy(nlgpair[64:128, p : p + 1], nlgrep[64:128, 2 * p + 1 : 2 * p + 2])
        SE.drain()
        s_gc = s_sca.inc(SE.activation(gcpair[:, :], nlgpair[:, :], AF.Exp, scale=-128.0))
        VE.wait_ge(s_sca.h, s_gc)
        VE.wait_ge(s_cm, 16)
        expt_c = cmat_s[:, 0:128]
        maskadd_c = cmat_s[:, 128:256]
        eqm_c = cmat_s[:, 256:384]
        e127_c = cmat_s[:, 384:448]
        for h in range(H):
            VE.scalar_tensor_tensor(
                out_sb[:, 128 * h : 128 * h + 128], expt_c, nlgrep[:, h : h + 1],
                maskadd_c, ALU.mult, ALU.add)
        for p in range(4):
            VE.tensor_scalar(out_sb[:, 1024 + 128 * p : 1024 + 128 * p + 128],
                             eqm_c, nlgpair[:, p : p + 1], None, ALU.mult)
        for h in range(H):
            ins = VE.tensor_scalar(out_sb[:, 1536 + 64 * h : 1536 + 64 * h + 64],
                                   e127_c, nlgrep[:, h : h + 1], None, ALU.mult)
        v_ini = s_vec.inc(ins)
        SE.wait_ge(s_vec.h, v_ini)
        SE.activation(masks[:, :], out_sb[:, 0:1024], AF.Exp)
        SE.activation(dpqf[:, :], out_sb[:, 1024:1536], AF.Exp)
        ss = s_sca.inc(SE.activation(dpkf[:, :], out_sb[:, 1536:2048], AF.Exp))
        VE.wait_ge(s_sca.h, ss)
        ini["sca"] = s_sca.n
        ini["vec"] = s_vec.n
        ini["gp"] = s_gp.n
        rec["gfree_s"] = s_sca.n


    rec = {
        "ow": [0] * NCHUNK, "res": [0] * NCHUNK, "supd": [0] * NCHUNK,
        "gat": [0] * NCHUNK, "ydeN": [0] * NCHUNK, "maskdone": [0] * NCHUNK,
        "recip": [0] * NCHUNK, "ytpe": [0] * NCHUNK,
        "tp_free_v": 0, "tp_free_s": 0,
        "qk_free": [[0, "v"], [0, "v"], [0, "v"]],
        "vfree": [0, 0], "gfree_s": s_sca.n, "gfree_v": 0,
        "th": [0, 0], "ge_evac": [0] * NCHUNK, "phi_done_v": 0, "s_w4": 0,
        "kq_ready": [0, 0],
        "stats": [0] * NCHUNK, "nmr": [0] * NBLK, "e8b": 0,
    }
    qk_par = [0]

    def qk_bank():
        i = qk_par[0] % 2
        qk_par[0] += 1
        return i

    rec["t_tp"] = [0] * NCHUNK
    rec["s_xn"] = [0] * NCHUNK
    rec["xn_ev"] = [("v", 0)] * NCHUNK
    rec["tp2_free_v"] = 0
    rec["vh0"] = [0] * NCHUNK
    rec["supd_v"] = [0] * NCHUNK
    rec["tp2_free_s"] = 0
    rec["vktm"] = [0] * NCHUNK
    rec["scanv"] = [0] * NCHUNK
    rec["vv"] = [0] * NCHUNK
    rec["qt"] = [0] * NCHUNK

    def A_stats(c):
        cx, cs = c % 10, c % 8
        xc = x_sb[:, cx * D : cx * D + D]
        SE.wait_ge(s_xc[c], 16)
        if c > 0:
            SE.wait_ge(s_sca.h, rec["stats"][c - 1])
        s_cp = s_sca.inc(SE.activation(scr1b[:, :], xc, AF.Copy,
                                       accum_out=cols[:, cs : cs + 1]))
        SE.wait_ge(s_sca.h, s_cp)
        rec["stats"][c] = s_sca.inc(SE.activation(
            scr1b[:, :], xc, AF.Square, accum_out=cols[:, 8 + cs : 8 + cs + 1]))

    def A_pre2(bb):
        base = (4 * bb) % 8
        GP.wait_ge(s_sca.h, rec["stats"][4 * bb + 3])
        for i in range(4):
            cs = (4 * bb + i) % 8
            ins = GP.tensor_scalar(tcol[:, cs : cs + 1], cols[:, cs : cs + 1],
                                   cols[:, cs : cs + 1], -1.0 / (D * D),
                                   ALU.mult, ALU.mult)
        g_t = s_gp.inc(ins)
        VE.wait_ge(s_gp.h, g_t)
        v_var = s_vec.inc(VE.scalar_tensor_tensor(
            varn[:, base : base + 4], cols[:, 8 + base : 8 + base + 4],
            1.0 / D, tcol[:, base : base + 4], ALU.mult, ALU.add))
        SE.wait_ge(s_vec.h, v_var)
        s_lnv = s_sca.inc(SE.activation(stdc[:, base : base + 4],
                                        varn[:, base : base + 4],
                                        AF.Ln, bias=eps5_col[:, :]))
        SE.wait_ge(s_sca.h, s_lnv)
        s_rs = s_sca.inc(SE.activation(rstdc[:, base : base + 4],
                                       stdc[:, base : base + 4],
                                       AF.Exp, scale=-0.5))
        VE.wait_ge(s_sca.h, s_rs)
        rec["nmr"][bb] = s_vec.inc(VE.scalar_tensor_tensor(
            nmrc[:, base : base + 4], cols[:, base : base + 4], -1.0 / D,
            rstdc[:, base : base + 4], ALU.mult, ALU.mult))

    def A_xn(c):
        b = c // 4
        cx, cs, c4 = c % 10, c % 8, c % 4
        xc = x_sb[:, cx * D : cx * D + D]
        if c >= 4:
            SE.wait_ge(s_pe.h, rec["t_tp"][c - 4])
        SE.wait_ge(s_vec.h, rec["nmr"][b])
        rec["s_xn"][c] = s_sca.inc(SE.activation(xn_tm[:, c4 * D : c4 * D + D], xc,
                                       AF.Identity, scale=rstdc[:, cs : cs + 1],
                                       bias=nmrc[:, cs : cs + 1]))

    def A_ln_tp(c, bank2=False):
        cs, cp, c4 = c % 8, c % 2, c % 4
        pTb = pT2 if bank2 else pT
        TE.wait_ge(s_sca.h, rec["s_xn"][c])
        if bank2:
            TE.wait_ge(s_vec.h, rec["tp2_free_v"])
            TE.wait_ge(s_sca.h, rec["tp2_free_s"])
        else:
            TE.wait_ge(s_vec.h, rec["tp_free_v"])
            TE.wait_ge(s_sca.h, rec["tp_free_s"])
        for j in range(KCH):
            ins = TE.transpose(pTb[:, 128 * j : 128 * j + 128],
                               xn_tm[:, c4 * D + 128 * j : c4 * D + 128 * j + 128],
                               ident[:, :])
        t_tp = s_pe.inc(ins)
        rec["t_tp"][c] = t_tp
        xdst = xnfm[:, :].rearrange("p (k w) -> p k w", k=KCH)[:, :, cs * 128 : cs * 128 + 128]
        xsrc = pTb[:, :].rearrange("p (k w) -> p k w", k=KCH)
        if cp == 0:
            if c >= 8:
                VE.wait_ge(s_pe.h, rec["ow"][c - 8])
            VE.wait_ge(s_pe.h, t_tp)
            ev = s_vec.inc(VE.tensor_copy(xdst, xsrc))
            if bank2:
                rec["tp2_free_v"] = ev
            else:
                rec["tp_free_v"] = ev
            rec["xn_ev"][c] = ("v", ev)
        else:
            if c >= 8:
                SE.wait_ge(s_pe.h, rec["ow"][c - 8])
            SE.wait_ge(s_pe.h, t_tp)
            ev = s_sca.inc(SE.copy(xdst, xsrc))
            if bank2:
                rec["tp2_free_s"] = ev
            else:
                rec["tp_free_s"] = ev
            rec["xn_ev"][c] = ("s", ev)

    def wait_xn(b):
        mv, ms = ini["vec"], ini["sca"]
        for cc in range(4 * b, 4 * b + 4):
            e, n = rec["xn_ev"][cc]
            if e == "v":
                mv = max(mv, n)
            else:
                ms = max(ms, n)
        TE.wait_ge(s_vec.h, mv)
        TE.wait_ge(s_sca.h, ms)

    def B_proj(b):
        toff = 512 * (b % 2)
        wait_xn(b)
        TE.wait_ge(s_wg["qk"], w_tot["qk"])
        pbanks = [pQK[0], pQK[1], pA0]
        if b > 0:
            rec["qk_free"][2] = [rec["gat"][4 * b - 1], "v"]
        outs = [(wsb, dstfm, m) for (wsb, dstfm) in [(wq_sb, qfm), (wk_sb, kfm)]
                for m in range(4)]
        pend = None
        v_min = [0, 0]
        v_stt = [0, 0]
        s_exp = [0, 0]
        for i, (wsb, dstfm, m) in enumerate(outs):
            if i == 4:
                TE.wait_ge(s_wg["k2"], w_tot["k2"])
            bank = [2, 0, 1][i % 3]
            cnt, eng = rec["qk_free"][bank]
            TE.wait_ge(s_vec.h if eng == "v" else s_sca.h, cnt)
            for k in range(KCH):
                ins = TE.matmul(pbanks[bank][:, :],
                                wsb[:, FH * k + 128 * m : FH * k + 128 * m + 128],
                                xnfm[:, 1024 * k + toff : 1024 * k + toff + 512],
                                start=(k == 0), stop=(k == KCH - 1))
            t_g = s_pe.inc(ins)
            tp = i % 2
            VE.wait_ge(s_pe.h, t_g)
            if s_exp[tp]:
                VE.wait_ge(s_sca.h, s_exp[tp])
            v_min[tp] = s_vec.inc(VE.tensor_scalar(
                phiw[:, tp * FH : tp * FH + FH], pbanks[bank][:, :], 0.0, None,
                ALU.min))
            SE.wait_ge(s_vec.h, v_min[tp])
            if v_stt[tp]:
                SE.wait_ge(s_vec.h, v_stt[tp])
            s_exp[tp] = s_sca.inc(SE.activation(phir[:, tp * FH : tp * FH + FH],
                                                phiw[:, tp * FH : tp * FH + FH],
                                                AF.Exp))
            if pend is not None:
                pb, pd, pm, ptp, pse = pend
                VE.wait_ge(s_sca.h, pse)
                v_stt[ptp] = s_vec.inc(VE.scalar_tensor_tensor(
                    pd[:, 1024 * pm + toff : 1024 * pm + toff + 512],
                    pbanks[pb][:, :], 0.0,
                    phir[:, ptp * FH : ptp * FH + FH], ALU.max, ALU.add))
                rec["qk_free"][pb] = [v_stt[ptp], "v"]
            pend = (bank, dstfm, m, tp, s_exp[tp])
        pb, pd, pm, ptp, pse = pend
        VE.wait_ge(s_sca.h, pse)
        v_last = s_vec.inc(VE.scalar_tensor_tensor(
            pd[:, 1024 * pm + toff : 1024 * pm + toff + 512],
            pbanks[pb][:, :], 0.0,
            phir[:, ptp * FH : ptp * FH + FH], ALU.max, ALU.add))
        rec["qk_free"][pb] = [v_last, "v"]
        rec["phi_done_v"] = v_last
        rec["kq_ready"][b % 2] = v_last

    def C_log_mm(b):
        blo = 4 * b
        TE.wait_ge(s_sca.h, max(rec["ge_evac"][4 * b], ini["sca"]))
        wait_xn(b)
        TE.wait_ge(s_wg["lg"], w_tot["lg"])
        for ci in range(4):
            cs = (blo + ci) % 8
            for k in range(KCH):
                ins = TE.matmul(pG[:, 8 * ci : 8 * ci + 8],
                                xnfm[:, 1024 * k + cs * 128 : 1024 * k + cs * 128 + 128],
                                gatew_sb[:, E * k : E * k + E],
                                start=(k == 0), stop=(k == KCH - 1))
        t_l = s_pe.inc(ins)

        scol = lambda c, i: cols[:, 16 + (c % 4) * 4 + i : 16 + (c % 4) * 4 + i + 1]
        VE.wait_ge(s_pe.h, t_l)
        for ci in range(4):
            ins = VE.reduce_max(scol(blo + ci, 0), pG[:, 8 * ci : 8 * ci + 8],
                                AXX, negate=True)
        v_nm = s_vec.inc(ins)
        SE.wait_ge(s_vec.h, v_nm)
        for ci in range(4):
            c4 = (blo + ci) % 4
            ins = SE.activation(e8[:, c4 * E : c4 * E + E],
                                pG[:, 8 * ci : 8 * ci + 8],
                                AF.Exp, bias=scol(blo + ci, 0))
        s_e8 = s_sca.inc(ins)
        rec["e8b"] = s_e8
        rec["gfree_s"] = s_e8

    def C_top2(b):
        blo = 4 * b
        scol = lambda c, i: cols[:, 16 + (c % 4) * 4 + i : 16 + (c % 4) * 4 + i + 1]
        VE.wait_ge(s_sca.h, max(rec["e8b"], rec["s_w4"]))
        esl = lambda c4: e8[:, c4 * E : c4 * E + E]
        for ci in range(4):
            c4 = (blo + ci) % 4
            VE.tensor_scalar(dmpA[:, c4 * E : c4 * E + E], esl(c4), 1.0, NEG,
                             ALU.mult, ALU.max, accum_out=scol(blo + ci, 1))
            ins = VE.tensor_scalar(dmpB[:, c4 * E : c4 * E + E], esl(c4),
                                   1.0, 0.0, ALU.mult, ALU.add,
                                   accum_out=scol(blo + ci, 2))
        v_m1 = s_vec.inc(ins)
        VE.drain()
        for ci in range(4):
            ins = VE.reciprocal(rs8[:, (blo + ci) % 4 : (blo + ci) % 4 + 1],
                                scol(blo + ci, 2))
        v_rs = s_vec.inc(ins)
        GP.wait_ge(s_vec.h, v_m1)
        for ci in range(4):
            c4 = (blo + ci) % 4
            ins = GP.tensor_scalar(ind1[:, c4 * E : c4 * E + E], esl(c4),
                                   scol(blo + ci, 1), None, ALU.is_ge)
        g_i1 = s_gp.inc(ins)
        VE.wait_ge(s_gp.h, g_i1)
        for ci in range(4):
            c4 = (blo + ci) % 4
            ins = VE.scalar_tensor_tensor(tmp2[:, c4 * E : c4 * E + E],
                                          ind1[:, c4 * E : c4 * E + E],
                                          -1e30, esl(c4), ALU.mult, ALU.add)
        v_t2 = s_vec.inc(ins)
        VE.wait_ge(s_vec.h, v_t2)
        for ci in range(4):
            c4 = (blo + ci) % 4
            ins = VE.reduce_max(scol(blo + ci, 3), tmp2[:, c4 * E : c4 * E + E], AXX)
        v_m2 = s_vec.inc(ins)
        GP.wait_ge(s_vec.h, v_m2)
        for ci in range(4):
            c4 = (blo + ci) % 4
            ins = GP.tensor_scalar(indf[:, c4 * E : c4 * E + E], esl(c4),
                                   scol(blo + ci, 3), None, ALU.is_ge)
        g_ind = s_gp.inc(ins)
        VE.wait_ge(s_gp.h, g_ind)
        VE.wait_ge(s_vec.h, v_rs)
        for ci in range(4):
            c4 = (blo + ci) % 4
            ins = VE.scalar_tensor_tensor(wt_tm[:, c4 * E : c4 * E + E],
                                          indf[:, c4 * E : c4 * E + E],
                                          rs8[:, c4 : c4 + 1], esl(c4),
                                          ALU.mult, ALU.mult)
        v_wt = s_vec.inc(ins)
        SE.wait_ge(s_vec.h, v_wt)
        for ci in range(4):
            c4 = (blo + ci) % 4
            for r in range(R):
                ins = SE.copy(wt4[:, c4 * 32 + 8 * r : c4 * 32 + 8 * r + 8],
                              wt_tm[:, c4 * E : c4 * E + E])
        rec["s_w4"] = s_sca.inc(ins)

    def C_rest(b):
        blo = 4 * b
        toff = 512 * (b % 2)
        bank_t = qk_bank()
        cnt, eng = rec["qk_free"][bank_t]
        TE.wait_ge(s_vec.h if eng == "v" else s_sca.h, cnt)
        for k in range(KCH):
            ins = TE.matmul(pQK[bank_t][0:32, :],
                            left_sb[:, 32 * k : 32 * k + 32],
                            xnfm[:, 1024 * k + toff : 1024 * k + toff + 512],
                            start=(k == 0), stop=(k == KCH - 1))
        t_t = s_pe.inc(ins)
        TE.wait_ge(s_sca.h, max(rec["s_w4"], rec["tp_free_s"]))
        TE.wait_ge(s_vec.h, rec["tp_free_v"])
        for ci in range(4):
            c4 = (blo + ci) % 4
            ins = TE.transpose(pT[0:32, 128 * ci : 128 * ci + 128],
                               wt4[:, c4 * 32 : c4 * 32 + 32], ident[:, :])
        t_wt = s_pe.inc(ins)
        SE.wait_ge(s_pe.h, t_wt)
        for ci in range(4):
            ins = SE.copy(wtT[:, 128 * (blo + ci) : 128 * (blo + ci) + 128],
                          pT[0:32, 128 * ci : 128 * ci + 128])
        rec["tp_free_s"] = s_sca.inc(ins)

        bp = b % 2
        VE.wait_ge(s_pe.h, t_t)
        VE.wait_ge(s_sca.h, rec["tp_free_s"])
        if b >= 2:
            VE.wait_ge(s_pe.h, rec["ow"][4 * (b - 2) + 3])
        v_th = s_vec.inc(VE.tensor_tensor(th_sb[:, bp * 512 : bp * 512 + 512],
                                          pQK[bank_t][0:32, :],
                                          wtT[:, 512 * b : 512 * b + 512], ALU.mult))
        rec["th"][bp] = v_th
        rec["qk_free"][bank_t] = [v_th, "v"]

    def E_vg(c):
        cs = c % 8
        pVb = pVQ[c % 2]
        eng, cnt = rec["xn_ev"][c]
        TE.wait_ge(s_vec.h if eng == "v" else s_sca.h, cnt)
        TE.wait_ge(s_wg["v"], w_tot["v"])
        if c >= 2:
            TE.wait_ge(s_vec.h, rec["vfree"][c % 2])
        for k in range(KCH):
            ins = TE.matmul(pVb[:, :],
                            xnfm[:, 1024 * k + cs * 128 : 1024 * k + cs * 128 + 128],
                            wv_sb[:, FH * k : FH * k + FH],
                            start=(k == 0), stop=(k == KCH - 1))
        rec["vv"][c] = s_pe.inc(ins)
        TE.wait_ge(s_wg["g"], w_tot["g"])
        TE.wait_ge(s_bl, 32)
        if c == 0:
            TE.wait_ge(s_gp.h, ini["gp"])
        TE.wait_ge(s_sca.h, rec["gfree_s"])
        TE.wait_ge(s_vec.h, rec["gfree_v"])
        for k in range(KCH):
            TE.matmul(pG[:, :],
                      xnfm[:, 1024 * k + cs * 128 : 1024 * k + cs * 128 + 128],
                      wg_sb[:, FH * k : FH * k + FH],
                      start=(k == 0), stop=False)
        t_gp = s_pe.inc(TE.matmul(pG[:, :], ones2[:, :], wgb_hl[:, :],
                                  start=False, stop=True))
        SE.wait_ge(s_pe.h, t_gp)
        s_gs = s_sca.inc(SE.activation(gsig[:, (c % 2) * FH : (c % 2) * FH + FH],
                                       pG[:, :], AF.Sigmoid))
        rec["ge_evac"][c] = s_gs

    def E_sc(c):
        b = c // 4
        ci = c % 4
        cs, cp = c % 8, c % 2
        toff = 512 * (b % 2)
        pVb = pVQ[c % 2]
        TE.wait_ge(s_vec.h, rec["kq_ready"][b % 2])
        if c > 0:
            TE.wait_ge(s_vec.h, rec["gat"][c - 1])
        for h in [0, 2, 4, 6]:
            ins = TE.matmul(
                pA0[:, 128 * (h // 2) : 128 * (h // 2) + 128],
                kfm[0:64, 1024 * (h // 2) + toff + 128 * ci :
                    1024 * (h // 2) + toff + 128 * ci + 128],
                qfm[0:64, 1024 * (h // 2) + toff + 128 * ci :
                    1024 * (h // 2) + toff + 128 * ci + 128],
                start=True, stop=True)
        t_sc0 = s_pe.inc(ins)
        TE.wait_ge(s_sca.h, max(rec["ge_evac"][c], rec["e8b"]))
        for h in [1, 3, 5, 7]:
            ins = TE.matmul(
                pG[:, 128 * (h // 2) : 128 * (h // 2) + 128],
                kfm[64:128, 1024 * (h // 2) + toff + 128 * ci :
                    1024 * (h // 2) + toff + 128 * ci + 128],
                qfm[64:128, 1024 * (h // 2) + toff + 128 * ci :
                    1024 * (h // 2) + toff + 128 * ci + 128],
                start=True, stop=True)
        t_sc1 = s_pe.inc(ins)
        if c >= 2:
            VE.wait_ge(s_pe.h, rec["ydeN"][c - 2])
        VE.wait_ge(s_pe.h, t_sc1)
        for h in range(H):
            psrc = pA0 if h % 2 == 0 else pG
            ins = VE.tensor_tensor(
                A_sb[:, cp * 1024 + 128 * h : cp * 1024 + 128 * h + 128],
                psrc[:, 128 * (h // 2) : 128 * (h // 2) + 128],
                masks[:, 128 * h : 128 * h + 128], ALU.mult)
        rec["maskdone"][c] = s_vec.inc(ins)
        qsrc = qfm[:, :].rearrange("q (p w) -> q p w", p=4)[
            :, :, toff + 128 * ci : toff + 128 * ci + 128]
        rec["qt"][c] = s_vec.inc(VE.tensor_tensor(
            qt_sb[:, cp * 512 : cp * 512 + 512].rearrange("q (p w) -> q p w", p=4),
            qsrc, dpqf[:, :].rearrange("q (p w) -> q p w", p=4), ALU.mult))

    def E_vev(c):
        cs = c % 8
        pVb = pVQ[c % 2]
        VE.wait_ge(s_pe.h, rec["vv"][c])
        rec["scanv"][c] = s_vec.inc(VE.tensor_copy(
            vtm[:, cs * 520 : cs * 520 + 520].rearrange("p (h w) -> p h w", h=8)[:, :, 0:64],
            pVb[:, :].rearrange("p (h w) -> p h w", h=8)))

    def ktp_stage(c):
        b = c // 4
        ci = c % 4
        cp = c % 2
        toff = 512 * (b % 2)
        TE.wait_ge(s_vec.h, max(rec["tp2_free_v"], rec["kq_ready"][b % 2]))
        TE.wait_ge(s_sca.h, rec["tp2_free_s"])
        if c >= 2:
            TE.wait_ge(s_pe.h, rec["ydeN"][c - 2])
        for p in range(4):
            ins = TE.transpose(
                pT2[:, 128 * p : 128 * p + 128],
                kfm[:, 1024 * p + toff + 128 * ci : 1024 * p + toff + 128 * ci + 128],
                ident[:, :])
        t_ktp = s_pe.inc(ins)
        VE.wait_ge(s_pe.h, t_ktp)
        if c >= 2:
            VE.wait_ge(s_pe.h, rec["ydeN"][c - 2])
        v_ktm = s_vec.inc(VE.tensor_tensor(ktm_sb[:, cp * 512 : cp * 512 + 512],
                                           pT2[:, 0:512], dpkf[:, :], ALU.mult))
        rec["tp2_free_v"] = v_ktm
        rec["vktm"][c] = v_ktm

    def F_scan(c):
        cs, cp = c % 8, c % 2
        vb = cs * 520
        pVb = pVQ[c % 2]
        TE.wait_ge(s_vec.h, max(rec["vktm"][c], rec["scanv"][c]))
        for h in range(H):
            tp_pos = (0, 64 * (h % 2))
            ins = TE.matmul(pVb[64 * (h % 2) : 64 * (h % 2) + 64,
                               65 * (h // 2) : 65 * (h // 2) + 65],
                            ktm_sb[:, cp * 512 + 128 * (h // 2) + 64 * (h % 2) :
                                   cp * 512 + 128 * (h // 2) + 64 * (h % 2) + 64],
                            vtm[:, vb + 65 * h : vb + 65 * h + 65],
                            start=True, stop=True, tile_position=tp_pos)
        t_st = s_pe.inc(ins)
        rec["ydeN"][c] = t_st

        TE.wait_ge(s_vec.h, max(rec["maskdone"][c], rec["qt"][c]))
        if c > 0:
            TE.wait_ge(s_vec.h, rec["supd_v"][c - 1])
        TE.wait_ge(s_sca.h, max(rec["ge_evac"][c], rec["e8b"]))
        TE.matmul(pG[:, 0:H], ones_rowb[:, 0:128], eps_row8[:, :],
                  start=True, stop=False)
        for h in range(H):
            TE.matmul(pG[:, h : h + 1],
                      A_sb[:, cp * 1024 + 128 * h : cp * 1024 + 128 * h + 128],
                      ones_colb[:, :], start=False, stop=False)
        for p in range(4):
            ins = TE.matmul(pG[:, 2 * p : 2 * p + 2],
                            qt_sb[:, cp * 512 + 128 * p : cp * 512 + 128 * p + 128],
                            z_blk[:, 2 * p : 2 * p + 2],
                            start=False, stop=(p == 3))
        t_den = s_pe.inc(ins)
        for p in range(4):
            for e in range(2):
                h = 2 * p + e
                TE.matmul(pA0[:, 64 * h : 64 * h + 64],
                          A_sb[:, cp * 1024 + 128 * h : cp * 1024 + 128 * h + 128],
                          vtm[:, vb + 65 * h : vb + 65 * h + 64],
                          start=(e == 0), stop=False)
            ins = TE.matmul(pA0[:, 128 * p : 128 * p + 128],
                      qt_sb[:, cp * 512 + 128 * p : cp * 512 + 128 * p + 128],
                      S_blk[:, 128 * p : 128 * p + 128],
                      start=False, stop=True)
        t_y = s_pe.inc(ins)

        VE.wait_ge(s_pe.h, t_st)
        for p in range(4):
            VE.scalar_tensor_tensor(S_sb[:, 64 * p : 64 * p + 64],
                                    S_sb[:, 64 * p : 64 * p + 64],
                                    gcpair[:, p : p + 1],
                                    pVb[:, 65 * p : 65 * p + 64],
                                    ALU.mult, ALU.add)
            ins = VE.scalar_tensor_tensor(z_sb[:, p : p + 1], z_sb[:, p : p + 1],
                                          gcpair[:, p : p + 1],
                                          pVb[:, 65 * p + 64 : 65 * p + 65],
                                          ALU.mult, ALU.add)
        v_su = s_vec.inc(ins)
        rec["vfree"][c % 2] = v_su
        VE.wait_ge(s_pe.h, t_den)
        rec["recip"][c] = s_vec.inc(VE.reciprocal(den_r[:, cp * H : cp * H + H],
                                                  pG[:, 0:H]))
        rec["gfree_v"] = rec["recip"][c]
        rec["gfree_s"] = 0
        VE.drain()
        VE.wait_ge(s_sca.h, rec["ge_evac"][c])
        VE.wait_ge(s_pe.h, t_y)
        for h in range(H):
            ins = VE.scalar_tensor_tensor(
                ygtm[:, cp * FH + 64 * h : cp * FH + 64 * h + 64],
                gsig[:, cp * FH + 64 * h : cp * FH + 64 * h + 64],
                den_r[:, cp * H + h : cp * H + h + 1],
                pA0[:, 64 * h : 64 * h + 64], ALU.mult, ALU.mult)
        rec["gat"][c] = s_vec.inc(ins)
        VE.tensor_copy(
            S_blk[0:64, :].rearrange("p (q w) -> p q w", q=4)[:, :, 0:64],
            S_sb[0:64, :].rearrange("p (q w) -> p q w", q=4))
        VE.tensor_copy(
            S_blk[64:128, :].rearrange("p (q w) -> p q w", q=4)[:, :, 64:128],
            S_sb[64:128, :].rearrange("p (q w) -> p q w", q=4))
        VE.tensor_copy(z_blk[0:64, :].rearrange("p (q w) -> p q w", q=4)[:, :, 0:1],
                       z_sb[0:64, :].rearrange("p (q w) -> p q w", q=4))
        ins = VE.tensor_copy(
            z_blk[64:128, :].rearrange("p (q w) -> p q w", q=4)[:, :, 1:2],
            z_sb[64:128, :].rearrange("p (q w) -> p q w", q=4))
        rec["supd_v"][c] = s_vec.inc(ins)

    def G_ytp(c):
        cp = c % 2
        TE.wait_ge(s_vec.h, max(rec["gat"][c], rec["tp2_free_v"]))
        if c >= 2:
            TE.wait_ge(s_pe.h, rec["ow"][c - 2])
        TE.wait_ge(s_sca.h, rec["tp2_free_s"])
        for j in range(4):
            ins = TE.transpose(pT2[:, 512 + 128 * j : 512 + 128 * j + 128],
                               ygtm[:, cp * FH + 128 * j : cp * FH + 128 * j + 128],
                               ident[:, :])
        t_ytp = s_pe.inc(ins)
        SE.wait_ge(s_pe.h, t_ytp)
        ins = SE.copy(ygfm[:, cp * FH : cp * FH + FH].rearrange("p (j w) -> p j w", j=4),
                      pT2[:, 512:1024].rearrange("p (j w) -> p j w", j=4))
        rec["ytpe"][c] = s_sca.inc(ins)
        rec["tp2_free_s"] = rec["ytpe"][c]

    def G_out(c, n):
        b = c // 4
        ci = c % 4
        cs, cp = c % 8, c % 2
        bank = qk_bank()
        cnt, eng = rec["qk_free"][bank]
        TE.wait_ge(s_vec.h if eng == "v" else s_sca.h, cnt)
        if n == 0:
            TE.wait_ge(s_vec.h, rec["th"][b % 2])
            TE.wait_ge(s_wg["ow"], w_tot["ow"])
            if c == 0:
                TE.wait_ge(s_gp.h, ini["gp"])
        for k in range(KCH):
            TE.matmul(pQK[bank][:, :],
                      xnfm[:, 1024 * k + cs * 128 : 1024 * k + cs * 128 + 128],
                      wlin_sb[:, D * k + 512 * n : D * k + 512 * n + 512],
                      start=(k == 0), stop=False)
        TE.matmul(pQK[bank][:, :],
                  th_sb[:, (b % 2) * 512 + 128 * ci : (b % 2) * 512 + 128 * ci + 128],
                  right_sb[:, 512 * n : 512 * n + 512], start=False, stop=False)
        TE.matmul(pQK[bank][:, :], ones2[:, :],
                  blin_hl[:, 512 * n : 512 * n + 512], start=False, stop=False)
        if n == 0:
            TE.wait_ge(s_sca.h, rec["ytpe"][c])
            TE.wait_ge(s_wg["wo2"], w_tot["wo2"])
        for j in range(4):
            ins = TE.matmul(pQK[bank][:, :],
                            ygfm[:, cp * FH + 128 * j : cp * FH + 128 * j + 128],
                            wo_sb[:, D * j + 512 * n : D * j + 512 * n + 512],
                            start=False, stop=(j == 3))
        t_half = s_pe.inc(ins)
        VE.wait_ge(s_pe.h, t_half)
        if n == 0 and c >= 2:
            VE.wait_ge(s_oc[c - 2], 16)
        cx = c % 10
        ins = VE.tensor_tensor(out_sb[:, cp * D + 512 * n : cp * D + 512 * n + 512],
                               pQK[bank][:, :],
                               x_sb[:, cx * D + 512 * n : cx * D + 512 * n + 512],
                               ALU.add)
        r_ev = s_vec.inc(ins)
        rec["qk_free"][bank] = [r_ev, "v"]
        if n == 1:
            rec["res"][c] = r_ev
            rec["ow"][c] = t_half
            SY.wait_ge(s_vec.h, r_ev)
            SY.dma_start(out_d[128 * c : 128 * c + 128, :],
                         out_sb[:, cp * D : cp * D + D]).then_inc(s_oc[c], 16)
            if c + 10 < NCHUNK:
                dma_xc(c + 10)
        else:
            rec["vh0"][c] = r_ev

    TE.wait_ge(s_c, 16)
    for c in range(4):
        A_stats(c)
    A_pre2(0)
    init_tail()
    for c in range(4):
        A_xn(c)
        A_ln_tp(c)
    B_proj(0)
    C_log_mm(0)
    E_vg(0)
    E_vev(0)
    ktp_stage(0)
    E_sc(0)
    C_top2(0)
    C_rest(0)
    for c in range(4, 8):
        A_stats(c)
    A_pre2(1)
    for c in range(NCHUNK):
        ci = c % 4
        F_scan(c)
        if ci != 3:
            E_vg(c + 1)
            E_vev(c + 1)
            G_ytp(c)
            if ci == 0 and c > 0:
                C_rest(c // 4)
            E_sc(c + 1)
            G_out(c, 0)
            ktp_stage(c + 1)
            G_out(c, 1)
            if c + 4 < NCHUNK:
                A_xn(c + 4)
            if c + 8 < NCHUNK:
                A_stats(c + 8)
            if ci in (1, 2) and c + 3 < NCHUNK:
                A_ln_tp(c + 3)
        else:
            if c < NCHUNK - 1:
                E_vg(c + 1)
                E_vev(c + 1)
            G_ytp(c)
            G_out(c, 0)
            G_out(c, 1)
            if c + 4 < NCHUNK:
                A_xn(c + 4)
            if c < NCHUNK - 1:
                A_ln_tp(c + 3)
                A_ln_tp(c + 4, bank2=True)
                B_proj((c + 1) // 4)
                C_log_mm((c + 1) // 4)
                if c + 8 < NCHUNK:
                    A_stats(c + 8)
                if (c + 1) // 4 + 1 < NBLK:
                    A_pre2((c + 1) // 4 + 1)
                ktp_stage(c + 1)
                E_sc(c + 1)
                C_top2((c + 1) // 4)
    for c in range(NCHUNK):
        SY.wait_ge(s_oc[c], 16)

    return nc


_NC_CACHE = []


def _get_nc():
    if not _NC_CACHE:
        _NC_CACHE.append(build_core())
    return _NC_CACHE[0]


def make_inmaps(inputs):
    f32 = np.float32
    asc = np.ascontiguousarray
    x = np.asarray(inputs["L0_attn_x"], f32)

    def wT(w, kk):
        return asc(np.asarray(w, f32).T).reshape(kk, 128, -1)

    wq = wT(inputs["L0_attn_W_Q"], KCH)
    wk = wT(inputs["L0_attn_W_K"], KCH)
    wv = wT(inputs["L0_attn_W_V"], KCH)
    wg = wT(inputs["L0_attn_W_g_w"], KCH)
    wo = wT(inputs["L0_attn_W_O"], 4)
    wlin = wT(inputs["L0_ffn_W_lin"], KCH)
    gatew = wT(inputs["L0_ffn_gate_w"], KCH)
    left = asc(np.transpose(np.asarray(inputs["L0_ffn_left"], f32), (1, 2, 0)
                            ).reshape(D, E * R)).reshape(KCH, 128, E * R)
    right = asc(np.transpose(np.asarray(inputs["L0_ffn_right"], f32), (1, 0, 2)
                             ).reshape(E * R, D))
    blin = np.asarray(inputs["L0_ffn_b_lin"], f32).reshape(1, D)
    wgb = np.asarray(inputs["L0_attn_W_g_b"], f32).reshape(1, FH)
    decay = np.asarray(inputs["L0_attn_decay_log"], f32).reshape(1, H)

    sig = np.arange(128, dtype=f32)[:, None]
    tau = np.arange(128, dtype=f32)[None, :]
    valid = (sig <= tau)
    expt = (-(tau - sig) * valid).astype(f32)
    maskadd = np.where(valid, 0.0, NEG).astype(f32)
    eqm = np.broadcast_to(-(tau + 1.0), (128, 128)).astype(f32)
    e127 = np.broadcast_to(-(127.0 - sig), (128, 64)).astype(f32)
    ident = np.eye(128, dtype=ml_dtypes.bfloat16)
    cmat = asc(np.concatenate([expt, maskadd, eqm, e127], axis=1))
    crow = asc(np.concatenate([blin, wgb, decay], axis=1))

    base = {"wq": wq, "wk": wk, "wv": wv, "wg": wg, "wo": wo, "wlin": wlin,
            "gatew": gatew, "left": left, "right": right,
            "ident": ident, "cmat": cmat, "crow": crow}
    in_maps = []
    for core in range(8):
        m = dict(base)
        m["x"] = asc(x[core])
        in_maps.append(m)
    return in_maps


def kernel(L0_attn_x, L0_attn_W_Q, L0_attn_W_K, L0_attn_W_V, L0_attn_W_g_w,
           L0_attn_W_g_b, L0_attn_W_O, L0_attn_decay_log, L0_ffn_W_lin,
           L0_ffn_b_lin, L0_ffn_gate_w, L0_ffn_left, L0_ffn_right):
    inputs = dict(
        L0_attn_x=L0_attn_x, L0_attn_W_Q=L0_attn_W_Q, L0_attn_W_K=L0_attn_W_K,
        L0_attn_W_V=L0_attn_W_V, L0_attn_W_g_w=L0_attn_W_g_w,
        L0_attn_W_g_b=L0_attn_W_g_b, L0_attn_W_O=L0_attn_W_O,
        L0_attn_decay_log=L0_attn_decay_log, L0_ffn_W_lin=L0_ffn_W_lin,
        L0_ffn_b_lin=L0_ffn_b_lin, L0_ffn_gate_w=L0_ffn_gate_w,
        L0_ffn_left=L0_ffn_left, L0_ffn_right=L0_ffn_right)
    nc = _get_nc()
    in_maps = make_inmaps(inputs)
    res = run_bass_kernel_spmd(nc, in_maps, core_ids=list(range(8)))
    return np.stack([np.asarray(r["out"], np.float32) for r in res.results], axis=0)


# version 38 (final)
# speedup vs baseline: 1.0240x; 1.0026x over previous
import numpy as np
import ml_dtypes
import concourse.bass as bass
import concourse.mybir as mybir
from concourse.bass_utils import run_bass_kernel_spmd

F32 = mybir.dt.float32
BF16 = mybir.dt.bfloat16
AF = mybir.ActivationFunctionType
ALU = mybir.AluOpType
AXX = mybir.AxisListType.X

T, D, H, DK = 2048, 1024, 8, 64
FH = H * DK
CH = 128
NCHUNK = T // CH
NBLK = NCHUNK // 4
KCH = D // 128
E, R = 8, 4
NEG = -1.0e30
TRUNC = 99


class Ctr:
    def __init__(self, nc, name):
        self.h = nc.alloc_semaphore(name)
        self.n = 0

    def inc(self, instr, k=1):
        instr.then_inc(self.h, k)
        self.n += k
        return self.n


def build_core():
    nc = bass.Bass("TRN2", target_bir_lowering=False)

    x_d = nc.dram_tensor("x", [T, D], F32, kind="ExternalInput").ap()
    wq_d = nc.dram_tensor("wq", [KCH, 128, FH], F32, kind="ExternalInput").ap()
    wk_d = nc.dram_tensor("wk", [KCH, 128, FH], F32, kind="ExternalInput").ap()
    wv_d = nc.dram_tensor("wv", [KCH, 128, FH], F32, kind="ExternalInput").ap()
    wg_d = nc.dram_tensor("wg", [KCH, 128, FH], F32, kind="ExternalInput").ap()
    wo_d = nc.dram_tensor("wo", [4, 128, D], F32, kind="ExternalInput").ap()
    wlin_d = nc.dram_tensor("wlin", [KCH, 128, D], F32, kind="ExternalInput").ap()
    gatew_d = nc.dram_tensor("gatew", [KCH, 128, E], F32, kind="ExternalInput").ap()
    left_d = nc.dram_tensor("left", [KCH, 128, E * R], F32, kind="ExternalInput").ap()
    right_d = nc.dram_tensor("right", [E * R, D], F32, kind="ExternalInput").ap()
    ident_d = nc.dram_tensor("ident", [128, 128], BF16, kind="ExternalInput").ap()
    cmat_d = nc.dram_tensor("cmat", [128, 448], F32, kind="ExternalInput").ap()
    crow_d = nc.dram_tensor("crow", [1, D + FH + H], F32, kind="ExternalInput").ap()
    out_d = nc.dram_tensor("out", [T, D], F32, kind="ExternalOutput").ap()

    A = nc.alloc_sbuf_tensor
    x_sb = A("x_sb", [128, 10 * D], F32)
    xnfm = A("xnfm", [128, KCH * 1024], BF16)
    qfm = A("qfm", [128, 4 * 1024], BF16)
    kfm = A("kfm", [128, 4 * 1024], BF16)
    vtm = A("vtm", [128, 8 * 520], BF16)
    gsig = A("gsig", [128, 2 * FH], BF16)
    ygfm = A("ygfm", [128, 2 * FH], BF16)
    wq_sb = A("wq_sb", [128, KCH * FH], BF16)
    wk_sb = A("wk_sb", [128, KCH * FH], BF16)
    wv_sb = A("wv_sb", [128, KCH * FH], BF16)
    wg_sb = A("wg_sb", [128, KCH * FH], BF16)
    wo_sb = A("wo_sb", [128, 4 * D], BF16)
    wlin_sb = A("wlin_sb", [128, KCH * D], BF16)
    gatew_sb = A("gatew_sb", [128, KCH * E], BF16)
    left_sb = A("left_sb", [128, KCH * E * R], BF16)
    right_sb = A("right_sb", [E * R, D], BF16)
    crow_s = A("crow_s", [1, D + FH + H], F32)
    blin_hl = A("blin_hl", [2, D], BF16)
    blin_lo_t = A("blin_lo_t", [1, D], BF16)
    wgb_lo_t = A("wgb_lo_t", [1, FH], BF16)
    wgb_hl = A("wgb_hl", [2, FH], BF16)
    nlg_u = A("nlg_u", [1, H], F32)
    nlg_row = A("nlg_row", [1, H], F32)
    nlg_hi = A("nlg_hi", [1, H], BF16)
    nlg_lo = A("nlg_lo", [1, H], BF16)
    ident = A("ident_s", [128, 128], BF16)
    cmat_s = A("cmat_s", [128, 448], F32)
    masks = A("masks", [128, H * 128], BF16)
    dpqf = A("dpqf", [128, 4 * 128], BF16)
    dpkf = A("dpkf", [128, 4 * 128], BF16)
    nlgrep = A("nlgrep", [128, H], F32)
    nlgpair = A("nlgpair", [128, 4], F32)
    gcpair = A("gcpair", [128, 4], F32)
    S_sb = A("S_sb", [128, 4 * 64], F32)
    S_blk = A("S_blk", [128, 4 * 128], BF16)
    z_blk = A("z_blk", [128, 8], BF16)
    z_sb = A("z_sb", [128, 4], F32)
    ones_rowb = A("ones_rowb", [1, 128], BF16)
    ones2 = A("ones2", [2, 128], BF16)
    ones_colb = A("ones_colb", [128, 1], BF16)
    eps_row8 = A("eps_row8", [1, H], BF16)
    eps5_col = A("eps5_col", [128, 1], F32)
    one_col = A("one_col", [128, 1], F32)
    xn_tm = A("xn_tm", [128, 4 * D], BF16)
    phiw = A("phiw", [128, 2 * FH], F32)
    phir = A("phir", [128, 2 * FH], BF16)
    A_sb = A("A_sb", [128, 2 * H * 128], BF16)
    qt_sb = A("qt_sb", [128, 2 * 512], BF16)
    ktm_sb = A("ktm_sb", [128, 2 * 512], BF16)
    ygtm = A("ygtm", [128, 2 * FH], BF16)
    den_r = A("den_r", [128, 2 * H], F32)
    wt_tm = A("wt_tm", [128, 4 * E], BF16)
    wt4 = A("wt4", [128, 4 * 4 * E], BF16)
    wtT = A("wtT", [E * R, T], BF16)
    th_sb = A("th_sb", [E * R, 2 * 512], BF16)
    e8 = A("e8", [128, 4 * E], F32)
    dmpA = A("dmpA", [128, 4 * E], F32)
    dmpB = A("dmpB", [128, 4 * E], F32)
    ind1 = A("ind1", [128, 4 * E], F32)
    i1inv = A("i1inv", [128, 4 * E], F32)
    tmp2 = A("tmp2", [128, 4 * E], F32)
    indf = A("indf", [128, 4 * E], F32)
    wtu = A("wtu", [128, 4 * E], F32)
    tcol = A("tcol", [128, 8], F32)
    varn = A("varn", [128, 8], F32)
    stdc = A("stdc", [128, 8], F32)
    rstdc = A("rstdc", [128, 8], F32)
    nmrc = A("nmrc", [128, 8], F32)
    rs8 = A("rs8", [128, 4], F32)
    cols = A("cols", [128, 48], F32)
    scr1b = A("scr1b", [128, D], BF16)
    out_sb = A("out_sb", [128, 2 * D], F32)

    pT = nc.alloc_psum_tensor("pT", [128, 1024], BF16)
    pQK = [nc.alloc_psum_tensor("pQK0", [128, 512], F32),
           nc.alloc_psum_tensor("pQK1", [128, 512], F32)]
    pV = nc.alloc_psum_tensor("pV", [128, 512], F32)
    pG = nc.alloc_psum_tensor("pG", [128, 512], F32)
    pA0 = nc.alloc_psum_tensor("pA0", [128, 512], F32)
    pT2 = nc.alloc_psum_tensor("pT2", [128, 1024], BF16)
    pO = nc.alloc_psum_tensor("pO", [128, 512], F32)
    pVQ = None

    s_pe = Ctr(nc, "s_pe")
    s_vec = Ctr(nc, "s_vec")
    s_sca = Ctr(nc, "s_sca")
    s_gp = Ctr(nc, "s_gp")
    s_xc = [nc.alloc_semaphore(f"s_x{i}") for i in range(NCHUNK)]
    s_wg = {g: nc.alloc_semaphore(f"s_w_{g}") for g in ["qk", "k2", "v", "g", "lg", "ow", "wo2"]}
    s_c = nc.alloc_semaphore("s_c")
    s_cr = nc.alloc_semaphore("s_cr")
    s_cm = nc.alloc_semaphore("s_cm")
    s_bl = nc.alloc_semaphore("s_bl")
    s_oc = [nc.alloc_semaphore(f"s_o{i}") for i in range(NCHUNK)]

    TE, VE, SE, SY, GP = nc.tensor, nc.vector, nc.scalar, nc.sync, nc.gpsimd
    pVQ = [pV, pO]

    def dma_xc(c):
        cx = c % 10
        SY.dma_start(x_sb[:, cx * D : cx * D + D],
                     x_d[128 * c : 128 * c + 128, :]).then_inc(s_xc[c], 16)

    SY.dma_start(ident[:, :], ident_d[:, :]).then_inc(s_c, 16)
    SY.dma_start(crow_s[:, :], crow_d[:, :]).then_inc(s_cr, 16)
    for _c in range(4):
        dma_xc(_c)

    for t_, v_ in [(ones_rowb, 1.0), (ones2, 1.0), (ones_colb, 1.0), (eps_row8, 1e-6),
                   (eps5_col, 1e-5), (one_col, 1.0), (S_sb, 0.0), (S_blk, 0.0),
                   (z_sb, 0.0), (z_blk, 0.0), (vtm, 1.0)]:
        ins = VE.memset(t_[:, :], v_)
    v_ms = s_vec.inc(ins)
    VE.wait_ge(s_cr, 16)
    VE.tensor_copy(blin_hl[0:1, :], crow_s[0:1, 0:D])
    v_bh = s_vec.inc(VE.tensor_copy(wgb_hl[0:1, :], crow_s[0:1, D : D + FH]))
    GP.wait_ge(s_vec.h, v_bh)
    GP.tensor_tensor(blin_lo_t[:, :], crow_s[0:1, 0:D], blin_hl[0:1, :], ALU.subtract)
    g_bl = s_gp.inc(GP.tensor_tensor(wgb_lo_t[:, :], crow_s[0:1, D : D + FH], wgb_hl[0:1, :],
                                     ALU.subtract))
    wmap = [(gatew_sb, gatew_d, "lg"), (left_sb, left_d, "lg"),
            (wq_sb, wq_d, "qk"), (wk_sb, wk_d, "k2"), (wv_sb, wv_d, "v"),
            (wg_sb, wg_d, "g"), (wlin_sb, wlin_d, "ow"), (wo_sb, wo_d, "wo2")]
    w_tot = {g: 0 for g in s_wg}
    for dst, srcd, grp in wmap:
        kk = srcd.shape[0]
        GP.dma_start(dst[:, :].rearrange("p (k c) -> p k c", k=kk),
                     srcd.rearrange("k p c -> p k c")).then_inc(s_wg[grp], 16)
        w_tot[grp] += 16
    GP.dma_start(right_sb[:, :], right_d[:, :]).then_inc(s_wg["ow"], 16)
    w_tot["ow"] += 16
    SY.wait_ge(s_gp.h, g_bl)
    SY.dma_start(blin_hl[1:2, :], blin_lo_t[:, :]).then_inc(s_bl, 16)
    SY.dma_start(wgb_hl[1:2, :], wgb_lo_t[:, :]).then_inc(s_bl, 16)
    SY.dma_start(cmat_s[:, :], cmat_d[:, :]).then_inc(s_cm, 16)
    for _c in range(4, 10):
        dma_xc(_c)

    ini = {"sca": 0, "vec": 0, "gp": 0}

    def init_tail():
        SE.wait_ge(s_cr, 16)
        s_e1 = s_sca.inc(SE.activation(nlg_u[:, :], crow_s[0:1, D + FH : D + FH + H], AF.Exp, scale=-1.0))
        VE.wait_ge(s_sca.h, s_e1)
        v_mn = s_vec.inc(VE.tensor_scalar(nlg_u[:, :], nlg_u[:, :], 1e15, None, ALU.min))
        SE.wait_ge(s_vec.h, max(v_mn, v_ms))
        s_ln = s_sca.inc(SE.activation(nlg_row[:, :], nlg_u[:, :], AF.Ln,
                                       bias=one_col[0:1, :]))
        VE.wait_ge(s_sca.h, s_ln)
        v_nh = s_vec.inc(VE.tensor_copy(nlg_hi[:, :], nlg_row[:, :]))
        GP.wait_ge(s_vec.h, v_nh)
        g_nl = s_gp.inc(GP.tensor_tensor(nlg_lo[:, :], nlg_row[:, :], nlg_hi[:, :],
                                         ALU.subtract))
        TE.wait_ge(s_vec.h, v_nh)
        TE.wait_ge(s_gp.h, g_nl)
        TE.matmul(pG[:, 0:H], ones_rowb[:, :], nlg_hi[:, :], start=True, stop=False)
        t_lg = s_pe.inc(TE.matmul(pG[:, 0:H], ones_rowb[:, :], nlg_lo[:, :],
                                  start=False, stop=True))
        SE.wait_ge(s_pe.h, t_lg)
        SE.copy(nlgrep[:, :], pG[:, 0:H])
        SE.drain()
        for p in range(4):
            SE.copy(nlgpair[0:64, p : p + 1], nlgrep[0:64, 2 * p : 2 * p + 1])
            ins = SE.copy(nlgpair[64:128, p : p + 1], nlgrep[64:128, 2 * p + 1 : 2 * p + 2])
        SE.drain()
        s_gc = s_sca.inc(SE.activation(gcpair[:, :], nlgpair[:, :], AF.Exp, scale=-128.0))
        VE.wait_ge(s_sca.h, s_gc)
        VE.wait_ge(s_cm, 16)
        expt_c = cmat_s[:, 0:128]
        maskadd_c = cmat_s[:, 128:256]
        eqm_c = cmat_s[:, 256:384]
        e127_c = cmat_s[:, 384:448]
        for h in range(H):
            VE.scalar_tensor_tensor(
                out_sb[:, 128 * h : 128 * h + 128], expt_c, nlgrep[:, h : h + 1],
                maskadd_c, ALU.mult, ALU.add)
        for p in range(4):
            VE.tensor_scalar(out_sb[:, 1024 + 128 * p : 1024 + 128 * p + 128],
                             eqm_c, nlgpair[:, p : p + 1], None, ALU.mult)
        for h in range(H):
            ins = VE.tensor_scalar(out_sb[:, 1536 + 64 * h : 1536 + 64 * h + 64],
                                   e127_c, nlgrep[:, h : h + 1], None, ALU.mult)
        v_ini = s_vec.inc(ins)
        SE.wait_ge(s_vec.h, v_ini)
        SE.activation(masks[:, :], out_sb[:, 0:1024], AF.Exp)
        SE.activation(dpqf[:, :], out_sb[:, 1024:1536], AF.Exp)
        ss = s_sca.inc(SE.activation(dpkf[:, :], out_sb[:, 1536:2048], AF.Exp))
        VE.wait_ge(s_sca.h, ss)
        ini["sca"] = s_sca.n
        ini["vec"] = s_vec.n
        ini["gp"] = s_gp.n
        rec["gfree_s"] = s_sca.n


    rec = {
        "ow": [0] * NCHUNK, "res": [0] * NCHUNK, "supd": [0] * NCHUNK,
        "gat": [0] * NCHUNK, "ydeN": [0] * NCHUNK, "maskdone": [0] * NCHUNK,
        "recip": [0] * NCHUNK, "ytpe": [0] * NCHUNK,
        "tp_free_v": 0, "tp_free_s": 0,
        "qk_free": [[0, "v"], [0, "v"], [0, "v"], [0, "v"]],
        "vfree": [0, 0], "gfree_s": s_sca.n, "gfree_v": 0,
        "th": [0, 0], "ge_evac": [0] * NCHUNK, "phi_done_v": 0, "s_w4": 0,
        "kq_ready": [0, 0],
        "stats": [0] * NCHUNK, "nmr": [0] * NBLK, "e8b": 0,
    }
    qk_par = [0]

    def qk_bank():
        i = qk_par[0] % 2
        qk_par[0] += 1
        return i

    rec["t_tp"] = [0] * NCHUNK
    rec["s_xn"] = [0] * NCHUNK
    rec["xn_ev"] = [("v", 0)] * NCHUNK
    rec["tp2_free_v"] = 0
    rec["vh0"] = [0] * NCHUNK
    rec["supd_v"] = [0] * NCHUNK
    rec["tp2_free_s"] = 0
    rec["vktm"] = [0] * NCHUNK
    rec["scanv"] = [0] * NCHUNK
    rec["vv"] = [0] * NCHUNK
    rec["qt"] = [0] * NCHUNK

    def A_stats(c):
        cx, cs = c % 10, c % 8
        xc = x_sb[:, cx * D : cx * D + D]
        SE.wait_ge(s_xc[c], 16)
        if c > 0:
            SE.wait_ge(s_sca.h, rec["stats"][c - 1])
        s_cp = s_sca.inc(SE.activation(scr1b[:, :], xc, AF.Copy,
                                       accum_out=cols[:, cs : cs + 1]))
        SE.wait_ge(s_sca.h, s_cp)
        rec["stats"][c] = s_sca.inc(SE.activation(
            scr1b[:, :], xc, AF.Square, accum_out=cols[:, 8 + cs : 8 + cs + 1]))

    def A_pre2(bb):
        base = (4 * bb) % 8
        GP.wait_ge(s_sca.h, rec["stats"][4 * bb + 3])
        for i in range(4):
            cs = (4 * bb + i) % 8
            ins = GP.tensor_scalar(tcol[:, cs : cs + 1], cols[:, cs : cs + 1],
                                   cols[:, cs : cs + 1], -1.0 / (D * D),
                                   ALU.mult, ALU.mult)
        g_t = s_gp.inc(ins)
        VE.wait_ge(s_gp.h, g_t)
        v_var = s_vec.inc(VE.scalar_tensor_tensor(
            varn[:, base : base + 4], cols[:, 8 + base : 8 + base + 4],
            1.0 / D, tcol[:, base : base + 4], ALU.mult, ALU.add))
        SE.wait_ge(s_vec.h, v_var)
        s_lnv = s_sca.inc(SE.activation(stdc[:, base : base + 4],
                                        varn[:, base : base + 4],
                                        AF.Ln, bias=eps5_col[:, :]))
        SE.wait_ge(s_sca.h, s_lnv)
        s_rs = s_sca.inc(SE.activation(rstdc[:, base : base + 4],
                                       stdc[:, base : base + 4],
                                       AF.Exp, scale=-0.5))
        VE.wait_ge(s_sca.h, s_rs)
        rec["nmr"][bb] = s_vec.inc(VE.scalar_tensor_tensor(
            nmrc[:, base : base + 4], cols[:, base : base + 4], -1.0 / D,
            rstdc[:, base : base + 4], ALU.mult, ALU.mult))

    def A_xn(c):
        b = c // 4
        cx, cs, c4 = c % 10, c % 8, c % 4
        xc = x_sb[:, cx * D : cx * D + D]
        if c >= 4:
            SE.wait_ge(s_pe.h, rec["t_tp"][c - 4])
        SE.wait_ge(s_vec.h, rec["nmr"][b])
        rec["s_xn"][c] = s_sca.inc(SE.activation(xn_tm[:, c4 * D : c4 * D + D], xc,
                                       AF.Identity, scale=rstdc[:, cs : cs + 1],
                                       bias=nmrc[:, cs : cs + 1]))

    def A_ln_tp(c, bank2=False):
        cs, cp, c4 = c % 8, c % 2, c % 4
        pTb = pT2 if bank2 else pT
        TE.wait_ge(s_sca.h, rec["s_xn"][c])
        if bank2:
            TE.wait_ge(s_vec.h, rec["tp2_free_v"])
            TE.wait_ge(s_sca.h, rec["tp2_free_s"])
        else:
            TE.wait_ge(s_vec.h, rec["tp_free_v"])
            TE.wait_ge(s_sca.h, rec["tp_free_s"])
        for j in range(KCH):
            ins = TE.transpose(pTb[:, 128 * j : 128 * j + 128],
                               xn_tm[:, c4 * D + 128 * j : c4 * D + 128 * j + 128],
                               ident[:, :])
        t_tp = s_pe.inc(ins)
        rec["t_tp"][c] = t_tp
        xdst = xnfm[:, :].rearrange("p (k w) -> p k w", k=KCH)[:, :, cs * 128 : cs * 128 + 128]
        xsrc = pTb[:, :].rearrange("p (k w) -> p k w", k=KCH)
        if cp == 0:
            if c >= 8:
                VE.wait_ge(s_pe.h, rec["ow"][c - 8])
            VE.wait_ge(s_pe.h, t_tp)
            ev = s_vec.inc(VE.tensor_copy(xdst, xsrc))
            if bank2:
                rec["tp2_free_v"] = ev
            else:
                rec["tp_free_v"] = ev
            rec["xn_ev"][c] = ("v", ev)
        else:
            if c >= 8:
                SE.wait_ge(s_pe.h, rec["ow"][c - 8])
            SE.wait_ge(s_pe.h, t_tp)
            ev = s_sca.inc(SE.copy(xdst, xsrc))
            if bank2:
                rec["tp2_free_s"] = ev
            else:
                rec["tp_free_s"] = ev
            rec["xn_ev"][c] = ("s", ev)

    def wait_xn(b):
        mv, ms = ini["vec"], ini["sca"]
        for cc in range(4 * b, 4 * b + 4):
            e, n = rec["xn_ev"][cc]
            if e == "v":
                mv = max(mv, n)
            else:
                ms = max(ms, n)
        TE.wait_ge(s_vec.h, mv)
        TE.wait_ge(s_sca.h, ms)

    def B_proj(b):
        toff = 512 * (b % 2)
        wait_xn(b)
        TE.wait_ge(s_wg["qk"], w_tot["qk"])
        pbanks = [pQK[0], pQK[1], pA0, pG]
        if b > 0:
            rec["qk_free"][2] = [rec["gat"][4 * b - 1], "v"]
        rec["qk_free"][3] = [max(rec["ge_evac"][4 * b], ini["sca"]), "s"]
        outs = [(wsb, dstfm, m) for (wsb, dstfm) in [(wq_sb, qfm), (wk_sb, kfm)]
                for m in range(4)]
        pend = None
        v_min = [0, 0]
        v_stt = [0, 0]
        s_exp = [0, 0]
        for i, (wsb, dstfm, m) in enumerate(outs):
            if i == 4:
                TE.wait_ge(s_wg["k2"], w_tot["k2"])
            bank = [2, 3, 0, 1][i % 4]
            cnt, eng = rec["qk_free"][bank]
            TE.wait_ge(s_vec.h if eng == "v" else s_sca.h, cnt)
            for k in range(KCH):
                ins = TE.matmul(pbanks[bank][:, :],
                                wsb[:, FH * k + 128 * m : FH * k + 128 * m + 128],
                                xnfm[:, 1024 * k + toff : 1024 * k + toff + 512],
                                start=(k == 0), stop=(k == KCH - 1))
            t_g = s_pe.inc(ins)
            tp = i % 2
            VE.wait_ge(s_pe.h, t_g)
            if s_exp[tp]:
                VE.wait_ge(s_sca.h, s_exp[tp])
            v_min[tp] = s_vec.inc(VE.tensor_scalar(
                phiw[:, tp * FH : tp * FH + FH], pbanks[bank][:, :], 0.0, None,
                ALU.min))
            SE.wait_ge(s_vec.h, v_min[tp])
            if v_stt[tp]:
                SE.wait_ge(s_vec.h, v_stt[tp])
            s_exp[tp] = s_sca.inc(SE.activation(phir[:, tp * FH : tp * FH + FH],
                                                phiw[:, tp * FH : tp * FH + FH],
                                                AF.Exp))
            if pend is not None:
                pb, pd, pm, ptp, pse = pend
                VE.wait_ge(s_sca.h, pse)
                v_stt[ptp] = s_vec.inc(VE.scalar_tensor_tensor(
                    pd[:, 1024 * pm + toff : 1024 * pm + toff + 512],
                    pbanks[pb][:, :], 0.0,
                    phir[:, ptp * FH : ptp * FH + FH], ALU.max, ALU.add))
                rec["qk_free"][pb] = [v_stt[ptp], "v"]
            pend = (bank, dstfm, m, tp, s_exp[tp])
        pb, pd, pm, ptp, pse = pend
        VE.wait_ge(s_sca.h, pse)
        v_last = s_vec.inc(VE.scalar_tensor_tensor(
            pd[:, 1024 * pm + toff : 1024 * pm + toff + 512],
            pbanks[pb][:, :], 0.0,
            phir[:, ptp * FH : ptp * FH + FH], ALU.max, ALU.add))
        rec["qk_free"][pb] = [v_last, "v"]
        rec["phi_done_v"] = v_last
        rec["kq_ready"][b % 2] = v_last

    def C_log_mm(b):
        blo = 4 * b
        cnt3, eng3 = rec["qk_free"][3]
        TE.wait_ge(s_vec.h if eng3 == "v" else s_sca.h, cnt3)
        wait_xn(b)
        TE.wait_ge(s_wg["lg"], w_tot["lg"])
        for ci in range(4):
            cs = (blo + ci) % 8
            for k in range(KCH):
                ins = TE.matmul(pG[:, 8 * ci : 8 * ci + 8],
                                xnfm[:, 1024 * k + cs * 128 : 1024 * k + cs * 128 + 128],
                                gatew_sb[:, E * k : E * k + E],
                                start=(k == 0), stop=(k == KCH - 1))
        t_l = s_pe.inc(ins)

        scol = lambda c, i: cols[:, 16 + (c % 4) * 4 + i : 16 + (c % 4) * 4 + i + 1]
        VE.wait_ge(s_pe.h, t_l)
        for ci in range(4):
            ins = VE.reduce_max(scol(blo + ci, 0), pG[:, 8 * ci : 8 * ci + 8],
                                AXX, negate=True)
        v_nm = s_vec.inc(ins)
        SE.wait_ge(s_vec.h, v_nm)
        for ci in range(4):
            c4 = (blo + ci) % 4
            ins = SE.activation(e8[:, c4 * E : c4 * E + E],
                                pG[:, 8 * ci : 8 * ci + 8],
                                AF.Exp, bias=scol(blo + ci, 0))
        s_e8 = s_sca.inc(ins)
        rec["e8b"] = s_e8
        rec["gfree_s"] = s_e8

    def C_top2(b):
        blo = 4 * b
        scol = lambda c, i: cols[:, 16 + (c % 4) * 4 + i : 16 + (c % 4) * 4 + i + 1]
        VE.wait_ge(s_sca.h, max(rec["e8b"], rec["s_w4"]))
        esl = lambda c4: e8[:, c4 * E : c4 * E + E]
        for ci in range(4):
            c4 = (blo + ci) % 4
            VE.tensor_scalar(dmpA[:, c4 * E : c4 * E + E], esl(c4), 1.0, NEG,
                             ALU.mult, ALU.max, accum_out=scol(blo + ci, 1))
            ins = VE.tensor_scalar(dmpB[:, c4 * E : c4 * E + E], esl(c4),
                                   1.0, 0.0, ALU.mult, ALU.add,
                                   accum_out=scol(blo + ci, 2))
        v_m1 = s_vec.inc(ins)
        VE.drain()
        for ci in range(4):
            ins = VE.reciprocal(rs8[:, (blo + ci) % 4 : (blo + ci) % 4 + 1],
                                scol(blo + ci, 2))
        v_rs = s_vec.inc(ins)
        GP.wait_ge(s_vec.h, v_m1)
        for ci in range(4):
            c4 = (blo + ci) % 4
            ins = GP.tensor_scalar(ind1[:, c4 * E : c4 * E + E], esl(c4),
                                   scol(blo + ci, 1), None, ALU.is_ge)
        g_i1 = s_gp.inc(ins)
        VE.wait_ge(s_gp.h, g_i1)
        for ci in range(4):
            c4 = (blo + ci) % 4
            ins = VE.scalar_tensor_tensor(tmp2[:, c4 * E : c4 * E + E],
                                          ind1[:, c4 * E : c4 * E + E],
                                          -1e30, esl(c4), ALU.mult, ALU.add)
        v_t2 = s_vec.inc(ins)
        VE.wait_ge(s_vec.h, v_t2)
        for ci in range(4):
            c4 = (blo + ci) % 4
            ins = VE.reduce_max(scol(blo + ci, 3), tmp2[:, c4 * E : c4 * E + E], AXX)
        v_m2 = s_vec.inc(ins)
        GP.wait_ge(s_vec.h, v_m2)
        for ci in range(4):
            c4 = (blo + ci) % 4
            ins = GP.tensor_scalar(indf[:, c4 * E : c4 * E + E], esl(c4),
                                   scol(blo + ci, 3), None, ALU.is_ge)
        g_ind = s_gp.inc(ins)
        VE.wait_ge(s_gp.h, g_ind)
        VE.wait_ge(s_vec.h, v_rs)
        for ci in range(4):
            c4 = (blo + ci) % 4
            ins = VE.scalar_tensor_tensor(wt_tm[:, c4 * E : c4 * E + E],
                                          indf[:, c4 * E : c4 * E + E],
                                          rs8[:, c4 : c4 + 1], esl(c4),
                                          ALU.mult, ALU.mult)
        v_wt = s_vec.inc(ins)
        SE.wait_ge(s_vec.h, v_wt)
        for ci in range(4):
            c4 = (blo + ci) % 4
            for r in range(R):
                ins = SE.copy(wt4[:, c4 * 32 + 8 * r : c4 * 32 + 8 * r + 8],
                              wt_tm[:, c4 * E : c4 * E + E])
        rec["s_w4"] = s_sca.inc(ins)

    def C_rest(b):
        blo = 4 * b
        toff = 512 * (b % 2)
        bank_t = qk_bank()
        cnt, eng = rec["qk_free"][bank_t]
        TE.wait_ge(s_vec.h if eng == "v" else s_sca.h, cnt)
        for k in range(KCH):
            ins = TE.matmul(pQK[bank_t][0:32, :],
                            left_sb[:, 32 * k : 32 * k + 32],
                            xnfm[:, 1024 * k + toff : 1024 * k + toff + 512],
                            start=(k == 0), stop=(k == KCH - 1))
        t_t = s_pe.inc(ins)
        TE.wait_ge(s_sca.h, max(rec["s_w4"], rec["tp_free_s"]))
        TE.wait_ge(s_vec.h, rec["tp_free_v"])
        for ci in range(4):
            c4 = (blo + ci) % 4
            ins = TE.transpose(pT[0:32, 128 * ci : 128 * ci + 128],
                               wt4[:, c4 * 32 : c4 * 32 + 32], ident[:, :])
        t_wt = s_pe.inc(ins)
        SE.wait_ge(s_pe.h, t_wt)
        for ci in range(4):
            ins = SE.copy(wtT[:, 128 * (blo + ci) : 128 * (blo + ci) + 128],
                          pT[0:32, 128 * ci : 128 * ci + 128])
        rec["tp_free_s"] = s_sca.inc(ins)

        bp = b % 2
        VE.wait_ge(s_pe.h, t_t)
        VE.wait_ge(s_sca.h, rec["tp_free_s"])
        if b >= 2:
            VE.wait_ge(s_pe.h, rec["ow"][4 * (b - 2) + 3])
        v_th = s_vec.inc(VE.tensor_tensor(th_sb[:, bp * 512 : bp * 512 + 512],
                                          pQK[bank_t][0:32, :],
                                          wtT[:, 512 * b : 512 * b + 512], ALU.mult))
        rec["th"][bp] = v_th
        rec["qk_free"][bank_t] = [v_th, "v"]

    def E_vg(c):
        cs = c % 8
        pVb = pVQ[c % 2]
        eng, cnt = rec["xn_ev"][c]
        TE.wait_ge(s_vec.h if eng == "v" else s_sca.h, cnt)
        TE.wait_ge(s_wg["v"], w_tot["v"])
        if c >= 2:
            TE.wait_ge(s_vec.h, rec["vfree"][c % 2])
        for k in range(KCH):
            ins = TE.matmul(pVb[:, :],
                            xnfm[:, 1024 * k + cs * 128 : 1024 * k + cs * 128 + 128],
                            wv_sb[:, FH * k : FH * k + FH],
                            start=(k == 0), stop=(k == KCH - 1))
        rec["vv"][c] = s_pe.inc(ins)
        TE.wait_ge(s_wg["g"], w_tot["g"])
        TE.wait_ge(s_bl, 32)
        if c == 0:
            TE.wait_ge(s_gp.h, ini["gp"])
        TE.wait_ge(s_sca.h, rec["gfree_s"])
        TE.wait_ge(s_vec.h, rec["gfree_v"])
        for k in range(KCH):
            TE.matmul(pG[:, :],
                      xnfm[:, 1024 * k + cs * 128 : 1024 * k + cs * 128 + 128],
                      wg_sb[:, FH * k : FH * k + FH],
                      start=(k == 0), stop=False)
        t_gp = s_pe.inc(TE.matmul(pG[:, :], ones2[:, :], wgb_hl[:, :],
                                  start=False, stop=True))
        SE.wait_ge(s_pe.h, t_gp)
        s_gs = s_sca.inc(SE.activation(gsig[:, (c % 2) * FH : (c % 2) * FH + FH],
                                       pG[:, :], AF.Sigmoid))
        rec["ge_evac"][c] = s_gs

    def E_sc(c):
        b = c // 4
        ci = c % 4
        cs, cp = c % 8, c % 2
        toff = 512 * (b % 2)
        pVb = pVQ[c % 2]
        TE.wait_ge(s_vec.h, rec["kq_ready"][b % 2])
        if c > 0:
            TE.wait_ge(s_vec.h, rec["gat"][c - 1])
        for h in [0, 2, 4, 6]:
            ins = TE.matmul(
                pA0[:, 128 * (h // 2) : 128 * (h // 2) + 128],
                kfm[0:64, 1024 * (h // 2) + toff + 128 * ci :
                    1024 * (h // 2) + toff + 128 * ci + 128],
                qfm[0:64, 1024 * (h // 2) + toff + 128 * ci :
                    1024 * (h // 2) + toff + 128 * ci + 128],
                start=True, stop=True)
        t_sc0 = s_pe.inc(ins)
        TE.wait_ge(s_sca.h, max(rec["ge_evac"][c], rec["e8b"]))
        for h in [1, 3, 5, 7]:
            ins = TE.matmul(
                pG[:, 128 * (h // 2) : 128 * (h // 2) + 128],
                kfm[64:128, 1024 * (h // 2) + toff + 128 * ci :
                    1024 * (h // 2) + toff + 128 * ci + 128],
                qfm[64:128, 1024 * (h // 2) + toff + 128 * ci :
                    1024 * (h // 2) + toff + 128 * ci + 128],
                start=True, stop=True)
        t_sc1 = s_pe.inc(ins)
        if c >= 2:
            VE.wait_ge(s_pe.h, rec["ydeN"][c - 2])
        VE.wait_ge(s_pe.h, t_sc1)
        for h in range(H):
            psrc = pA0 if h % 2 == 0 else pG
            ins = VE.tensor_tensor(
                A_sb[:, cp * 1024 + 128 * h : cp * 1024 + 128 * h + 128],
                psrc[:, 128 * (h // 2) : 128 * (h // 2) + 128],
                masks[:, 128 * h : 128 * h + 128], ALU.mult)
        rec["maskdone"][c] = s_vec.inc(ins)
        qsrc = qfm[:, :].rearrange("q (p w) -> q p w", p=4)[
            :, :, toff + 128 * ci : toff + 128 * ci + 128]
        rec["qt"][c] = s_vec.inc(VE.tensor_tensor(
            qt_sb[:, cp * 512 : cp * 512 + 512].rearrange("q (p w) -> q p w", p=4),
            qsrc, dpqf[:, :].rearrange("q (p w) -> q p w", p=4), ALU.mult))

    def E_vev(c):
        cs = c % 8
        pVb = pVQ[c % 2]
        VE.wait_ge(s_pe.h, rec["vv"][c])
        rec["scanv"][c] = s_vec.inc(VE.tensor_copy(
            vtm[:, cs * 520 : cs * 520 + 520].rearrange("p (h w) -> p h w", h=8)[:, :, 0:64],
            pVb[:, :].rearrange("p (h w) -> p h w", h=8)))

    def ktp_stage(c):
        b = c // 4
        ci = c % 4
        cp = c % 2
        toff = 512 * (b % 2)
        TE.wait_ge(s_vec.h, max(rec["tp2_free_v"], rec["kq_ready"][b % 2]))
        TE.wait_ge(s_sca.h, rec["tp2_free_s"])
        if c >= 2:
            TE.wait_ge(s_pe.h, rec["ydeN"][c - 2])
        for p in range(4):
            ins = TE.transpose(
                pT2[:, 128 * p : 128 * p + 128],
                kfm[:, 1024 * p + toff + 128 * ci : 1024 * p + toff + 128 * ci + 128],
                ident[:, :])
        t_ktp = s_pe.inc(ins)
        VE.wait_ge(s_pe.h, t_ktp)
        if c >= 2:
            VE.wait_ge(s_pe.h, rec["ydeN"][c - 2])
        v_ktm = s_vec.inc(VE.tensor_tensor(ktm_sb[:, cp * 512 : cp * 512 + 512],
                                           pT2[:, 0:512], dpkf[:, :], ALU.mult))
        rec["tp2_free_v"] = v_ktm
        rec["vktm"][c] = v_ktm

    def F_scan(c):
        cs, cp = c % 8, c % 2
        vb = cs * 520
        pVb = pVQ[c % 2]
        TE.wait_ge(s_vec.h, max(rec["vktm"][c], rec["scanv"][c]))
        for h in range(H):
            tp_pos = (0, 64 * (h % 2))
            ins = TE.matmul(pVb[64 * (h % 2) : 64 * (h % 2) + 64,
                               65 * (h // 2) : 65 * (h // 2) + 65],
                            ktm_sb[:, cp * 512 + 128 * (h // 2) + 64 * (h % 2) :
                                   cp * 512 + 128 * (h // 2) + 64 * (h % 2) + 64],
                            vtm[:, vb + 65 * h : vb + 65 * h + 65],
                            start=True, stop=True, tile_position=tp_pos)
        t_st = s_pe.inc(ins)
        rec["ydeN"][c] = t_st

        TE.wait_ge(s_vec.h, max(rec["maskdone"][c], rec["qt"][c]))
        if c > 0:
            TE.wait_ge(s_vec.h, rec["supd_v"][c - 1])
        TE.wait_ge(s_sca.h, max(rec["ge_evac"][c], rec["e8b"]))
        TE.matmul(pG[:, 0:H], ones_rowb[:, 0:128], eps_row8[:, :],
                  start=True, stop=False)
        for h in range(H):
            TE.matmul(pG[:, h : h + 1],
                      A_sb[:, cp * 1024 + 128 * h : cp * 1024 + 128 * h + 128],
                      ones_colb[:, :], start=False, stop=False)
        for p in range(4):
            ins = TE.matmul(pG[:, 2 * p : 2 * p + 2],
                            qt_sb[:, cp * 512 + 128 * p : cp * 512 + 128 * p + 128],
                            z_blk[:, 2 * p : 2 * p + 2],
                            start=False, stop=(p == 3))
        t_den = s_pe.inc(ins)
        for p in range(4):
            for e in range(2):
                h = 2 * p + e
                TE.matmul(pA0[:, 64 * h : 64 * h + 64],
                          A_sb[:, cp * 1024 + 128 * h : cp * 1024 + 128 * h + 128],
                          vtm[:, vb + 65 * h : vb + 65 * h + 64],
                          start=(e == 0), stop=False)
            ins = TE.matmul(pA0[:, 128 * p : 128 * p + 128],
                      qt_sb[:, cp * 512 + 128 * p : cp * 512 + 128 * p + 128],
                      S_blk[:, 128 * p : 128 * p + 128],
                      start=False, stop=True)
        t_y = s_pe.inc(ins)

        VE.wait_ge(s_pe.h, t_st)
        for p in range(4):
            VE.scalar_tensor_tensor(S_sb[:, 64 * p : 64 * p + 64],
                                    S_sb[:, 64 * p : 64 * p + 64],
                                    gcpair[:, p : p + 1],
                                    pVb[:, 65 * p : 65 * p + 64],
                                    ALU.mult, ALU.add)
            ins = VE.scalar_tensor_tensor(z_sb[:, p : p + 1], z_sb[:, p : p + 1],
                                          gcpair[:, p : p + 1],
                                          pVb[:, 65 * p + 64 : 65 * p + 65],
                                          ALU.mult, ALU.add)
        v_su = s_vec.inc(ins)
        rec["vfree"][c % 2] = v_su
        VE.wait_ge(s_pe.h, t_den)
        rec["recip"][c] = s_vec.inc(VE.reciprocal(den_r[:, cp * H : cp * H + H],
                                                  pG[:, 0:H]))
        rec["gfree_v"] = rec["recip"][c]
        rec["gfree_s"] = 0
        VE.drain()
        VE.wait_ge(s_sca.h, rec["ge_evac"][c])
        VE.wait_ge(s_pe.h, t_y)
        for h in range(H):
            ins = VE.scalar_tensor_tensor(
                ygtm[:, cp * FH + 64 * h : cp * FH + 64 * h + 64],
                gsig[:, cp * FH + 64 * h : cp * FH + 64 * h + 64],
                den_r[:, cp * H + h : cp * H + h + 1],
                pA0[:, 64 * h : 64 * h + 64], ALU.mult, ALU.mult)
        rec["gat"][c] = s_vec.inc(ins)
        VE.tensor_copy(
            S_blk[0:64, :].rearrange("p (q w) -> p q w", q=4)[:, :, 0:64],
            S_sb[0:64, :].rearrange("p (q w) -> p q w", q=4))
        VE.tensor_copy(
            S_blk[64:128, :].rearrange("p (q w) -> p q w", q=4)[:, :, 64:128],
            S_sb[64:128, :].rearrange("p (q w) -> p q w", q=4))
        VE.tensor_copy(z_blk[0:64, :].rearrange("p (q w) -> p q w", q=4)[:, :, 0:1],
                       z_sb[0:64, :].rearrange("p (q w) -> p q w", q=4))
        ins = VE.tensor_copy(
            z_blk[64:128, :].rearrange("p (q w) -> p q w", q=4)[:, :, 1:2],
            z_sb[64:128, :].rearrange("p (q w) -> p q w", q=4))
        rec["supd_v"][c] = s_vec.inc(ins)

    def G_ytp(c):
        cp = c % 2
        TE.wait_ge(s_vec.h, max(rec["gat"][c], rec["tp2_free_v"]))
        if c >= 2:
            TE.wait_ge(s_pe.h, rec["ow"][c - 2])
        TE.wait_ge(s_sca.h, rec["tp2_free_s"])
        for j in range(4):
            ins = TE.transpose(pT2[:, 512 + 128 * j : 512 + 128 * j + 128],
                               ygtm[:, cp * FH + 128 * j : cp * FH + 128 * j + 128],
                               ident[:, :])
        t_ytp = s_pe.inc(ins)
        SE.wait_ge(s_pe.h, t_ytp)
        ins = SE.copy(ygfm[:, cp * FH : cp * FH + FH].rearrange("p (j w) -> p j w", j=4),
                      pT2[:, 512:1024].rearrange("p (j w) -> p j w", j=4))
        rec["ytpe"][c] = s_sca.inc(ins)
        rec["tp2_free_s"] = rec["ytpe"][c]

    def G_out(c, n):
        b = c // 4
        ci = c % 4
        cs, cp = c % 8, c % 2
        bank = qk_bank()
        cnt, eng = rec["qk_free"][bank]
        TE.wait_ge(s_vec.h if eng == "v" else s_sca.h, cnt)
        if n == 0:
            TE.wait_ge(s_vec.h, rec["th"][b % 2])
            TE.wait_ge(s_wg["ow"], w_tot["ow"])
            if c == 0:
                TE.wait_ge(s_gp.h, ini["gp"])
        for k in range(KCH):
            TE.matmul(pQK[bank][:, :],
                      xnfm[:, 1024 * k + cs * 128 : 1024 * k + cs * 128 + 128],
                      wlin_sb[:, D * k + 512 * n : D * k + 512 * n + 512],
                      start=(k == 0), stop=False)
        TE.matmul(pQK[bank][:, :],
                  th_sb[:, (b % 2) * 512 + 128 * ci : (b % 2) * 512 + 128 * ci + 128],
                  right_sb[:, 512 * n : 512 * n + 512], start=False, stop=False)
        TE.matmul(pQK[bank][:, :], ones2[:, :],
                  blin_hl[:, 512 * n : 512 * n + 512], start=False, stop=False)
        if n == 0:
            TE.wait_ge(s_sca.h, rec["ytpe"][c])
            TE.wait_ge(s_wg["wo2"], w_tot["wo2"])
        for j in range(4):
            ins = TE.matmul(pQK[bank][:, :],
                            ygfm[:, cp * FH + 128 * j : cp * FH + 128 * j + 128],
                            wo_sb[:, D * j + 512 * n : D * j + 512 * n + 512],
                            start=False, stop=(j == 3))
        t_half = s_pe.inc(ins)
        VE.wait_ge(s_pe.h, t_half)
        if n == 0 and c >= 2:
            VE.wait_ge(s_oc[c - 2], 16)
        cx = c % 10
        ins = VE.tensor_tensor(out_sb[:, cp * D + 512 * n : cp * D + 512 * n + 512],
                               pQK[bank][:, :],
                               x_sb[:, cx * D + 512 * n : cx * D + 512 * n + 512],
                               ALU.add)
        r_ev = s_vec.inc(ins)
        rec["qk_free"][bank] = [r_ev, "v"]
        if n == 1:
            rec["res"][c] = r_ev
            rec["ow"][c] = t_half
            SY.wait_ge(s_vec.h, r_ev)
            SY.dma_start(out_d[128 * c : 128 * c + 128, :],
                         out_sb[:, cp * D : cp * D + D]).then_inc(s_oc[c], 16)
            if c + 10 < NCHUNK:
                dma_xc(c + 10)
        else:
            rec["vh0"][c] = r_ev

    TE.wait_ge(s_c, 16)
    for c in range(4):
        A_stats(c)
    A_pre2(0)
    init_tail()
    for c in range(4):
        A_xn(c)
        A_ln_tp(c)
    B_proj(0)
    C_log_mm(0)
    E_vg(0)
    E_vev(0)
    ktp_stage(0)
    E_sc(0)
    C_top2(0)
    C_rest(0)
    for c in range(4, 8):
        A_stats(c)
    A_pre2(1)
    for c in range(NCHUNK):
        ci = c % 4
        F_scan(c)
        if ci != 3:
            E_vg(c + 1)
            E_vev(c + 1)
            G_ytp(c)
            if ci == 0 and c > 0:
                C_rest(c // 4)
            E_sc(c + 1)
            G_out(c, 0)
            ktp_stage(c + 1)
            G_out(c, 1)
            if c + 4 < NCHUNK:
                A_xn(c + 4)
            if c + 8 < NCHUNK:
                A_stats(c + 8)
            if ci in (1, 2) and c + 3 < NCHUNK:
                A_ln_tp(c + 3)
        else:
            if c < NCHUNK - 1:
                E_vg(c + 1)
                E_vev(c + 1)
            G_ytp(c)
            G_out(c, 0)
            G_out(c, 1)
            if c + 4 < NCHUNK:
                A_xn(c + 4)
            if c < NCHUNK - 1:
                A_ln_tp(c + 3)
                A_ln_tp(c + 4, bank2=True)
                B_proj((c + 1) // 4)
                C_log_mm((c + 1) // 4)
                if c + 8 < NCHUNK:
                    A_stats(c + 8)
                if (c + 1) // 4 + 1 < NBLK:
                    A_pre2((c + 1) // 4 + 1)
                ktp_stage(c + 1)
                E_sc(c + 1)
                C_top2((c + 1) // 4)
    for c in range(NCHUNK):
        SY.wait_ge(s_oc[c], 16)

    return nc


_NC_CACHE = []


def _get_nc():
    if not _NC_CACHE:
        _NC_CACHE.append(build_core())
    return _NC_CACHE[0]


def make_inmaps(inputs):
    f32 = np.float32
    asc = np.ascontiguousarray
    x = np.asarray(inputs["L0_attn_x"], f32)

    def wT(w, kk):
        return asc(np.asarray(w, f32).T).reshape(kk, 128, -1)

    wq = wT(inputs["L0_attn_W_Q"], KCH)
    wk = wT(inputs["L0_attn_W_K"], KCH)
    wv = wT(inputs["L0_attn_W_V"], KCH)
    wg = wT(inputs["L0_attn_W_g_w"], KCH)
    wo = wT(inputs["L0_attn_W_O"], 4)
    wlin = wT(inputs["L0_ffn_W_lin"], KCH)
    gatew = wT(inputs["L0_ffn_gate_w"], KCH)
    left = asc(np.transpose(np.asarray(inputs["L0_ffn_left"], f32), (1, 2, 0)
                            ).reshape(D, E * R)).reshape(KCH, 128, E * R)
    right = asc(np.transpose(np.asarray(inputs["L0_ffn_right"], f32), (1, 0, 2)
                             ).reshape(E * R, D))
    blin = np.asarray(inputs["L0_ffn_b_lin"], f32).reshape(1, D)
    wgb = np.asarray(inputs["L0_attn_W_g_b"], f32).reshape(1, FH)
    decay = np.asarray(inputs["L0_attn_decay_log"], f32).reshape(1, H)

    sig = np.arange(128, dtype=f32)[:, None]
    tau = np.arange(128, dtype=f32)[None, :]
    valid = (sig <= tau)
    expt = (-(tau - sig) * valid).astype(f32)
    maskadd = np.where(valid, 0.0, NEG).astype(f32)
    eqm = np.broadcast_to(-(tau + 1.0), (128, 128)).astype(f32)
    e127 = np.broadcast_to(-(127.0 - sig), (128, 64)).astype(f32)
    ident = np.eye(128, dtype=ml_dtypes.bfloat16)
    cmat = asc(np.concatenate([expt, maskadd, eqm, e127], axis=1))
    crow = asc(np.concatenate([blin, wgb, decay], axis=1))

    base = {"wq": wq, "wk": wk, "wv": wv, "wg": wg, "wo": wo, "wlin": wlin,
            "gatew": gatew, "left": left, "right": right,
            "ident": ident, "cmat": cmat, "crow": crow}
    in_maps = []
    for core in range(8):
        m = dict(base)
        m["x"] = asc(x[core])
        in_maps.append(m)
    return in_maps


def kernel(L0_attn_x, L0_attn_W_Q, L0_attn_W_K, L0_attn_W_V, L0_attn_W_g_w,
           L0_attn_W_g_b, L0_attn_W_O, L0_attn_decay_log, L0_ffn_W_lin,
           L0_ffn_b_lin, L0_ffn_gate_w, L0_ffn_left, L0_ffn_right):
    inputs = dict(
        L0_attn_x=L0_attn_x, L0_attn_W_Q=L0_attn_W_Q, L0_attn_W_K=L0_attn_W_K,
        L0_attn_W_V=L0_attn_W_V, L0_attn_W_g_w=L0_attn_W_g_w,
        L0_attn_W_g_b=L0_attn_W_g_b, L0_attn_W_O=L0_attn_W_O,
        L0_attn_decay_log=L0_attn_decay_log, L0_ffn_W_lin=L0_ffn_W_lin,
        L0_ffn_b_lin=L0_ffn_b_lin, L0_ffn_gate_w=L0_ffn_gate_w,
        L0_ffn_left=L0_ffn_left, L0_ffn_right=L0_ffn_right)
    nc = _get_nc()
    in_maps = make_inmaps(inputs)
    res = run_bass_kernel_spmd(nc, in_maps, core_ids=list(range(8)))
    return np.stack([np.asarray(r["out"], np.float32) for r in res.results], axis=0)


# version 40
# speedup vs baseline: 1.0247x; 1.0007x over previous
import numpy as np
import ml_dtypes
import concourse.bass as bass
import concourse.mybir as mybir
from concourse.bass_utils import run_bass_kernel_spmd

F32 = mybir.dt.float32
BF16 = mybir.dt.bfloat16
AF = mybir.ActivationFunctionType
ALU = mybir.AluOpType
AXX = mybir.AxisListType.X

T, D, H, DK = 2048, 1024, 8, 64
FH = H * DK
CH = 128
NCHUNK = T // CH
NBLK = NCHUNK // 4
KCH = D // 128
E, R = 8, 4
NEG = -1.0e30
TRUNC = 99


class Ctr:
    def __init__(self, nc, name):
        self.h = nc.alloc_semaphore(name)
        self.n = 0

    def inc(self, instr, k=1):
        instr.then_inc(self.h, k)
        self.n += k
        return self.n


def build_core():
    nc = bass.Bass("TRN2", target_bir_lowering=False)

    x_d = nc.dram_tensor("x", [T, D], F32, kind="ExternalInput").ap()
    wq_d = nc.dram_tensor("wq", [KCH, 128, FH], F32, kind="ExternalInput").ap()
    wk_d = nc.dram_tensor("wk", [KCH, 128, FH], F32, kind="ExternalInput").ap()
    wv_d = nc.dram_tensor("wv", [KCH, 128, FH], F32, kind="ExternalInput").ap()
    wg_d = nc.dram_tensor("wg", [KCH, 128, FH], F32, kind="ExternalInput").ap()
    wo_d = nc.dram_tensor("wo", [4, 128, D], F32, kind="ExternalInput").ap()
    wlin_d = nc.dram_tensor("wlin", [KCH, 128, D], F32, kind="ExternalInput").ap()
    gatew_d = nc.dram_tensor("gatew", [KCH, 128, E], F32, kind="ExternalInput").ap()
    left_d = nc.dram_tensor("left", [KCH, 128, E * R], F32, kind="ExternalInput").ap()
    right_d = nc.dram_tensor("right", [E * R, D], F32, kind="ExternalInput").ap()
    ident_d = nc.dram_tensor("ident", [128, 128], BF16, kind="ExternalInput").ap()
    cmat_d = nc.dram_tensor("cmat", [128, 448], F32, kind="ExternalInput").ap()
    crow_d = nc.dram_tensor("crow", [1, D + FH + H], F32, kind="ExternalInput").ap()
    out_d = nc.dram_tensor("out", [T, D], F32, kind="ExternalOutput").ap()

    A = nc.alloc_sbuf_tensor
    x_sb = A("x_sb", [128, 10 * D], F32)
    xnfm = A("xnfm", [128, KCH * 1024], BF16)
    qfm = A("qfm", [128, 4 * 1024], BF16)
    kfm = A("kfm", [128, 4 * 1024], BF16)
    vtm = A("vtm", [128, 8 * 520], BF16)
    gsig = A("gsig", [128, 2 * FH], BF16)
    ygfm = A("ygfm", [128, 2 * FH], BF16)
    wq_sb = A("wq_sb", [128, KCH * FH], BF16)
    wk_sb = A("wk_sb", [128, KCH * FH], BF16)
    wv_sb = A("wv_sb", [128, KCH * FH], BF16)
    wg_sb = A("wg_sb", [128, KCH * FH], BF16)
    wo_sb = A("wo_sb", [128, 4 * D], BF16)
    wlin_sb = A("wlin_sb", [128, KCH * D], BF16)
    gatew_sb = A("gatew_sb", [128, KCH * E], BF16)
    left_sb = A("left_sb", [128, KCH * E * R], BF16)
    right_sb = A("right_sb", [E * R, D], BF16)
    crow_s = A("crow_s", [1, D + FH + H], F32)
    blin_hl = A("blin_hl", [2, D], BF16)
    blin_lo_t = A("blin_lo_t", [1, D], BF16)
    wgb_lo_t = A("wgb_lo_t", [1, FH], BF16)
    wgb_hl = A("wgb_hl", [2, FH], BF16)
    nlg_u = A("nlg_u", [1, H], F32)
    nlg_row = A("nlg_row", [1, H], F32)
    nlg_hi = A("nlg_hi", [1, H], BF16)
    nlg_lo = A("nlg_lo", [1, H], BF16)
    ident = A("ident_s", [128, 128], BF16)
    cmat_s = A("cmat_s", [128, 448], F32)
    masks = A("masks", [128, H * 128], BF16)
    dpqf = A("dpqf", [128, 4 * 128], BF16)
    dpkf = A("dpkf", [128, 4 * 128], BF16)
    nlgrep = A("nlgrep", [128, H], F32)
    nlgpair = A("nlgpair", [128, 4], F32)
    gcpair = A("gcpair", [128, 4], F32)
    S_sb = A("S_sb", [128, 4 * 64], F32)
    S_blk = A("S_blk", [128, 4 * 128], BF16)
    z_blk = A("z_blk", [128, 8], BF16)
    z_sb = A("z_sb", [128, 4], F32)
    ones_rowb = A("ones_rowb", [1, 128], BF16)
    ones2 = A("ones2", [2, 128], BF16)
    ones_colb = A("ones_colb", [128, 1], BF16)
    eps_row8 = A("eps_row8", [1, H], BF16)
    eps5_col = A("eps5_col", [128, 1], F32)
    one_col = A("one_col", [128, 1], F32)
    xn_tm = A("xn_tm", [128, 4 * D], BF16)
    phiw = A("phiw", [128, 2 * FH], F32)
    phir = A("phir", [128, 2 * FH], BF16)
    A_sb = A("A_sb", [128, 2 * H * 128], BF16)
    qt_sb = A("qt_sb", [128, 2 * 512], BF16)
    ktm_sb = A("ktm_sb", [128, 2 * 512], BF16)
    ygtm = A("ygtm", [128, 2 * FH], BF16)
    den_r = A("den_r", [128, 2 * H], F32)
    wt_tm = A("wt_tm", [128, 4 * E], BF16)
    wt4 = A("wt4", [128, 4 * 4 * E], BF16)
    wtT = A("wtT", [E * R, T], BF16)
    th_sb = A("th_sb", [E * R, 2 * 512], BF16)
    e8 = A("e8", [128, 4 * E], F32)
    dmpA = A("dmpA", [128, 4 * E], F32)
    dmpB = A("dmpB", [128, 4 * E], F32)
    ind1 = A("ind1", [128, 4 * E], F32)
    i1inv = A("i1inv", [128, 4 * E], F32)
    tmp2 = A("tmp2", [128, 4 * E], F32)
    indf = A("indf", [128, 4 * E], F32)
    wtu = A("wtu", [128, 4 * E], F32)
    tcol = A("tcol", [128, 8], F32)
    varn = A("varn", [128, 8], F32)
    stdc = A("stdc", [128, 8], F32)
    rstdc = A("rstdc", [128, 8], F32)
    nmrc = A("nmrc", [128, 8], F32)
    rs8 = A("rs8", [128, 4], F32)
    cols = A("cols", [128, 48], F32)
    scr1b = A("scr1b", [128, D], BF16)
    out_sb = A("out_sb", [128, 2 * D], F32)

    pT = nc.alloc_psum_tensor("pT", [128, 1024], BF16)
    pQK = [nc.alloc_psum_tensor("pQK0", [128, 512], F32),
           nc.alloc_psum_tensor("pQK1", [128, 512], F32)]
    pV = nc.alloc_psum_tensor("pV", [128, 512], F32)
    pG = nc.alloc_psum_tensor("pG", [128, 512], F32)
    pA0 = nc.alloc_psum_tensor("pA0", [128, 512], F32)
    pT2 = nc.alloc_psum_tensor("pT2", [128, 1024], BF16)
    pO = nc.alloc_psum_tensor("pO", [128, 512], F32)
    pVQ = None

    s_pe = Ctr(nc, "s_pe")
    s_vec = Ctr(nc, "s_vec")
    s_sca = Ctr(nc, "s_sca")
    s_gp = Ctr(nc, "s_gp")
    s_xc = [nc.alloc_semaphore(f"s_x{i}") for i in range(NCHUNK)]
    s_wg = {g: nc.alloc_semaphore(f"s_w_{g}") for g in ["qk", "k2", "v", "g", "lg", "ow", "wo2"]}
    s_c = nc.alloc_semaphore("s_c")
    s_cr = nc.alloc_semaphore("s_cr")
    s_cm = nc.alloc_semaphore("s_cm")
    s_bl = nc.alloc_semaphore("s_bl")
    s_oc = [nc.alloc_semaphore(f"s_o{i}") for i in range(NCHUNK)]

    TE, VE, SE, SY, GP = nc.tensor, nc.vector, nc.scalar, nc.sync, nc.gpsimd
    pVQ = [pV, pO]

    def dma_xc(c, eng=None):
        cx = c % 10
        (eng or SY).dma_start(x_sb[:, cx * D : cx * D + D],
                              x_d[128 * c : 128 * c + 128, :]).then_inc(s_xc[c], 16)

    dma_xc(0)
    dma_xc(1, SE)
    SY.dma_start(crow_s[:, :], crow_d[:, :]).then_inc(s_cr, 16)
    SY.dma_start(ident[:, :], ident_d[:, :]).then_inc(s_c, 16)
    dma_xc(2)
    dma_xc(3, SE)

    for t_, v_ in [(ones_rowb, 1.0), (ones2, 1.0), (ones_colb, 1.0), (eps_row8, 1e-6),
                   (eps5_col, 1e-5), (one_col, 1.0), (S_sb, 0.0), (S_blk, 0.0),
                   (z_sb, 0.0), (z_blk, 0.0), (vtm, 1.0)]:
        ins = VE.memset(t_[:, :], v_)
    v_ms = s_vec.inc(ins)
    VE.wait_ge(s_cr, 16)
    VE.tensor_copy(blin_hl[0:1, :], crow_s[0:1, 0:D])
    v_bh = s_vec.inc(VE.tensor_copy(wgb_hl[0:1, :], crow_s[0:1, D : D + FH]))
    GP.wait_ge(s_vec.h, v_bh)
    GP.tensor_tensor(blin_lo_t[:, :], crow_s[0:1, 0:D], blin_hl[0:1, :], ALU.subtract)
    g_bl = s_gp.inc(GP.tensor_tensor(wgb_lo_t[:, :], crow_s[0:1, D : D + FH], wgb_hl[0:1, :],
                                     ALU.subtract))
    wmap = [(gatew_sb, gatew_d, "lg"), (left_sb, left_d, "lg"),
            (wq_sb, wq_d, "qk"), (wk_sb, wk_d, "k2"), (wv_sb, wv_d, "v"),
            (wg_sb, wg_d, "g"), (wlin_sb, wlin_d, "ow"), (wo_sb, wo_d, "wo2")]
    w_tot = {g: 0 for g in s_wg}
    for dst, srcd, grp in wmap:
        kk = srcd.shape[0]
        GP.dma_start(dst[:, :].rearrange("p (k c) -> p k c", k=kk),
                     srcd.rearrange("k p c -> p k c")).then_inc(s_wg[grp], 16)
        w_tot[grp] += 16
    GP.dma_start(right_sb[:, :], right_d[:, :]).then_inc(s_wg["ow"], 16)
    w_tot["ow"] += 16
    SY.wait_ge(s_gp.h, g_bl)
    SY.dma_start(blin_hl[1:2, :], blin_lo_t[:, :]).then_inc(s_bl, 16)
    SY.dma_start(wgb_hl[1:2, :], wgb_lo_t[:, :]).then_inc(s_bl, 16)
    SY.dma_start(cmat_s[:, :], cmat_d[:, :]).then_inc(s_cm, 16)
    for _c in range(4, 10):
        dma_xc(_c, SE if _c % 2 else None)

    ini = {"sca": 0, "vec": 0, "gp": 0}

    def init_tail():
        SE.wait_ge(s_cr, 16)
        s_e1 = s_sca.inc(SE.activation(nlg_u[:, :], crow_s[0:1, D + FH : D + FH + H], AF.Exp, scale=-1.0))
        VE.wait_ge(s_sca.h, s_e1)
        v_mn = s_vec.inc(VE.tensor_scalar(nlg_u[:, :], nlg_u[:, :], 1e15, None, ALU.min))
        SE.wait_ge(s_vec.h, max(v_mn, v_ms))
        s_ln = s_sca.inc(SE.activation(nlg_row[:, :], nlg_u[:, :], AF.Ln,
                                       bias=one_col[0:1, :]))
        VE.wait_ge(s_sca.h, s_ln)
        v_nh = s_vec.inc(VE.tensor_copy(nlg_hi[:, :], nlg_row[:, :]))
        GP.wait_ge(s_vec.h, v_nh)
        g_nl = s_gp.inc(GP.tensor_tensor(nlg_lo[:, :], nlg_row[:, :], nlg_hi[:, :],
                                         ALU.subtract))
        TE.wait_ge(s_vec.h, v_nh)
        TE.wait_ge(s_gp.h, g_nl)
        TE.matmul(pG[:, 0:H], ones_rowb[:, :], nlg_hi[:, :], start=True, stop=False)
        t_lg = s_pe.inc(TE.matmul(pG[:, 0:H], ones_rowb[:, :], nlg_lo[:, :],
                                  start=False, stop=True))
        SE.wait_ge(s_pe.h, t_lg)
        SE.copy(nlgrep[:, :], pG[:, 0:H])
        SE.drain()
        for p in range(4):
            SE.copy(nlgpair[0:64, p : p + 1], nlgrep[0:64, 2 * p : 2 * p + 1])
            ins = SE.copy(nlgpair[64:128, p : p + 1], nlgrep[64:128, 2 * p + 1 : 2 * p + 2])
        SE.drain()
        s_gc = s_sca.inc(SE.activation(gcpair[:, :], nlgpair[:, :], AF.Exp, scale=-128.0))
        VE.wait_ge(s_sca.h, s_gc)
        VE.wait_ge(s_cm, 16)
        expt_c = cmat_s[:, 0:128]
        maskadd_c = cmat_s[:, 128:256]
        eqm_c = cmat_s[:, 256:384]
        e127_c = cmat_s[:, 384:448]
        for h in range(H):
            VE.scalar_tensor_tensor(
                out_sb[:, 128 * h : 128 * h + 128], expt_c, nlgrep[:, h : h + 1],
                maskadd_c, ALU.mult, ALU.add)
        for p in range(4):
            VE.tensor_scalar(out_sb[:, 1024 + 128 * p : 1024 + 128 * p + 128],
                             eqm_c, nlgpair[:, p : p + 1], None, ALU.mult)
        for h in range(H):
            ins = VE.tensor_scalar(out_sb[:, 1536 + 64 * h : 1536 + 64 * h + 64],
                                   e127_c, nlgrep[:, h : h + 1], None, ALU.mult)
        v_ini = s_vec.inc(ins)
        SE.wait_ge(s_vec.h, v_ini)
        SE.activation(masks[:, :], out_sb[:, 0:1024], AF.Exp)
        SE.activation(dpqf[:, :], out_sb[:, 1024:1536], AF.Exp)
        ss = s_sca.inc(SE.activation(dpkf[:, :], out_sb[:, 1536:2048], AF.Exp))
        VE.wait_ge(s_sca.h, ss)
        ini["sca"] = s_sca.n
        ini["vec"] = s_vec.n
        ini["gp"] = s_gp.n
        rec["gfree_s"] = s_sca.n


    rec = {
        "ow": [0] * NCHUNK, "res": [0] * NCHUNK, "supd": [0] * NCHUNK,
        "gat": [0] * NCHUNK, "ydeN": [0] * NCHUNK, "maskdone": [0] * NCHUNK,
        "recip": [0] * NCHUNK, "ytpe": [0] * NCHUNK,
        "tp_free_v": 0, "tp_free_s": 0,
        "qk_free": [[0, "v"], [0, "v"], [0, "v"], [0, "v"]],
        "vfree": [0, 0], "gfree_s": s_sca.n, "gfree_v": 0,
        "th": [0, 0], "ge_evac": [0] * NCHUNK, "phi_done_v": 0, "s_w4": 0,
        "kq_ready": [0, 0],
        "stats": [0] * NCHUNK, "nmr": [0] * NBLK, "e8b": 0,
    }
    qk_par = [0]

    def qk_bank():
        i = qk_par[0] % 2
        qk_par[0] += 1
        return i

    rec["t_tp"] = [0] * NCHUNK
    rec["s_xn"] = [0] * NCHUNK
    rec["xn_ev"] = [("v", 0)] * NCHUNK
    rec["tp2_free_v"] = 0
    rec["vh0"] = [0] * NCHUNK
    rec["supd_v"] = [0] * NCHUNK
    rec["tp2_free_s"] = 0
    rec["vktm"] = [0] * NCHUNK
    rec["scanv"] = [0] * NCHUNK
    rec["vv"] = [0] * NCHUNK
    rec["qt"] = [0] * NCHUNK

    def A_stats(c):
        cx, cs = c % 10, c % 8
        xc = x_sb[:, cx * D : cx * D + D]
        SE.wait_ge(s_xc[c], 16)
        if c > 0:
            SE.wait_ge(s_sca.h, rec["stats"][c - 1])
        s_cp = s_sca.inc(SE.activation(scr1b[:, :], xc, AF.Copy,
                                       accum_out=cols[:, cs : cs + 1]))
        SE.wait_ge(s_sca.h, s_cp)
        rec["stats"][c] = s_sca.inc(SE.activation(
            scr1b[:, :], xc, AF.Square, accum_out=cols[:, 8 + cs : 8 + cs + 1]))

    def A_pre2(bb):
        base = (4 * bb) % 8
        GP.wait_ge(s_sca.h, rec["stats"][4 * bb + 3])
        for i in range(4):
            cs = (4 * bb + i) % 8
            ins = GP.tensor_scalar(tcol[:, cs : cs + 1], cols[:, cs : cs + 1],
                                   cols[:, cs : cs + 1], -1.0 / (D * D),
                                   ALU.mult, ALU.mult)
        g_t = s_gp.inc(ins)
        VE.wait_ge(s_gp.h, g_t)
        v_var = s_vec.inc(VE.scalar_tensor_tensor(
            varn[:, base : base + 4], cols[:, 8 + base : 8 + base + 4],
            1.0 / D, tcol[:, base : base + 4], ALU.mult, ALU.add))
        SE.wait_ge(s_vec.h, v_var)
        s_lnv = s_sca.inc(SE.activation(stdc[:, base : base + 4],
                                        varn[:, base : base + 4],
                                        AF.Ln, bias=eps5_col[:, :]))
        SE.wait_ge(s_sca.h, s_lnv)
        s_rs = s_sca.inc(SE.activation(rstdc[:, base : base + 4],
                                       stdc[:, base : base + 4],
                                       AF.Exp, scale=-0.5))
        VE.wait_ge(s_sca.h, s_rs)
        rec["nmr"][bb] = s_vec.inc(VE.scalar_tensor_tensor(
            nmrc[:, base : base + 4], cols[:, base : base + 4], -1.0 / D,
            rstdc[:, base : base + 4], ALU.mult, ALU.mult))

    def A_xn(c):
        b = c // 4
        cx, cs, c4 = c % 10, c % 8, c % 4
        xc = x_sb[:, cx * D : cx * D + D]
        if c >= 4:
            SE.wait_ge(s_pe.h, rec["t_tp"][c - 4])
        SE.wait_ge(s_vec.h, rec["nmr"][b])
        rec["s_xn"][c] = s_sca.inc(SE.activation(xn_tm[:, c4 * D : c4 * D + D], xc,
                                       AF.Identity, scale=rstdc[:, cs : cs + 1],
                                       bias=nmrc[:, cs : cs + 1]))

    def A_ln_tp(c, bank2=False):
        cs, cp, c4 = c % 8, c % 2, c % 4
        pTb = pT2 if bank2 else pT
        TE.wait_ge(s_sca.h, rec["s_xn"][c])
        if bank2:
            TE.wait_ge(s_vec.h, rec["tp2_free_v"])
            TE.wait_ge(s_sca.h, rec["tp2_free_s"])
        else:
            TE.wait_ge(s_vec.h, rec["tp_free_v"])
            TE.wait_ge(s_sca.h, rec["tp_free_s"])
        for j in range(KCH):
            ins = TE.transpose(pTb[:, 128 * j : 128 * j + 128],
                               xn_tm[:, c4 * D + 128 * j : c4 * D + 128 * j + 128],
                               ident[:, :])
        t_tp = s_pe.inc(ins)
        rec["t_tp"][c] = t_tp
        xdst = xnfm[:, :].rearrange("p (k w) -> p k w", k=KCH)[:, :, cs * 128 : cs * 128 + 128]
        xsrc = pTb[:, :].rearrange("p (k w) -> p k w", k=KCH)
        if cp == 0:
            if c >= 8:
                VE.wait_ge(s_pe.h, rec["ow"][c - 8])
            VE.wait_ge(s_pe.h, t_tp)
            ev = s_vec.inc(VE.tensor_copy(xdst, xsrc))
            if bank2:
                rec["tp2_free_v"] = ev
            else:
                rec["tp_free_v"] = ev
            rec["xn_ev"][c] = ("v", ev)
        else:
            if c >= 8:
                SE.wait_ge(s_pe.h, rec["ow"][c - 8])
            SE.wait_ge(s_pe.h, t_tp)
            ev = s_sca.inc(SE.copy(xdst, xsrc))
            if bank2:
                rec["tp2_free_s"] = ev
            else:
                rec["tp_free_s"] = ev
            rec["xn_ev"][c] = ("s", ev)

    def wait_xn(b):
        mv, ms = ini["vec"], ini["sca"]
        for cc in range(4 * b, 4 * b + 4):
            e, n = rec["xn_ev"][cc]
            if e == "v":
                mv = max(mv, n)
            else:
                ms = max(ms, n)
        TE.wait_ge(s_vec.h, mv)
        TE.wait_ge(s_sca.h, ms)

    def B_proj(b):
        toff = 512 * (b % 2)
        wait_xn(b)
        TE.wait_ge(s_wg["qk"], w_tot["qk"])
        pbanks = [pQK[0], pQK[1], pA0, pG]
        if b > 0:
            rec["qk_free"][2] = [rec["gat"][4 * b - 1], "v"]
        rec["qk_free"][3] = [max(rec["ge_evac"][4 * b], ini["sca"]), "s"]
        outs = [(wsb, dstfm, m) for (wsb, dstfm) in [(wq_sb, qfm), (wk_sb, kfm)]
                for m in range(4)]
        pend = None
        v_min = [0, 0]
        v_stt = [0, 0]
        s_exp = [0, 0]
        for i, (wsb, dstfm, m) in enumerate(outs):
            if i == 4:
                TE.wait_ge(s_wg["k2"], w_tot["k2"])
            bank = [2, 3, 0, 1][i % 4]
            cnt, eng = rec["qk_free"][bank]
            TE.wait_ge(s_vec.h if eng == "v" else s_sca.h, cnt)
            for k in range(KCH):
                ins = TE.matmul(pbanks[bank][:, :],
                                wsb[:, FH * k + 128 * m : FH * k + 128 * m + 128],
                                xnfm[:, 1024 * k + toff : 1024 * k + toff + 512],
                                start=(k == 0), stop=(k == KCH - 1))
            t_g = s_pe.inc(ins)
            tp = i % 2
            VE.wait_ge(s_pe.h, t_g)
            if s_exp[tp]:
                VE.wait_ge(s_sca.h, s_exp[tp])
            v_min[tp] = s_vec.inc(VE.tensor_scalar(
                phiw[:, tp * FH : tp * FH + FH], pbanks[bank][:, :], 0.0, None,
                ALU.min))
            SE.wait_ge(s_vec.h, v_min[tp])
            if v_stt[tp]:
                SE.wait_ge(s_vec.h, v_stt[tp])
            s_exp[tp] = s_sca.inc(SE.activation(phir[:, tp * FH : tp * FH + FH],
                                                phiw[:, tp * FH : tp * FH + FH],
                                                AF.Exp))
            if pend is not None:
                pb, pd, pm, ptp, pse = pend
                VE.wait_ge(s_sca.h, pse)
                v_stt[ptp] = s_vec.inc(VE.scalar_tensor_tensor(
                    pd[:, 1024 * pm + toff : 1024 * pm + toff + 512],
                    pbanks[pb][:, :], 0.0,
                    phir[:, ptp * FH : ptp * FH + FH], ALU.max, ALU.add))
                rec["qk_free"][pb] = [v_stt[ptp], "v"]
            pend = (bank, dstfm, m, tp, s_exp[tp])
        pb, pd, pm, ptp, pse = pend
        VE.wait_ge(s_sca.h, pse)
        v_last = s_vec.inc(VE.scalar_tensor_tensor(
            pd[:, 1024 * pm + toff : 1024 * pm + toff + 512],
            pbanks[pb][:, :], 0.0,
            phir[:, ptp * FH : ptp * FH + FH], ALU.max, ALU.add))
        rec["qk_free"][pb] = [v_last, "v"]
        rec["phi_done_v"] = v_last
        rec["kq_ready"][b % 2] = v_last

    def C_log_mm(b):
        blo = 4 * b
        cnt3, eng3 = rec["qk_free"][3]
        TE.wait_ge(s_vec.h if eng3 == "v" else s_sca.h, cnt3)
        wait_xn(b)
        TE.wait_ge(s_wg["lg"], w_tot["lg"])
        for ci in range(4):
            cs = (blo + ci) % 8
            for k in range(KCH):
                ins = TE.matmul(pG[:, 8 * ci : 8 * ci + 8],
                                xnfm[:, 1024 * k + cs * 128 : 1024 * k + cs * 128 + 128],
                                gatew_sb[:, E * k : E * k + E],
                                start=(k == 0), stop=(k == KCH - 1))
        t_l = s_pe.inc(ins)

        scol = lambda c, i: cols[:, 16 + (c % 4) * 4 + i : 16 + (c % 4) * 4 + i + 1]
        VE.wait_ge(s_pe.h, t_l)
        for ci in range(4):
            ins = VE.reduce_max(scol(blo + ci, 0), pG[:, 8 * ci : 8 * ci + 8],
                                AXX, negate=True)
        v_nm = s_vec.inc(ins)
        SE.wait_ge(s_vec.h, v_nm)
        for ci in range(4):
            c4 = (blo + ci) % 4
            ins = SE.activation(e8[:, c4 * E : c4 * E + E],
                                pG[:, 8 * ci : 8 * ci + 8],
                                AF.Exp, bias=scol(blo + ci, 0))
        s_e8 = s_sca.inc(ins)
        rec["e8b"] = s_e8
        rec["gfree_s"] = s_e8

    def C_top2(b):
        blo = 4 * b
        scol = lambda c, i: cols[:, 16 + (c % 4) * 4 + i : 16 + (c % 4) * 4 + i + 1]
        VE.wait_ge(s_sca.h, max(rec["e8b"], rec["s_w4"]))
        esl = lambda c4: e8[:, c4 * E : c4 * E + E]
        for ci in range(4):
            c4 = (blo + ci) % 4
            VE.tensor_scalar(dmpA[:, c4 * E : c4 * E + E], esl(c4), 1.0, NEG,
                             ALU.mult, ALU.max, accum_out=scol(blo + ci, 1))
            ins = VE.tensor_scalar(dmpB[:, c4 * E : c4 * E + E], esl(c4),
                                   1.0, 0.0, ALU.mult, ALU.add,
                                   accum_out=scol(blo + ci, 2))
        v_m1 = s_vec.inc(ins)
        VE.drain()
        for ci in range(4):
            ins = VE.reciprocal(rs8[:, (blo + ci) % 4 : (blo + ci) % 4 + 1],
                                scol(blo + ci, 2))
        v_rs = s_vec.inc(ins)
        GP.wait_ge(s_vec.h, v_m1)
        for ci in range(4):
            c4 = (blo + ci) % 4
            ins = GP.tensor_scalar(ind1[:, c4 * E : c4 * E + E], esl(c4),
                                   scol(blo + ci, 1), None, ALU.is_ge)
        g_i1 = s_gp.inc(ins)
        VE.wait_ge(s_gp.h, g_i1)
        for ci in range(4):
            c4 = (blo + ci) % 4
            ins = VE.scalar_tensor_tensor(tmp2[:, c4 * E : c4 * E + E],
                                          ind1[:, c4 * E : c4 * E + E],
                                          -1e30, esl(c4), ALU.mult, ALU.add)
        v_t2 = s_vec.inc(ins)
        VE.wait_ge(s_vec.h, v_t2)
        for ci in range(4):
            c4 = (blo + ci) % 4
            ins = VE.reduce_max(scol(blo + ci, 3), tmp2[:, c4 * E : c4 * E + E], AXX)
        v_m2 = s_vec.inc(ins)
        GP.wait_ge(s_vec.h, v_m2)
        for ci in range(4):
            c4 = (blo + ci) % 4
            ins = GP.tensor_scalar(indf[:, c4 * E : c4 * E + E], esl(c4),
                                   scol(blo + ci, 3), None, ALU.is_ge)
        g_ind = s_gp.inc(ins)
        VE.wait_ge(s_gp.h, g_ind)
        VE.wait_ge(s_vec.h, v_rs)
        for ci in range(4):
            c4 = (blo + ci) % 4
            ins = VE.scalar_tensor_tensor(wt_tm[:, c4 * E : c4 * E + E],
                                          indf[:, c4 * E : c4 * E + E],
                                          rs8[:, c4 : c4 + 1], esl(c4),
                                          ALU.mult, ALU.mult)
        v_wt = s_vec.inc(ins)
        SE.wait_ge(s_vec.h, v_wt)
        for ci in range(4):
            c4 = (blo + ci) % 4
            for r in range(R):
                ins = SE.copy(wt4[:, c4 * 32 + 8 * r : c4 * 32 + 8 * r + 8],
                              wt_tm[:, c4 * E : c4 * E + E])
        rec["s_w4"] = s_sca.inc(ins)

    def C_rest(b):
        blo = 4 * b
        toff = 512 * (b % 2)
        bank_t = qk_bank()
        cnt, eng = rec["qk_free"][bank_t]
        TE.wait_ge(s_vec.h if eng == "v" else s_sca.h, cnt)
        for k in range(KCH):
            ins = TE.matmul(pQK[bank_t][0:32, :],
                            left_sb[:, 32 * k : 32 * k + 32],
                            xnfm[:, 1024 * k + toff : 1024 * k + toff + 512],
                            start=(k == 0), stop=(k == KCH - 1))
        t_t = s_pe.inc(ins)
        TE.wait_ge(s_sca.h, max(rec["s_w4"], rec["tp_free_s"]))
        TE.wait_ge(s_vec.h, rec["tp_free_v"])
        for ci in range(4):
            c4 = (blo + ci) % 4
            ins = TE.transpose(pT[0:32, 128 * ci : 128 * ci + 128],
                               wt4[:, c4 * 32 : c4 * 32 + 32], ident[:, :])
        t_wt = s_pe.inc(ins)
        SE.wait_ge(s_pe.h, t_wt)
        for ci in range(4):
            ins = SE.copy(wtT[:, 128 * (blo + ci) : 128 * (blo + ci) + 128],
                          pT[0:32, 128 * ci : 128 * ci + 128])
        rec["tp_free_s"] = s_sca.inc(ins)

        bp = b % 2
        VE.wait_ge(s_pe.h, t_t)
        VE.wait_ge(s_sca.h, rec["tp_free_s"])
        if b >= 2:
            VE.wait_ge(s_pe.h, rec["ow"][4 * (b - 2) + 3])
        v_th = s_vec.inc(VE.tensor_tensor(th_sb[:, bp * 512 : bp * 512 + 512],
                                          pQK[bank_t][0:32, :],
                                          wtT[:, 512 * b : 512 * b + 512], ALU.mult))
        rec["th"][bp] = v_th
        rec["qk_free"][bank_t] = [v_th, "v"]

    def E_vg(c):
        cs = c % 8
        pVb = pVQ[c % 2]
        eng, cnt = rec["xn_ev"][c]
        TE.wait_ge(s_vec.h if eng == "v" else s_sca.h, cnt)
        TE.wait_ge(s_wg["v"], w_tot["v"])
        if c >= 2:
            TE.wait_ge(s_vec.h, rec["vfree"][c % 2])
        for k in range(KCH):
            ins = TE.matmul(pVb[:, :],
                            xnfm[:, 1024 * k + cs * 128 : 1024 * k + cs * 128 + 128],
                            wv_sb[:, FH * k : FH * k + FH],
                            start=(k == 0), stop=(k == KCH - 1))
        rec["vv"][c] = s_pe.inc(ins)
        TE.wait_ge(s_wg["g"], w_tot["g"])
        TE.wait_ge(s_bl, 32)
        if c == 0:
            TE.wait_ge(s_gp.h, ini["gp"])
        TE.wait_ge(s_sca.h, rec["gfree_s"])
        TE.wait_ge(s_vec.h, rec["gfree_v"])
        for k in range(KCH):
            TE.matmul(pG[:, :],
                      xnfm[:, 1024 * k + cs * 128 : 1024 * k + cs * 128 + 128],
                      wg_sb[:, FH * k : FH * k + FH],
                      start=(k == 0), stop=False)
        t_gp = s_pe.inc(TE.matmul(pG[:, :], ones2[:, :], wgb_hl[:, :],
                                  start=False, stop=True))
        SE.wait_ge(s_pe.h, t_gp)
        s_gs = s_sca.inc(SE.activation(gsig[:, (c % 2) * FH : (c % 2) * FH + FH],
                                       pG[:, :], AF.Sigmoid))
        rec["ge_evac"][c] = s_gs

    def E_sc(c):
        b = c // 4
        ci = c % 4
        cs, cp = c % 8, c % 2
        toff = 512 * (b % 2)
        pVb = pVQ[c % 2]
        TE.wait_ge(s_vec.h, rec["kq_ready"][b % 2])
        if c > 0:
            TE.wait_ge(s_vec.h, rec["gat"][c - 1])
        for h in [0, 2, 4, 6]:
            ins = TE.matmul(
                pA0[:, 128 * (h // 2) : 128 * (h // 2) + 128],
                kfm[0:64, 1024 * (h // 2) + toff + 128 * ci :
                    1024 * (h // 2) + toff + 128 * ci + 128],
                qfm[0:64, 1024 * (h // 2) + toff + 128 * ci :
                    1024 * (h // 2) + toff + 128 * ci + 128],
                start=True, stop=True)
        t_sc0 = s_pe.inc(ins)
        TE.wait_ge(s_sca.h, max(rec["ge_evac"][c], rec["e8b"]))
        for h in [1, 3, 5, 7]:
            ins = TE.matmul(
                pG[:, 128 * (h // 2) : 128 * (h // 2) + 128],
                kfm[64:128, 1024 * (h // 2) + toff + 128 * ci :
                    1024 * (h // 2) + toff + 128 * ci + 128],
                qfm[64:128, 1024 * (h // 2) + toff + 128 * ci :
                    1024 * (h // 2) + toff + 128 * ci + 128],
                start=True, stop=True)
        t_sc1 = s_pe.inc(ins)
        if c >= 2:
            VE.wait_ge(s_pe.h, rec["ydeN"][c - 2])
        VE.wait_ge(s_pe.h, t_sc1)
        for h in range(H):
            psrc = pA0 if h % 2 == 0 else pG
            ins = VE.tensor_tensor(
                A_sb[:, cp * 1024 + 128 * h : cp * 1024 + 128 * h + 128],
                psrc[:, 128 * (h // 2) : 128 * (h // 2) + 128],
                masks[:, 128 * h : 128 * h + 128], ALU.mult)
        rec["maskdone"][c] = s_vec.inc(ins)
        qsrc = qfm[:, :].rearrange("q (p w) -> q p w", p=4)[
            :, :, toff + 128 * ci : toff + 128 * ci + 128]
        rec["qt"][c] = s_vec.inc(VE.tensor_tensor(
            qt_sb[:, cp * 512 : cp * 512 + 512].rearrange("q (p w) -> q p w", p=4),
            qsrc, dpqf[:, :].rearrange("q (p w) -> q p w", p=4), ALU.mult))

    def E_vev(c):
        cs = c % 8
        pVb = pVQ[c % 2]
        VE.wait_ge(s_pe.h, rec["vv"][c])
        rec["scanv"][c] = s_vec.inc(VE.tensor_copy(
            vtm[:, cs * 520 : cs * 520 + 520].rearrange("p (h w) -> p h w", h=8)[:, :, 0:64],
            pVb[:, :].rearrange("p (h w) -> p h w", h=8)))

    def ktp_stage(c):
        b = c // 4
        ci = c % 4
        cp = c % 2
        toff = 512 * (b % 2)
        TE.wait_ge(s_vec.h, max(rec["tp2_free_v"], rec["kq_ready"][b % 2]))
        TE.wait_ge(s_sca.h, rec["tp2_free_s"])
        if c >= 2:
            TE.wait_ge(s_pe.h, rec["ydeN"][c - 2])
        for p in range(4):
            ins = TE.transpose(
                pT2[:, 128 * p : 128 * p + 128],
                kfm[:, 1024 * p + toff + 128 * ci : 1024 * p + toff + 128 * ci + 128],
                ident[:, :])
        t_ktp = s_pe.inc(ins)
        VE.wait_ge(s_pe.h, t_ktp)
        if c >= 2:
            VE.wait_ge(s_pe.h, rec["ydeN"][c - 2])
        v_ktm = s_vec.inc(VE.tensor_tensor(ktm_sb[:, cp * 512 : cp * 512 + 512],
                                           pT2[:, 0:512], dpkf[:, :], ALU.mult))
        rec["tp2_free_v"] = v_ktm
        rec["vktm"][c] = v_ktm

    def F_scan(c):
        cs, cp = c % 8, c % 2
        vb = cs * 520
        pVb = pVQ[c % 2]
        TE.wait_ge(s_vec.h, max(rec["vktm"][c], rec["scanv"][c]))
        for h in range(H):
            tp_pos = (0, 64 * (h % 2))
            ins = TE.matmul(pVb[64 * (h % 2) : 64 * (h % 2) + 64,
                               65 * (h // 2) : 65 * (h // 2) + 65],
                            ktm_sb[:, cp * 512 + 128 * (h // 2) + 64 * (h % 2) :
                                   cp * 512 + 128 * (h // 2) + 64 * (h % 2) + 64],
                            vtm[:, vb + 65 * h : vb + 65 * h + 65],
                            start=True, stop=True, tile_position=tp_pos)
        t_st = s_pe.inc(ins)
        rec["ydeN"][c] = t_st

        TE.wait_ge(s_vec.h, max(rec["maskdone"][c], rec["qt"][c]))
        if c > 0:
            TE.wait_ge(s_vec.h, rec["supd_v"][c - 1])
        TE.wait_ge(s_sca.h, max(rec["ge_evac"][c], rec["e8b"]))
        TE.matmul(pG[:, 0:H], ones_rowb[:, 0:128], eps_row8[:, :],
                  start=True, stop=False)
        for h in range(H):
            TE.matmul(pG[:, h : h + 1],
                      A_sb[:, cp * 1024 + 128 * h : cp * 1024 + 128 * h + 128],
                      ones_colb[:, :], start=False, stop=False)
        for p in range(4):
            ins = TE.matmul(pG[:, 2 * p : 2 * p + 2],
                            qt_sb[:, cp * 512 + 128 * p : cp * 512 + 128 * p + 128],
                            z_blk[:, 2 * p : 2 * p + 2],
                            start=False, stop=(p == 3))
        t_den = s_pe.inc(ins)
        for p in range(4):
            for e in range(2):
                h = 2 * p + e
                TE.matmul(pA0[:, 64 * h : 64 * h + 64],
                          A_sb[:, cp * 1024 + 128 * h : cp * 1024 + 128 * h + 128],
                          vtm[:, vb + 65 * h : vb + 65 * h + 64],
                          start=(e == 0), stop=False)
            ins = TE.matmul(pA0[:, 128 * p : 128 * p + 128],
                      qt_sb[:, cp * 512 + 128 * p : cp * 512 + 128 * p + 128],
                      S_blk[:, 128 * p : 128 * p + 128],
                      start=False, stop=True)
        t_y = s_pe.inc(ins)

        VE.wait_ge(s_pe.h, t_st)
        for p in range(4):
            VE.scalar_tensor_tensor(S_sb[:, 64 * p : 64 * p + 64],
                                    S_sb[:, 64 * p : 64 * p + 64],
                                    gcpair[:, p : p + 1],
                                    pVb[:, 65 * p : 65 * p + 64],
                                    ALU.mult, ALU.add)
            ins = VE.scalar_tensor_tensor(z_sb[:, p : p + 1], z_sb[:, p : p + 1],
                                          gcpair[:, p : p + 1],
                                          pVb[:, 65 * p + 64 : 65 * p + 65],
                                          ALU.mult, ALU.add)
        v_su = s_vec.inc(ins)
        rec["vfree"][c % 2] = v_su
        VE.wait_ge(s_pe.h, t_den)
        rec["recip"][c] = s_vec.inc(VE.reciprocal(den_r[:, cp * H : cp * H + H],
                                                  pG[:, 0:H]))
        rec["gfree_v"] = rec["recip"][c]
        rec["gfree_s"] = 0
        VE.drain()
        VE.wait_ge(s_sca.h, rec["ge_evac"][c])
        VE.wait_ge(s_pe.h, t_y)
        for h in range(H):
            ins = VE.scalar_tensor_tensor(
                ygtm[:, cp * FH + 64 * h : cp * FH + 64 * h + 64],
                gsig[:, cp * FH + 64 * h : cp * FH + 64 * h + 64],
                den_r[:, cp * H + h : cp * H + h + 1],
                pA0[:, 64 * h : 64 * h + 64], ALU.mult, ALU.mult)
        rec["gat"][c] = s_vec.inc(ins)
        VE.tensor_copy(
            S_blk[0:64, :].rearrange("p (q w) -> p q w", q=4)[:, :, 0:64],
            S_sb[0:64, :].rearrange("p (q w) -> p q w", q=4))
        VE.tensor_copy(
            S_blk[64:128, :].rearrange("p (q w) -> p q w", q=4)[:, :, 64:128],
            S_sb[64:128, :].rearrange("p (q w) -> p q w", q=4))
        VE.tensor_copy(z_blk[0:64, :].rearrange("p (q w) -> p q w", q=4)[:, :, 0:1],
                       z_sb[0:64, :].rearrange("p (q w) -> p q w", q=4))
        ins = VE.tensor_copy(
            z_blk[64:128, :].rearrange("p (q w) -> p q w", q=4)[:, :, 1:2],
            z_sb[64:128, :].rearrange("p (q w) -> p q w", q=4))
        rec["supd_v"][c] = s_vec.inc(ins)

    def G_ytp(c):
        cp = c % 2
        TE.wait_ge(s_vec.h, max(rec["gat"][c], rec["tp2_free_v"]))
        if c >= 2:
            TE.wait_ge(s_pe.h, rec["ow"][c - 2])
        TE.wait_ge(s_sca.h, rec["tp2_free_s"])
        for j in range(4):
            ins = TE.transpose(pT2[:, 512 + 128 * j : 512 + 128 * j + 128],
                               ygtm[:, cp * FH + 128 * j : cp * FH + 128 * j + 128],
                               ident[:, :])
        t_ytp = s_pe.inc(ins)
        SE.wait_ge(s_pe.h, t_ytp)
        ins = SE.copy(ygfm[:, cp * FH : cp * FH + FH].rearrange("p (j w) -> p j w", j=4),
                      pT2[:, 512:1024].rearrange("p (j w) -> p j w", j=4))
        rec["ytpe"][c] = s_sca.inc(ins)
        rec["tp2_free_s"] = rec["ytpe"][c]

    def G_out(c, n):
        b = c // 4
        ci = c % 4
        cs, cp = c % 8, c % 2
        bank = qk_bank()
        cnt, eng = rec["qk_free"][bank]
        TE.wait_ge(s_vec.h if eng == "v" else s_sca.h, cnt)
        if n == 0:
            TE.wait_ge(s_vec.h, rec["th"][b % 2])
            TE.wait_ge(s_wg["ow"], w_tot["ow"])
            if c == 0:
                TE.wait_ge(s_gp.h, ini["gp"])
        for k in range(KCH):
            TE.matmul(pQK[bank][:, :],
                      xnfm[:, 1024 * k + cs * 128 : 1024 * k + cs * 128 + 128],
                      wlin_sb[:, D * k + 512 * n : D * k + 512 * n + 512],
                      start=(k == 0), stop=False)
        TE.matmul(pQK[bank][:, :],
                  th_sb[:, (b % 2) * 512 + 128 * ci : (b % 2) * 512 + 128 * ci + 128],
                  right_sb[:, 512 * n : 512 * n + 512], start=False, stop=False)
        TE.matmul(pQK[bank][:, :], ones2[:, :],
                  blin_hl[:, 512 * n : 512 * n + 512], start=False, stop=False)
        if n == 0:
            TE.wait_ge(s_sca.h, rec["ytpe"][c])
            TE.wait_ge(s_wg["wo2"], w_tot["wo2"])
        for j in range(4):
            ins = TE.matmul(pQK[bank][:, :],
                            ygfm[:, cp * FH + 128 * j : cp * FH + 128 * j + 128],
                            wo_sb[:, D * j + 512 * n : D * j + 512 * n + 512],
                            start=False, stop=(j == 3))
        t_half = s_pe.inc(ins)
        VE.wait_ge(s_pe.h, t_half)
        if n == 0 and c >= 2:
            VE.wait_ge(s_oc[c - 2], 16)
        cx = c % 10
        ins = VE.tensor_tensor(out_sb[:, cp * D + 512 * n : cp * D + 512 * n + 512],
                               pQK[bank][:, :],
                               x_sb[:, cx * D + 512 * n : cx * D + 512 * n + 512],
                               ALU.add)
        r_ev = s_vec.inc(ins)
        rec["qk_free"][bank] = [r_ev, "v"]
        if n == 1:
            rec["res"][c] = r_ev
            rec["ow"][c] = t_half
            SY.wait_ge(s_vec.h, r_ev)
            SY.dma_start(out_d[128 * c : 128 * c + 128, :],
                         out_sb[:, cp * D : cp * D + D]).then_inc(s_oc[c], 16)
            if c + 10 < NCHUNK:
                dma_xc(c + 10)
        else:
            rec["vh0"][c] = r_ev

    TE.wait_ge(s_c, 16)
    for c in range(4):
        A_stats(c)
    A_pre2(0)
    init_tail()
    for c in range(4):
        A_xn(c)
        A_ln_tp(c)
    B_proj(0)
    C_log_mm(0)
    E_vg(0)
    E_vev(0)
    ktp_stage(0)
    E_sc(0)
    C_top2(0)
    C_rest(0)
    for c in range(4, 8):
        A_stats(c)
    A_pre2(1)
    for c in range(NCHUNK):
        ci = c % 4
        F_scan(c)
        if ci != 3:
            E_vg(c + 1)
            E_vev(c + 1)
            G_ytp(c)
            if ci == 0 and c > 0:
                C_rest(c // 4)
            E_sc(c + 1)
            G_out(c, 0)
            ktp_stage(c + 1)
            G_out(c, 1)
            if c + 4 < NCHUNK:
                A_xn(c + 4)
            if c + 8 < NCHUNK:
                A_stats(c + 8)
            if ci in (1, 2) and c + 3 < NCHUNK:
                A_ln_tp(c + 3)
        else:
            if c < NCHUNK - 1:
                E_vg(c + 1)
                E_vev(c + 1)
            G_ytp(c)
            G_out(c, 0)
            G_out(c, 1)
            if c + 4 < NCHUNK:
                A_xn(c + 4)
            if c < NCHUNK - 1:
                A_ln_tp(c + 3)
                A_ln_tp(c + 4, bank2=True)
                B_proj((c + 1) // 4)
                C_log_mm((c + 1) // 4)
                if c + 8 < NCHUNK:
                    A_stats(c + 8)
                if (c + 1) // 4 + 1 < NBLK:
                    A_pre2((c + 1) // 4 + 1)
                ktp_stage(c + 1)
                E_sc(c + 1)
                C_top2((c + 1) // 4)
    for c in range(NCHUNK):
        SY.wait_ge(s_oc[c], 16)

    return nc


_NC_CACHE = []


def _get_nc():
    if not _NC_CACHE:
        _NC_CACHE.append(build_core())
    return _NC_CACHE[0]


def make_inmaps(inputs):
    f32 = np.float32
    asc = np.ascontiguousarray
    x = np.asarray(inputs["L0_attn_x"], f32)

    def wT(w, kk):
        return asc(np.asarray(w, f32).T).reshape(kk, 128, -1)

    wq = wT(inputs["L0_attn_W_Q"], KCH)
    wk = wT(inputs["L0_attn_W_K"], KCH)
    wv = wT(inputs["L0_attn_W_V"], KCH)
    wg = wT(inputs["L0_attn_W_g_w"], KCH)
    wo = wT(inputs["L0_attn_W_O"], 4)
    wlin = wT(inputs["L0_ffn_W_lin"], KCH)
    gatew = wT(inputs["L0_ffn_gate_w"], KCH)
    left = asc(np.transpose(np.asarray(inputs["L0_ffn_left"], f32), (1, 2, 0)
                            ).reshape(D, E * R)).reshape(KCH, 128, E * R)
    right = asc(np.transpose(np.asarray(inputs["L0_ffn_right"], f32), (1, 0, 2)
                             ).reshape(E * R, D))
    blin = np.asarray(inputs["L0_ffn_b_lin"], f32).reshape(1, D)
    wgb = np.asarray(inputs["L0_attn_W_g_b"], f32).reshape(1, FH)
    decay = np.asarray(inputs["L0_attn_decay_log"], f32).reshape(1, H)

    sig = np.arange(128, dtype=f32)[:, None]
    tau = np.arange(128, dtype=f32)[None, :]
    valid = (sig <= tau)
    expt = (-(tau - sig) * valid).astype(f32)
    maskadd = np.where(valid, 0.0, NEG).astype(f32)
    eqm = np.broadcast_to(-(tau + 1.0), (128, 128)).astype(f32)
    e127 = np.broadcast_to(-(127.0 - sig), (128, 64)).astype(f32)
    ident = np.eye(128, dtype=ml_dtypes.bfloat16)
    cmat = asc(np.concatenate([expt, maskadd, eqm, e127], axis=1))
    crow = asc(np.concatenate([blin, wgb, decay], axis=1))

    base = {"wq": wq, "wk": wk, "wv": wv, "wg": wg, "wo": wo, "wlin": wlin,
            "gatew": gatew, "left": left, "right": right,
            "ident": ident, "cmat": cmat, "crow": crow}
    in_maps = []
    for core in range(8):
        m = dict(base)
        m["x"] = asc(x[core])
        in_maps.append(m)
    return in_maps


def kernel(L0_attn_x, L0_attn_W_Q, L0_attn_W_K, L0_attn_W_V, L0_attn_W_g_w,
           L0_attn_W_g_b, L0_attn_W_O, L0_attn_decay_log, L0_ffn_W_lin,
           L0_ffn_b_lin, L0_ffn_gate_w, L0_ffn_left, L0_ffn_right):
    inputs = dict(
        L0_attn_x=L0_attn_x, L0_attn_W_Q=L0_attn_W_Q, L0_attn_W_K=L0_attn_W_K,
        L0_attn_W_V=L0_attn_W_V, L0_attn_W_g_w=L0_attn_W_g_w,
        L0_attn_W_g_b=L0_attn_W_g_b, L0_attn_W_O=L0_attn_W_O,
        L0_attn_decay_log=L0_attn_decay_log, L0_ffn_W_lin=L0_ffn_W_lin,
        L0_ffn_b_lin=L0_ffn_b_lin, L0_ffn_gate_w=L0_ffn_gate_w,
        L0_ffn_left=L0_ffn_left, L0_ffn_right=L0_ffn_right)
    nc = _get_nc()
    in_maps = make_inmaps(inputs)
    res = run_bass_kernel_spmd(nc, in_maps, core_ids=list(range(8)))
    return np.stack([np.asarray(r["out"], np.float32) for r in res.results], axis=0)
